# version 05
# speedup vs baseline: 1.5579x; 1.0231x over previous
import numpy as np
import ml_dtypes
from contextlib import ExitStack
import concourse.bass as bass
import concourse.mybir as mybir
from concourse.bass_utils import run_bass_kernel_spmd

F32 = mybir.dt.float32
BF16 = mybir.dt.bfloat16
AF = mybir.ActivationFunctionType
ALU = mybir.AluOpType
AX = mybir.AxisListType
BF16NP = ml_dtypes.bfloat16

NCORES = 8
B = 4
T = 4096
D = 1024
NTOK = 2048
NTILE = 16
LTILES = 4

ENGINES = ("sync", "scalar", "vector", "tensor", "gpsimd")


class P:

    def __init__(self):
        self.ops = {e: [] for e in ENGINES}
        self.cnt = {}

    def add(self, eng, fn):
        self.ops[eng].append(fn)

    def bump(self, sem_name, n=1):
        v = self.cnt.get(sem_name, 0) + n
        self.cnt[sem_name] = v
        return v

    def val(self, sem_name):
        return self.cnt.get(sem_name, 0)


class Ctx:

    def __init__(self, nc, p):
        self.nc = nc
        self.p = p
        self.sb = {}
        self.ps = {}
        self.sem = {}
        self.dram = {}

    def wait(self, eng, sem_name, val):
        self.p.add(eng, lambda e, s=self.sem[sem_name], v=val: e.wait_ge(s, v))

    def wait_cur(self, eng, sem_name):
        self.wait(eng, sem_name, self.p.val(sem_name))

    def sig(self, eng, sem_name, n=1):
        v = self.p.bump(sem_name, n)
        self.p.add(eng, lambda e, s=self.sem[sem_name], k=n: e.drain().then_inc(s, k))
        return v

    def drain(self, eng):
        self.p.add(eng, lambda e: e.drain())

    def op(self, eng, method, *args, **kw):
        def fn(e, m=method, a=args, k=kw):
            getattr(e, m)(*a, **k)
        self.p.add(eng, fn)

    def dma(self, queue, out_ap, in_ap, sem_name, **kw):
        v = self.p.bump(sem_name, 16)
        def fn(e, o=out_ap, i=in_ap, s=self.sem[sem_name], k=kw):
            e.dma_start(out=o, in_=i, **k).then_inc(s, 16)
        self.p.add(queue, fn)
        return v


def finish(ctx):
    nc, p = ctx.nc, ctx.p
    with nc.Block() as block:
        for eng in ENGINES:
            ops = p.ops[eng]
            if not ops:
                continue
            def runner(e, ops=ops):
                for f in ops:
                    f(e)
            getattr(block, eng)(runner)



def alloc_common(ctx, es):
    nc = ctx.nc
    for name in ("dS", "dG", "dC", "dO", "dOa", "dOb",
                 "dKa", "dKb", "dVa", "dVb", "dQ", "dGa", "dGb", "dGc", "dGd",
                 "sv", "ss", "st", "sg", "sp"):
        ctx.sem[name] = es.enter_context(nc.semaphore(name))
    ctx.sb["ones"] = es.enter_context(nc.sbuf_tensor("ones_c", [1, 128], F32))
    ctx.sb["ones_bf"] = es.enter_context(nc.sbuf_tensor("ones_bf", [1, 1024], BF16))
    ctx.op("vector", "memset", ctx.sb["ones"][:, :], 1.0)
    ctx.op("vector", "memset", ctx.sb["ones_bf"][:, :], 1.0)
    ctx.sig("vector", "sv")


def bcast_rows(ctx, vec_sb_ap, out_sb_ap, psum_ap, n, wait_list=(), dt=F32):
    for (eng, sem, val) in wait_list:
        ctx.wait(eng, sem, val)
    ctx.wait("tensor", "sv", 1)
    ctx.op("tensor", "matmul", psum_ap, ctx.sb["ones"][:, :].nop
           if False else ctx.sb["ones"][:, :], vec_sb_ap, start=True, stop=True)
    tv = ctx.sig("tensor", "st")
    ctx.wait("scalar", "st", tv)
    ctx.op("scalar", "copy", out_sb_ap, psum_ap)
    return ctx.sig("scalar", "ss")


def emit_ln(ctx, xt, xln, stats, sq, eps=1e-5):
    for i in range(NTILE):
        sl = slice(i * 1024, (i + 1) * 1024)
        ctx.op("vector", "reduce_sum", stats[:, i:i + 1], xt[:, sl], axis=AX.X)
        ctx.op("vector", "tensor_mul", sq[:, :], xt[:, sl], xt[:, sl])
        ctx.drain("vector")
        ctx.op("vector", "reduce_sum", stats[:, 16 + i:17 + i], sq[:, :], axis=AX.X)
        ctx.drain("vector")
    ctx.drain("vector")
    ctx.op("vector", "tensor_scalar_mul", stats[:, 32:48], stats[:, 0:16], 1.0 / 1024.0)
    ctx.drain("vector")
    ctx.op("vector", "tensor_scalar_mul", stats[:, 48:64], stats[:, 16:32], 1.0 / 1024.0)
    ctx.drain("vector")
    ctx.op("vector", "tensor_mul", sq[:, 0:16], stats[:, 32:48], stats[:, 32:48])
    ctx.drain("vector")
    ctx.op("vector", "tensor_sub", stats[:, 48:64], stats[:, 48:64], sq[:, 0:16])
    ctx.drain("vector")
    ctx.op("vector", "tensor_scalar_add", stats[:, 48:64], stats[:, 48:64], eps)
    v = ctx.sig("vector", "sv")
    ctx.wait("scalar", "sv", v)
    ctx.op("scalar", "activation", stats[:, 48:64], stats[:, 48:64], AF.Sqrt)
    sv2 = ctx.sig("scalar", "ss")
    ctx.wait("vector", "ss", sv2)
    ctx.op("vector", "reciprocal", stats[:, 48:64], stats[:, 48:64])
    ctx.drain("vector")
    for i in range(NTILE):
        sl = slice(i * 1024, (i + 1) * 1024)
        ctx.op("vector", "tensor_scalar", xln[:, sl], xt[:, sl],
               stats[:, 32 + i:33 + i], stats[:, 48 + i:49 + i],
               op0=ALU.subtract, op1=ALU.mult)
    return ctx.sig("vector", "sv")


def emit_tm_to_fm(ctx, src_tm, dst_fm, ident_bf, wait_sv_val, ncols=8,
                  copy_eng="scalar", tiles=None):
    p = ctx.p
    pst = [ctx.ps["ptA"], ctx.ps["ptB"]]
    if tiles is None:
        tiles = range(NTILE)
    k = 0
    base_st = p.val("st")
    base_cp = p.val("ss" if copy_eng == "scalar" else "sv")
    cp_sem = "ss" if copy_eng == "scalar" else "sv"
    ctx.wait("tensor", "sv", wait_sv_val)
    ctx.wait("tensor", "dC", p.val("dC"))
    for i in tiles:
        for dc in range(ncols):
            pt = pst[k % 2][:, :]
            ctx.wait("tensor", cp_sem, base_cp + k - 1 if k >= 2 else base_cp)
            ctx.op("tensor", "transpose", pt, src_tm[:, i * 1024 + dc * 128: i * 1024 + dc * 128 + 128], ident_bf)
            tv = ctx.sig("tensor", "st")
            ctx.wait(copy_eng, "st", tv)
            dst = dst_fm[:, dc * 2048 + i * 128: dc * 2048 + i * 128 + 128]
            if copy_eng == "scalar":
                ctx.op("scalar", "copy", dst, pt)
            else:
                ctx.op("vector", "tensor_copy", dst, pt)
            ctx.sig(copy_eng, cp_sem)
            k += 1
    return p.val(cp_sem)


def emit_weight_prep(ctx, w_in, wnat, wt, idn, oc_n=8, ec_n=8, pre_sv=None):
    p = ctx.p
    E = ec_n * 128
    w_res = w_in.rearrange("(c p) e -> p c e", p=128)
    dv = ctx.dma("gpsimd", wnat.rearrange("p (c e) -> p c e", c=oc_n), w_res, "dG")
    pst = [ctx.ps["ptA"], ctx.ps["ptB"]]
    ctx.wait("tensor", "dG", dv)
    ctx.wait("tensor", "dC", p.val("dC"))
    if pre_sv is not None:
        ctx.wait("tensor", "sv", pre_sv)
    k = 0
    base_ss = p.val("ss")
    for oc in range(oc_n):
        for ec in range(ec_n):
            pt = pst[k % 2][:, :]
            ctx.wait("tensor", "ss", base_ss + k - 1 if k >= 2 else base_ss)
            ctx.op("tensor", "transpose", pt,
                   wnat[:, oc * E + ec * 128: oc * E + ec * 128 + 128], idn)
            tv = ctx.sig("tensor", "st")
            ctx.wait("scalar", "st", tv)
            ctx.op("scalar", "copy",
                   wt[:, ec * (oc_n * 128) + oc * 128: ec * (oc_n * 128) + oc * 128 + 128], pt)
            ctx.sig("scalar", "ss")
            k += 1
    return p.val("ss")



def build_s2(core_id):
    nc = bass.Bass("TRN2", target_bir_lowering=False, debug=False)
    p = P()
    ctx = Ctx(nc, p)
    x_in = nc.dram_tensor("x", [NTOK, D], F32, kind="ExternalInput").ap()
    w_in = nc.dram_tensor("w", [D, D], F32, kind="ExternalInput").ap()
    alpha_in = nc.dram_tensor("alpha", [1, 1], F32, kind="ExternalInput").ap()
    bias_in = nc.dram_tensor("bias", [1, D], F32, kind="ExternalInput").ap()
    ident_in = nc.dram_tensor("ident", [128, 128], BF16, kind="ExternalInput").ap()
    x_out = nc.dram_tensor("xo", [NTOK, D], F32, kind="ExternalOutput").ap()

    with ExitStack() as es:
        sb = lambda name, shape, dt: es.enter_context(nc.sbuf_tensor(name, shape, dt))
        psa = lambda name, shape, dt: es.enter_context(nc.psum_tensor(name, shape, dt))
        alloc_common(ctx, es)
        XT = sb("XT", [128, NTILE * 1024], F32)
        XLN = sb("XLN", [128, NTILE * 1024], BF16)
        XLF = sb("XLF", [128, 8 * NTOK], BF16)
        WF = sb("WF", [128, 8 * 1024], F32)
        WTN = XLN
        WT = sb("WT", [128, 8 * 1024], BF16)
        IDN = sb("IDN", [128, 128], BF16)
        STATS = sb("STATS", [128, 64], F32)
        SQ = sb("SQ", [128, 1024], F32)
        SMALL = sb("SMALL", [128, 8], F32)
        VROW = sb("VROW", [1, 1024], F32)
        BB = sb("BB", [128, 1024], F32)
        YT = sb("YT", [128, 512], F32)
        ctx.ps["ptA"] = psa("ptA", [128, 128], BF16)
        ctx.ps["ptB"] = psa("ptB", [128, 128], BF16)
        PSJ = [psa("psj0", [128, 512], F32), psa("psj1", [128, 512], F32)]
        PSB = psa("psb", [128, 1024], F32)
        PSD = psa("psd", [128, 8], F32)

        x_res = x_in.rearrange("(i p) e -> p i e", p=128)
        ctx.dma("sync", XT[:, :].rearrange("p (i e) -> p i e", i=NTILE), x_res, "dS")
        w_res = w_in.rearrange("(c p) e -> p c e", p=128)
        ctx.dma("sync", WF[:, :].rearrange("p (c e) -> p c e", c=8), w_res, "dS")
        ctx.dma("scalar", IDN[:, :], ident_in, "dC")
        ctx.dma("scalar", SMALL[0:1, 2:3], alpha_in, "dC")
        ctx.dma("scalar", VROW[:, :], bias_in, "dC")

        ctx.wait("vector", "dS", p.val("dS"))
        ctx.op("vector", "reduce_sum", SMALL[:, 0:1], WF[:, :], axis=AX.X,
               apply_absolute_value=True)
        v_rows = ctx.sig("vector", "sv")
        ctx.wait("tensor", "sv", v_rows)
        ctx.op("vector", "memset", SMALL[:, 1:2], 1.0)
        v_ones = ctx.sig("vector", "sv")
        ctx.wait("tensor", "sv", v_ones)
        ctx.op("tensor", "matmul", PSD[0:1, 0:1], SMALL[:, 0:1], SMALL[:, 1:2],
               start=True, stop=True)
        t_sum = ctx.sig("tensor", "st")
        ctx.wait("scalar", "st", t_sum)
        ctx.op("scalar", "mul", SMALL[0:1, 3:4], PSD[0:1, 0:1], 0.7 / (1024.0 * 1024.0))
        s_delta = ctx.sig("scalar", "ss")
        ctx.wait("tensor", "ss", s_delta)
        ctx.wait("tensor", "sv", 1)
        ctx.op("tensor", "matmul", PSD[:, 1:2], ctx.sb["ones"][:, :], SMALL[0:1, 3:4],
               start=True, stop=True)
        ctx.wait("tensor", "dC", p.val("dC"))
        ctx.op("tensor", "matmul", PSD[:, 2:3], ctx.sb["ones"][:, :], SMALL[0:1, 2:3],
               start=True, stop=True)
        t_bc = ctx.sig("tensor", "st")
        ctx.wait("scalar", "st", t_bc)
        ctx.op("scalar", "copy", SMALL[:, 4:5], PSD[:, 1:2])
        ctx.op("scalar", "mul", SMALL[:, 5:6], PSD[:, 1:2], -1.0)
        ctx.op("scalar", "copy", SMALL[:, 6:7], PSD[:, 2:3])
        s_cols = ctx.sig("scalar", "ss")

        ctx.wait("vector", "ss", s_cols)
        ctx.op("vector", "tensor_scalar", SQ[:, 0:1024], WF[:, 0:1024],
               SMALL[:, 4:5], None, op0=ALU.is_gt)
        p.ops["vector"].pop()
        GT = XLF
        ctx.op("vector", "tensor_scalar", GT[:, 0:8192], WF[:, :], SMALL[:, 4:5],
               None, op0=ALU.is_gt)
        ctx.op("vector", "tensor_scalar", GT[:, 8192:16384], WF[:, :], SMALL[:, 5:6],
               None, op0=ALU.is_lt)
        ctx.drain("vector")
        ctx.op("vector", "tensor_sub", WTN[:, 0:8192], GT[:, 0:8192], GT[:, 8192:16384])
        v_tern = ctx.sig("vector", "sv")

        pst = [ctx.ps["ptA"], ctx.ps["ptB"]]
        ctx.wait("tensor", "sv", v_tern)
        ctx.wait("tensor", "dC", p.val("dC"))
        k = 0
        base_ss = p.val("ss")
        for oc in range(8):
            for ec in range(8):
                pt = pst[k % 2][:, :]
                ctx.wait("tensor", "ss", base_ss + k - 1 if k >= 2 else base_ss)
                ctx.op("tensor", "transpose", pt,
                       WTN[:, oc * 1024 + ec * 128: oc * 1024 + ec * 128 + 128], IDN[:, :])
                tv = ctx.sig("tensor", "st")
                ctx.wait("scalar", "st", tv)
                ctx.op("scalar", "copy", WT[:, ec * 1024 + oc * 128: ec * 1024 + oc * 128 + 128], pt)
                ctx.sig("scalar", "ss")
                k += 1
        wt_done = p.val("ss")

        ctx.wait("tensor", "dC", p.val("dC"))
        ctx.op("tensor", "matmul", PSB[:, 0:512], ctx.sb["ones"][:, :], VROW[0:1, 0:512],
               start=True, stop=True)
        ctx.op("tensor", "matmul", PSB[:, 512:1024], ctx.sb["ones"][:, :], VROW[0:1, 512:1024],
               start=True, stop=True)
        t_bb = ctx.sig("tensor", "st")
        ctx.wait("scalar", "st", t_bb)
        ctx.op("scalar", "copy", BB[:, :], PSB[:, :])
        bb_done = ctx.sig("scalar", "ss")

        ctx.wait("vector", "dS", 32)
        ctx.wait("vector", "ss", wt_done)
        ln_done = emit_ln(ctx, XT, XLN, STATS, SQ)

        fm_done = emit_tm_to_fm(ctx, XLN, XLF, IDN[:, :], ln_done)

        ctx.wait("tensor", "ss", wt_done)
        ctx.wait("tensor", "ss", fm_done)
        ctx.wait("vector", "ss", bb_done)
        base_v = p.val("sv")
        kk = 0
        for i in range(NTILE):
            for j in range(2):
                psj = PSJ[j][:, :]
                if kk >= 2:
                    ctx.wait("tensor", "sv", base_v + 2 * (kk - 1))
                for c in range(8):
                    ctx.op("tensor", "matmul", psj,
                           XLF[:, c * 2048 + i * 128: c * 2048 + i * 128 + 128],
                           WT[:, c * 1024 + j * 512: c * 1024 + (j + 1) * 512],
                           start=(c == 0), stop=(c == 7))
                tv = ctx.sig("tensor", "st")
                ctx.wait("vector", "st", tv)
                sl = slice(i * 1024 + j * 512, i * 1024 + (j + 1) * 512)
                bsl = slice(j * 512, (j + 1) * 512)
                ctx.op("vector", "scalar_tensor_tensor", YT[:, :], psj,
                       SMALL[:, 6:7], BB[:, bsl], op0=ALU.mult, op1=ALU.add)
                ctx.drain("vector")
                ctx.op("vector", "tensor_add", XT[:, sl], XT[:, sl], YT[:, :])
                ctx.sig("vector", "sv", 2)
                kk += 1
        out_ready = p.val("sv")

        ctx.wait("sync", "sv", out_ready)
        ctx.dma("sync", x_out.rearrange("(i p) e -> p i e", p=128),
                XT[:, :].rearrange("p (i e) -> p i e", i=NTILE), "dO")
        ctx.wait("sync", "dO", p.val("dO"))
        finish(ctx)
    return nc



def emit_proj_fm(ctx, wt_view, brow_bf, xlf, psA, obuf, obuf_sl, out_dram, gate_vals):
    p = ctx.p
    ones_bf = ctx.sb["ones_bf"]
    ctx.wait("scalar", "dOa", p.val("dOa"))
    ctx.wait("scalar", "dOb", p.val("dOb"))
    par_sem = ["dOa", "dOb"]
    last_dma = [None, None]
    for m in range(8):
        ctx.wait("tensor", "ss", p.val("ss"))
        for g in gate_vals:
            ctx.wait("tensor", g[0], g[1])
        gate_vals = ()
        for t in range(4):
            for c in range(8):
                ctx.op("tensor", "matmul", psA[t][:, :],
                       wt_view[:, c * 1024 + m * 128: c * 1024 + m * 128 + 128],
                       xlf[:, c * 2048 + t * 512: c * 2048 + (t + 1) * 512],
                       start=(c == 0), stop=False)
            ctx.op("tensor", "matmul", psA[t][:, :],
                   brow_bf[0:1, m * 128: m * 128 + 128],
                   ones_bf[0:1, 0:512], start=False, stop=True)
        tv = ctx.sig("tensor", "st")
        ctx.wait("scalar", "st", tv)
        sl = obuf_sl[m % 2]
        if last_dma[m % 2] is not None:
            ctx.wait("scalar", par_sem[m % 2], last_dma[m % 2])
        for t in range(4):
            ctx.op("scalar", "copy", obuf[:, sl.start + t * 512: sl.start + (t + 1) * 512], psA[t][:, :])
        sv = ctx.sig("scalar", "ss")
        ctx.wait("sync", "ss", sv)
        out_ap = bass.AP(out_dram.tensor, m * 128 * NTOK, [[NTOK, 128], [1, NTOK]])
        last_dma[m % 2] = ctx.dma("sync", out_ap, obuf[:, sl], par_sem[m % 2])


def emit_proj_tm(ctx, wt_view, brow_bf, xlf, psB, obuf, out_dram, gate_vals,
                 nout=1024, out_sbuf=None, resid_xt=None):
    p = ctx.p
    ones_bf = ctx.sb["ones_bf"]
    ctx.wait("scalar", "dOa", p.val("dOa"))
    ctx.wait("scalar", "dOb", p.val("dOb"))
    par_sem = ["dOa", "dOb"]
    nj = nout // 512
    last_dma = [None, None]
    for i in range(NTILE):
        ctx.wait("tensor", "ss", p.val("ss"))
        for g in gate_vals:
            ctx.wait("tensor", g[0], g[1])
        gate_vals = ()
        for j in range(nj):
            for c in range(8):
                ctx.op("tensor", "matmul", psB[j][:, :],
                       xlf[:, c * 2048 + i * 128: c * 2048 + i * 128 + 128],
                       wt_view[:, c * nout + j * 512: c * nout + (j + 1) * 512],
                       start=(c == 0), stop=False)
            ctx.op("tensor", "matmul", psB[j][:, :],
                   ones_bf[0:1, 0:128],
                   brow_bf[0:1, j * 512: (j + 1) * 512], start=False, stop=True)
        tv = ctx.sig("tensor", "st")
        if out_sbuf is not None:
            ctx.wait("scalar", "st", tv)
            for j in range(nj):
                ctx.op("scalar", "copy", out_sbuf[:, i * nout + j * 512: i * nout + (j + 1) * 512],
                       psB[j][:, :])
            ctx.sig("scalar", "ss")
        elif resid_xt is not None:
            ctx.wait("vector", "st", tv)
            for j in range(nj):
                sl = slice(i * nout + j * 512, i * nout + (j + 1) * 512)
                ctx.op("vector", "tensor_add", resid_xt[:, sl], resid_xt[:, sl], psB[j][:, :])
            ctx.sig("vector", "ss")
        else:
            ctx.wait("scalar", "st", tv)
            base = (i % 2) * nout
            if last_dma[i % 2] is not None:
                ctx.wait("scalar", par_sem[i % 2], last_dma[i % 2])
            for j in range(nj):
                ctx.op("scalar", "copy", obuf[:, base + j * 512: base + (j + 1) * 512], psB[j][:, :])
            sv = ctx.sig("scalar", "ss")
            ctx.wait("sync", "ss", sv)
            out_ap = bass.AP(out_dram.tensor, i * 128 * nout, [[nout, 128], [1, nout]])
            last_dma[i % 2] = ctx.dma("sync", out_ap, obuf[:, base: base + nout], par_sem[i % 2])


def build_s1a():
    nc = bass.Bass("TRN2", target_bir_lowering=False, debug=False)
    p = P()
    ctx = Ctx(nc, p)
    x_in = nc.dram_tensor("x", [NTOK, D], F32, kind="ExternalInput").ap()
    wq_in = nc.dram_tensor("wq", [D, D], F32, kind="ExternalInput").ap()
    wk_in = nc.dram_tensor("wk", [D, D], F32, kind="ExternalInput").ap()
    wv_in = nc.dram_tensor("wv", [D, D], F32, kind="ExternalInput").ap()
    bq_in = nc.dram_tensor("bq", [1, D], F32, kind="ExternalInput").ap()
    bk_in = nc.dram_tensor("bk", [1, D], F32, kind="ExternalInput").ap()
    bv_in = nc.dram_tensor("bv", [1, D], F32, kind="ExternalInput").ap()
    ident_in = nc.dram_tensor("ident", [128, 128], BF16, kind="ExternalInput").ap()
    qT_out = nc.dram_tensor("qT", [D, NTOK], BF16, kind="ExternalOutput").ap()
    kT_out = nc.dram_tensor("kT", [D, NTOK], BF16, kind="ExternalOutput").ap()
    v_out = nc.dram_tensor("v", [NTOK, D], BF16, kind="ExternalOutput").ap()

    with ExitStack() as es:
        sb = lambda name, shape, dt: es.enter_context(nc.sbuf_tensor(name, shape, dt))
        psa = lambda name, shape, dt: es.enter_context(nc.psum_tensor(name, shape, dt))
        alloc_common(ctx, es)
        XT = sb("XT", [128, NTILE * 1024], F32)
        ABF = sb("ABF", [128, 55296], BF16)
        XLN = ABF[:, 0:16384]
        XLF = ABF[:, 16384:32768]
        WNAT = ABF[:, 0:8192]
        WTQ = ABF[:, 8192:16384]
        WTK = ABF[:, 32768:40960]
        WTV = ABF[:, 40960:49152]
        OBUF = ABF[:, 49152:53248]
        IDN = sb("IDN", [128, 128], BF16)
        STATS = sb("STATS", [128, 64], F32)
        SQ = sb("SQ", [128, 1024], F32)
        VROW = sb("VROW", [1, 1024], F32)
        BROWS = sb("BROWS", [1, 3 * 1024], BF16)
        ctx.ps["ptA"] = psa("ptA", [128, 128], BF16)
        ctx.ps["ptB"] = psa("ptB", [128, 128], BF16)
        PSA = [psa(f"psa{t}", [128, 512], F32) for t in range(4)]

        x_res = x_in.rearrange("(i p) e -> p i e", p=128)
        ctx.dma("sync", XT[:, :].rearrange("p (i e) -> p i e", i=NTILE), x_res, "dS")
        ctx.dma("scalar", IDN[:, :], ident_in, "dC")
        for bi, b_in in enumerate((bq_in, bk_in, bv_in)):
            dv = ctx.dma("scalar", VROW[:, :], b_in, "dC")
            ctx.wait("vector", "dC", dv)
            ctx.op("vector", "tensor_copy", BROWS[0:1, bi * 1024:(bi + 1) * 1024], VROW[0:1, :])
            bv_done = ctx.sig("vector", "sv")
            if bi < 2:
                ctx.wait("scalar", "sv", bv_done)

        ctx.wait("vector", "dS", 16)
        ln_done = emit_ln(ctx, XT, XLN, STATS, SQ)
        fm_done = emit_tm_to_fm(ctx, XLN, XLF, IDN[:, :], ln_done)

        ctx.wait("gpsimd", "st", p.val("st"))
        wq_done = emit_weight_prep(ctx, wq_in, WNAT, WTQ, IDN[:, :])
        ctx.wait("gpsimd", "st", p.val("st"))
        wk_done = emit_weight_prep(ctx, wk_in, WNAT, WTK, IDN[:, :])
        ctx.wait("gpsimd", "st", p.val("st"))
        wv_done = emit_weight_prep(ctx, wv_in, WNAT, WTV, IDN[:, :])

        emit_proj_fm(ctx, WTQ, BROWS[:, 0:1024], XLF, PSA, OBUF,
                     [slice(0, 2048), slice(2048, 4096)], qT_out,
                     [("ss", wq_done), ("sv", p.val("sv"))])
        emit_proj_fm(ctx, WTK, BROWS[:, 1024:2048], XLF, PSA, OBUF,
                     [slice(0, 2048), slice(2048, 4096)], kT_out,
                     [("ss", wk_done)])
        emit_proj_tm(ctx, WTV, BROWS[:, 2048:3072], XLF, PSA[:2], OBUF, v_out,
                     [("ss", wv_done)])
        ctx.wait("sync", "dO", p.val("dO"))
        finish(ctx)
    return nc


def emit_tm_to_fm_stream(ctx, src_tm, out_dram, obuf, ident_bf, wait_sv_val):
    p = ctx.p
    pst = [ctx.ps["ptA"], ctx.ps["ptB"]]
    par_sem = ["dOa", "dOb"]
    ctx.wait("scalar", "dOa", p.val("dOa"))
    ctx.wait("scalar", "dOb", p.val("dOb"))
    last_dma = [None, None]
    k = 0
    base_cp = p.val("ss")
    ctx.wait("tensor", "sv", wait_sv_val)
    ctx.wait("tensor", "dC", p.val("dC"))
    for dc in range(8):
        par = dc % 2
        if last_dma[par] is not None:
            ctx.wait("scalar", par_sem[par], last_dma[par])
        for i in range(NTILE):
            pt = pst[k % 2][:, :]
            ctx.wait("tensor", "ss", base_cp + k - 1 if k >= 2 else base_cp)
            ctx.op("tensor", "transpose", pt,
                   src_tm[:, i * 1024 + dc * 128: i * 1024 + dc * 128 + 128], ident_bf)
            tv = ctx.sig("tensor", "st")
            ctx.wait("scalar", "st", tv)
            ctx.op("scalar", "copy", obuf[:, par * 2048 + i * 128: par * 2048 + (i + 1) * 128], pt)
            ctx.sig("scalar", "ss")
            k += 1
        sv = p.val("ss")
        ctx.wait("sync", "ss", sv)
        out_ap = bass.AP(out_dram.tensor, dc * 128 * NTOK, [[NTOK, 128], [1, NTOK]])
        last_dma[par] = ctx.dma("sync", out_ap, obuf[:, par * 2048: par * 2048 + 2048], par_sem[par])


def emit_rope(ctx, qtm, costab, sintab, rs):
    for i in range(NTILE):
        l = i % LTILES
        base = i * 1024
        q1 = qtm[:, base:].ap if False else None
        a_q1 = bass.AP(qtm.tensor, qtm.offset + base, [[qtm.ap[0][0], 128], [64, 16], [1, 32]])
        a_q2 = bass.AP(qtm.tensor, qtm.offset + base + 32, [[qtm.ap[0][0], 128], [64, 16], [1, 32]])
        a_cos = bass.AP(costab.tensor, costab.offset + l * 32, [[costab.ap[0][0], 128], [0, 16], [1, 32]])
        a_sin = bass.AP(sintab.tensor, sintab.offset + l * 32, [[sintab.ap[0][0], 128], [0, 16], [1, 32]])
        a_t1 = bass.AP(rs.tensor, rs.offset, [[rs.ap[0][0], 128], [32, 16], [1, 32]])
        a_t2 = bass.AP(rs.tensor, rs.offset + 512, [[rs.ap[0][0], 128], [32, 16], [1, 32]])
        a_t3 = bass.AP(rs.tensor, rs.offset + 1024, [[rs.ap[0][0], 128], [32, 16], [1, 32]])
        a_t4 = bass.AP(rs.tensor, rs.offset + 1536, [[rs.ap[0][0], 128], [32, 16], [1, 32]])
        ctx.op("vector", "tensor_mul", a_t1, a_q1, a_cos)
        ctx.op("vector", "tensor_mul", a_t2, a_q2, a_sin)
        ctx.op("vector", "tensor_mul", a_t3, a_q2, a_cos)
        ctx.op("vector", "tensor_mul", a_t4, a_q1, a_sin)
        ctx.drain("vector")
        ctx.op("vector", "tensor_sub", a_q1, a_t1, a_t2)
        ctx.op("vector", "tensor_add", a_q2, a_t3, a_t4)
        ctx.drain("vector")
    return ctx.sig("vector", "sv")


def build_s3a():
    nc = bass.Bass("TRN2", target_bir_lowering=False, debug=False)
    p = P()
    ctx = Ctx(nc, p)
    x_in = nc.dram_tensor("x", [NTOK, D], F32, kind="ExternalInput").ap()
    wq_in = nc.dram_tensor("wq", [D, D], F32, kind="ExternalInput").ap()
    wk_in = nc.dram_tensor("wk", [D, D], F32, kind="ExternalInput").ap()
    wv_in = nc.dram_tensor("wv", [D, D], F32, kind="ExternalInput").ap()
    bq_in = nc.dram_tensor("bq", [1, D], F32, kind="ExternalInput").ap()
    bk_in = nc.dram_tensor("bk", [1, D], F32, kind="ExternalInput").ap()
    bv_in = nc.dram_tensor("bv", [1, D], F32, kind="ExternalInput").ap()
    cos_in = nc.dram_tensor("costab", [512, 32], F32, kind="ExternalInput").ap()
    sin_in = nc.dram_tensor("sintab", [512, 32], F32, kind="ExternalInput").ap()
    ident_in = nc.dram_tensor("ident", [128, 128], BF16, kind="ExternalInput").ap()
    qT_out = nc.dram_tensor("qT", [D, NTOK], BF16, kind="ExternalOutput").ap()
    kT_out = nc.dram_tensor("kT", [D, NTOK], BF16, kind="ExternalOutput").ap()
    v_out = nc.dram_tensor("v", [NTOK, D], BF16, kind="ExternalOutput").ap()

    with ExitStack() as es:
        sb = lambda name, shape, dt: es.enter_context(nc.sbuf_tensor(name, shape, dt))
        psa = lambda name, shape, dt: es.enter_context(nc.psum_tensor(name, shape, dt))
        alloc_common(ctx, es)
        XT = sb("XT", [128, NTILE * 1024], F32)
        ABF = sb("ABF", [128, 53248], BF16)
        XLN = ABF[:, 0:16384]
        XLF = ABF[:, 16384:32768]
        WNAT = ABF[:, 0:8192]
        WT = ABF[:, 8192:16384]
        QTM = ABF[:, 32768:49152]
        OBUF = ABF[:, 49152:53248]
        IDN = sb("IDN", [128, 128], BF16)
        STATS = sb("STATS", [128, 64], F32)
        SQ = sb("SQ", [128, 1024], F32)
        VROW = sb("VROW", [1, 1024], F32)
        BROWS = sb("BROWS", [1, 3 * 1024], BF16)
        COS = sb("COS", [128, 128], F32)
        SIN = sb("SIN", [128, 128], F32)
        RS = sb("RS", [128, 2048], F32)
        ctx.ps["ptA"] = psa("ptA", [128, 128], BF16)
        ctx.ps["ptB"] = psa("ptB", [128, 128], BF16)
        PSA = [psa(f"psa{t}", [128, 512], F32) for t in range(2)]

        x_res = x_in.rearrange("(i p) e -> p i e", p=128)
        ctx.dma("sync", XT[:, :].rearrange("p (i e) -> p i e", i=NTILE), x_res, "dS")
        ctx.dma("scalar", IDN[:, :], ident_in, "dC")
        ctx.dma("sync", COS[:, :].rearrange("p (l f) -> p l f", l=4),
                cos_in.rearrange("(l p) f -> p l f", p=128), "dS")
        ctx.dma("sync", SIN[:, :].rearrange("p (l f) -> p l f", l=4),
                sin_in.rearrange("(l p) f -> p l f", p=128), "dS")
        for bi, b_in in enumerate((bq_in, bk_in, bv_in)):
            dv = ctx.dma("scalar", VROW[:, :], b_in, "dC")
            ctx.wait("vector", "dC", dv)
            ctx.op("vector", "tensor_copy", BROWS[0:1, bi * 1024:(bi + 1) * 1024], VROW[0:1, :])
            bv_done = ctx.sig("vector", "sv")
            if bi < 2:
                ctx.wait("scalar", "sv", bv_done)

        ctx.wait("vector", "dS", p.val("dS"))
        ln_done = emit_ln(ctx, XT, XLN, STATS, SQ)
        fm_done = emit_tm_to_fm(ctx, XLN, XLF, IDN[:, :], ln_done)

        ctx.wait("gpsimd", "st", p.val("st"))
        wq_done = emit_weight_prep(ctx, wq_in, WNAT, WT, IDN[:, :])
        emit_proj_tm(ctx, WT, BROWS[:, 0:1024], XLF, PSA, OBUF, None,
                     [("ss", wq_done)], out_sbuf=QTM)
        ctx.wait("vector", "ss", p.val("ss"))
        ctx.wait("vector", "dS", 48)
        rq_done = emit_rope(ctx, QTM, COS[:, :], SIN[:, :], RS[:, :])
        emit_tm_to_fm_stream(ctx, QTM, qT_out, OBUF, IDN[:, :], rq_done)

        ctx.wait("gpsimd", "st", p.val("st"))
        wk_done = emit_weight_prep(ctx, wk_in, WNAT, WT, IDN[:, :])
        ctx.wait("tensor", "st", p.val("st"))
        emit_proj_tm(ctx, WT, BROWS[:, 1024:2048], XLF, PSA, OBUF, None,
                     [("ss", wk_done)], out_sbuf=QTM)
        ctx.wait("vector", "ss", p.val("ss"))
        rk_done = emit_rope(ctx, QTM, COS[:, :], SIN[:, :], RS[:, :])
        emit_tm_to_fm_stream(ctx, QTM, kT_out, OBUF, IDN[:, :], rk_done)

        ctx.wait("gpsimd", "st", p.val("st"))
        wv_done = emit_weight_prep(ctx, wv_in, WNAT, WT, IDN[:, :])
        emit_proj_tm(ctx, WT, BROWS[:, 2048:3072], XLF, PSA, OBUF, v_out,
                     [("ss", wv_done)])
        ctx.wait("sync", "dOa", p.val("dOa"))
        ctx.wait("sync", "dOb", p.val("dOb"))
        finish(ctx)
    return nc


def build_s3b():
    nc = bass.Bass("TRN2", target_bir_lowering=False, debug=False)
    p = P()
    ctx = Ctx(nc, p)
    x_in = nc.dram_tensor("x", [NTOK, D], F32, kind="ExternalInput").ap()
    qT_in = nc.dram_tensor("qT", [D, NTOK], BF16, kind="ExternalInput").ap()
    kT_in = nc.dram_tensor("kband", [LTILES * 4 * D, 512], BF16, kind="ExternalInput").ap()
    v_in = nc.dram_tensor("vband", [LTILES * 4 * 512, D], BF16, kind="ExternalInput").ap()
    wo_in = nc.dram_tensor("wo", [D, D], F32, kind="ExternalInput").ap()
    bo_in = nc.dram_tensor("bo", [1, D], F32, kind="ExternalInput").ap()
    masks_in = nc.dram_tensor("masks", [LTILES * 128, 512], F32, kind="ExternalInput").ap()
    ident_in = nc.dram_tensor("ident", [128, 128], BF16, kind="ExternalInput").ap()
    x_out = nc.dram_tensor("xo", [NTOK, D], F32, kind="ExternalOutput").ap()

    with ExitStack() as es:
        sb = lambda name, shape, dt: es.enter_context(nc.sbuf_tensor(name, shape, dt))
        psa = lambda name, shape, dt: es.enter_context(nc.psum_tensor(name, shape, dt))
        alloc_common(ctx, es)
        XT = sb("XT", [128, NTILE * 1024], F32)
        QF = sb("QF", [128, 8 * 2048], BF16)
        ZFM = sb("ZFM", [128, 8 * 2048], BF16)
        KBa = sb("KBa", [128, 4096], BF16)
        VBa = sb("VBa", [128, 4096], BF16)
        MASKS = sb("MASKS", [128, 4 * 512], F32)
        SC = sb("SC", [128, 2 * 512], F32)
        PB = sb("PB", [128, 2 * 512], BF16)
        PBT = sb("PBT", [128, 2 * 512], BF16)
        ZTM = sb("ZTM", [128, 2 * 1024], BF16)
        SMALL = sb("SMALL", [128, 16], F32)
        WNAT = sb("WNAT", [128, 8192], BF16)
        WTO = sb("WTO", [128, 8192], BF16)
        VROW = sb("VROW", [1, 1024], F32)
        BROW = sb("BROW", [1, 1024], BF16)
        IDN = sb("IDN", [128, 128], BF16)
        KBbH = nc.virtual_sbuf_tensor("KBb", [128, 4096], BF16)
        VBbH = nc.virtual_sbuf_tensor("VBb", [128, 4096], BF16)
        for _h, _off in ((KBbH, 0), (VBbH, 8192)):
            nc._tensor(_h.name, _h.shape, _h.dtype, type="SB")
            _m = nc.lookup_mloc(_h)
            _m.allocated = True
            _m.addr = nc.lookup_mloc(WNAT).addr + _off
        KBs = [KBa, KBbH]
        VBs = [VBa, VBbH]
        ctx.ps["ptA"] = psa("ptA", [128, 128], BF16)
        ctx.ps["ptB"] = psa("ptB", [128, 128], BF16)
        PSS = [psa(f"psS{k}", [128, 512], F32) for k in range(2)]
        PSZ = [psa(f"psZh{k}", [128, 64], F32) for k in range(2)]
        PSB = [psa(f"psb{j}", [128, 512], F32) for j in range(2)]

        x_res = x_in.rearrange("(i p) e -> p i e", p=128)
        ctx.dma("sync", XT[:, :].rearrange("p (i e) -> p i e", i=NTILE), x_res, "dS")
        ctx.dma("scalar", IDN[:, :], ident_in, "dC")
        ctx.dma("sync", QF[:, :].rearrange("p (c t) -> p c t", c=8),
                qT_in.rearrange("(c p) t -> p c t", p=128), "dS")
        ctx.dma("sync", MASKS[:, :].rearrange("p (l n) -> p l n", l=4),
                masks_in.rearrange("(l p) n -> p l n", p=128), "dS")
        dv = ctx.dma("scalar", VROW[:, :], bo_in, "dC")
        ctx.wait("vector", "dC", dv)
        ctx.op("vector", "tensor_copy", BROW[0:1, :], VROW[0:1, :])
        ctx.sig("vector", "sv")
        wo_done = emit_weight_prep(ctx, wo_in, WNAT, WTO, IDN[:, :])

        q_loaded = p.val("dS")
        wo_prep_st = p.val("st")
        ctx.wait("tensor", "dS", q_loaded)
        maskadd_done = [None, None]
        exp_done = [None, None]
        tp_done = [None, None]
        zevac_done = [None, None]
        kb_consumed = [None, None]
        vb_consumed = [None, None]
        ztmT_done = [None, None]
        kv_ready = [None, None]
        kv_ready_v = [None, None]
        ksem_par = ["dKa", "dKb"]
        vsem_par = ["dVa", "dVb"]
        pst = [ctx.ps["ptA"], ctx.ps["ptB"]]
        for b in range(B):
            for l in range(LTILES):
                i = b * 4 + l
                blp = i % 2
                if kb_consumed[blp] is not None:
                    ctx.wait("sync", "st", kb_consumed[blp])
                    ctx.wait("scalar", "st", vb_consumed[blp])
                elif blp == 1:
                    ctx.wait("sync", "st", wo_prep_st)
                    ctx.wait("scalar", "st", wo_prep_st)
                for slot in range(4):
                    blk = l * 4 + slot
                    ka = bass.AP(kT_in.tensor, (blk * D) * 512 + b * 128,
                                 [[512, 128], [128 * 512, 8], [1, 128]])
                    ko = bass.AP(KBs[blp][:, :].tensor, slot * 128,
                                 [[4096, 128], [512, 8], [1, 128]])
                    ctx.dma("sync", ko, ka, ksem_par[blp])
                    va = bass.AP(v_in.tensor, (blk * 512 + b * 128) * D,
                                 [[D, 128], [1, D]])
                    ctx.dma("scalar", VBs[blp][:, slot * 1024:(slot + 1) * 1024],
                            va, vsem_par[blp])
                kv_ready[blp] = p.val(ksem_par[blp])
                kv_ready_v[blp] = p.val(vsem_par[blp])
                for h in range(16):
                    hp = h % 2
                    hb = (h % 2) * 64
                    c = h // 2
                    sb8 = hp * 8
                    if h == 0:
                        ctx.wait("tensor", ksem_par[blp], kv_ready[blp])
                    if maskadd_done[hp] is not None:
                        ctx.wait("tensor", "sv", maskadd_done[hp])
                    ctx.op("tensor", "matmul", PSS[hp][:, :],
                           QF[hb:hb + 64, c * 2048 + b * 512 + l * 128: c * 2048 + b * 512 + l * 128 + 128],
                           KBs[blp][hb:hb + 64, c * 512: (c + 1) * 512],
                           start=True, stop=True)
                    tsc = ctx.sig("tensor", "st")
                    if h == 15:
                        kb_consumed[blp] = tsc
                    ctx.wait("vector", "st", tsc)
                    if exp_done[hp] is not None:
                        ctx.wait("vector", "ss", exp_done[hp])
                    ctx.op("vector", "tensor_add", SC[:, hp * 512:(hp + 1) * 512],
                           PSS[hp][:, :], MASKS[:, l * 512:(l + 1) * 512])
                    ctx.drain("vector")
                    ctx.op("vector", "reduce_max", SMALL[:, sb8:sb8 + 1],
                           SC[:, hp * 512:(hp + 1) * 512], axis=AX.X)
                    ctx.drain("vector")
                    ctx.op("vector", "tensor_scalar_mul", SMALL[:, sb8 + 1:sb8 + 2],
                           SMALL[:, sb8:sb8 + 1], -1.0 / 8.0)
                    maskadd_done[hp] = ctx.sig("vector", "sv")
                    ctx.wait("scalar", "sv", maskadd_done[hp])
                    if tp_done[hp] is not None:
                        ctx.wait("scalar", "st", tp_done[hp])
                    ctx.op("scalar", "activation", PB[:, hp * 512:(hp + 1) * 512],
                           SC[:, hp * 512:(hp + 1) * 512], AF.Exp,
                           bias=SMALL[:, sb8 + 1:sb8 + 2], scale=1.0 / 8.0,
                           accum_out=SMALL[:, sb8 + 2:sb8 + 3])
                    exp_done[hp] = ctx.sig("scalar", "ss")
                    ctx.wait("vector", "ss", exp_done[hp])
                    ctx.op("vector", "reciprocal", SMALL[:, sb8 + 3:sb8 + 4],
                           SMALL[:, sb8 + 2:sb8 + 3])
                    rc_done = ctx.sig("vector", "sv")
                    ctx.wait("tensor", "ss", exp_done[hp])
                    for t in range(4):
                        ctx.wait("tensor", "ss", p.val("ss") - 1 if t >= 2 else p.val("ss"))
                        ctx.op("tensor", "transpose", pst[t % 2][:, :],
                               PB[:, hp * 512 + t * 128: hp * 512 + (t + 1) * 128], IDN[:, :])
                        tv = ctx.sig("tensor", "st")
                        ctx.wait("scalar", "st", tv)
                        ctx.op("scalar", "copy",
                               PBT[:, hp * 512 + t * 128: hp * 512 + (t + 1) * 128],
                               pst[t % 2][:, :])
                        ctx.sig("scalar", "ss")
                    tp_done[hp] = p.val("st")
                    ctx.wait("tensor", "ss", p.val("ss"))
                    if h < 2:
                        ctx.wait("tensor", vsem_par[blp], kv_ready_v[blp])
                    if zevac_done[hp] is not None:
                        ctx.wait("tensor", "sv", zevac_done[hp])
                    if h == 0 and ztmT_done[blp] is not None:
                        ctx.wait("tensor", "st", ztmT_done[blp])
                    for t in range(4):
                        ctx.op("tensor", "matmul", PSZ[hp][:, :],
                               PBT[:, hp * 512 + t * 128: hp * 512 + (t + 1) * 128],
                               VBs[blp][:, t * 1024 + h * 64: t * 1024 + h * 64 + 64],
                               start=(t == 0), stop=(t == 3))
                    tz = ctx.sig("tensor", "st")
                    if h == 15:
                        vb_consumed[blp] = tz
                    ctx.wait("vector", "st", tz)
                    ctx.wait("vector", "sv", rc_done)
                    if h == 0 and ztmT_done[blp] is not None:
                        ctx.wait("vector", "st", ztmT_done[blp])
                    ctx.op("vector", "tensor_scalar_mul", ZTM[:, blp * 1024 + h * 64: blp * 1024 + (h + 1) * 64],
                           PSZ[hp][:, :], SMALL[:, sb8 + 3:sb8 + 4])
                    zevac_done[hp] = ctx.sig("vector", "sv")
                zs = p.val("sv")
                ctx.wait("tensor", "sv", zs)
                for dc in range(8):
                    ctx.wait("tensor", "ss", p.val("ss") - 1 if dc >= 2 else p.val("ss"))
                    ctx.op("tensor", "transpose", pst[dc % 2][:, :],
                           ZTM[:, blp * 1024 + dc * 128: blp * 1024 + (dc + 1) * 128], IDN[:, :])
                    tv = ctx.sig("tensor", "st")
                    ctx.wait("scalar", "st", tv)
                    ctx.op("scalar", "copy", ZFM[:, dc * 2048 + i * 128: dc * 2048 + (i + 1) * 128],
                           pst[dc % 2][:, :])
                    ctx.sig("scalar", "ss")
                ztmT_done[blp] = p.val("st")

        emit_proj_tm(ctx, WTO, BROW, ZFM, PSB, None, None,
                     [("ss", p.val("ss"))], resid_xt=XT)
        out_ready = p.val("ss")
        ctx.wait("sync", "ss", out_ready)
        ctx.dma("sync", x_out.rearrange("(i p) e -> p i e", p=128),
                XT[:, :].rearrange("p (i e) -> p i e", i=NTILE), "dO")
        ctx.wait("sync", "dO", p.val("dO"))
        finish(ctx)
    return nc


def host_rope_tables():
    DH = 64
    inv = (1.0 / 10000.0 ** (np.arange(0, DH, 2, dtype=np.float32) / DH)).astype(np.float32)
    pos = np.arange(T, dtype=np.float32)
    fr = (pos[:, None] * inv[None, :]).astype(np.float32)
    return np.cos(fr).astype(np.float32), np.sin(fr).astype(np.float32)


def host_s3_masks(core_id):
    WIN = 256
    out = np.zeros((LTILES, 128, 4, 128), np.float32)
    NEG = -1.0e30
    for l in range(LTILES):
        g = l * 8 + core_id
        kts = [0, g - 2, g - 1, g]
        seen = set()
        for slot, kt in enumerate(kts):
            if kt < 0 or (kt in seen and not (slot == 0)):
                out[l, :, slot, :] = NEG
                continue
            if slot == 0 and (g - 2 <= 0 <= g):
                out[l, :, slot, :] = NEG
                continue
            seen.add(kt)
            ii = g * 128 + np.arange(128)[:, None]
            jj = kt * 128 + np.arange(128)[None, :]
            keep = (jj <= ii) & ((ii - jj < WIN) | (jj == 0))
            out[l, :, slot, :] = np.where(keep, 0.0, NEG)
    return out.reshape(LTILES * 128, 512)


def run_s3(x_full, q_w, q_b, k_w, k_b, v_w, v_b, o_w, o_b):
    nc_a = build_s3a()
    cos_t, sin_t = host_rope_tables()
    in_maps = []
    for c in range(NCORES):
        rows = ((np.arange(LTILES)[:, None] * 8 + c) * 128 + np.arange(128)[None, :]).reshape(-1)
        in_maps.append({
            "x": shard_tokens(x_full, c),
            "wq": np.asarray(q_w, np.float32), "wk": np.asarray(k_w, np.float32),
            "wv": np.asarray(v_w, np.float32),
            "bq": np.asarray(q_b, np.float32).reshape(1, D),
            "bk": np.asarray(k_b, np.float32).reshape(1, D),
            "bv": np.asarray(v_b, np.float32).reshape(1, D),
            "costab": np.ascontiguousarray(cos_t[rows]),
            "sintab": np.ascontiguousarray(sin_t[rows]),
            "ident": IDENT_BF,
        })
    res = run_bass_kernel_spmd(nc_a, in_maps, core_ids=list(range(NCORES)))
    qT = [res.results[c]["qT"] for c in range(NCORES)]
    kT_all = np.concatenate([res.results[c]["kT"] for c in range(NCORES)], axis=0)
    v_all = np.concatenate([res.results[c]["v"] for c in range(NCORES)], axis=0)

    nc_b = build_s3b()
    in_maps = []
    for c in range(NCORES):
        in_maps.append({
            "x": shard_tokens(x_full, c),
            "qT": qT[c],
            "kband": host_gather_s3_kband(kT_all, c),
            "vband": host_gather_s3_vband(v_all, c),
            "wo": np.asarray(o_w, np.float32),
            "bo": np.asarray(o_b, np.float32).reshape(1, D),
            "masks": host_s3_masks(c),
            "ident": IDENT_BF,
        })
    res = run_bass_kernel_spmd(nc_b, in_maps, core_ids=list(range(NCORES)))
    return unshard_tokens([res.results[c]["xo"] for c in range(NCORES)])


def host_gather_s3_kband(kT_all, core):
    out = np.zeros((LTILES, 4, 1024, 4, 128), dtype=kT_all.dtype)
    for l in range(LTILES):
        g = l * 8 + core
        for slot, kt in enumerate([0, g - 2, g - 1, g]):
            ktt = max(kt, 0)
            r, lt = ktt % 8, ktt // 8
            for b in range(B):
                out[l, slot, :, b] = kT_all[r * D:(r + 1) * D, b * 512 + lt * 128: b * 512 + (lt + 1) * 128]
    return np.ascontiguousarray(out.reshape(LTILES * 4 * 1024, 512))


def host_gather_s3_vband(v_all, core):
    out = np.zeros((LTILES, 4, 4, 128, D), dtype=v_all.dtype)
    for l in range(LTILES):
        g = l * 8 + core
        for slot, kt in enumerate([0, g - 2, g - 1, g]):
            ktt = max(kt, 0)
            r, lt = ktt % 8, ktt // 8
            for b in range(B):
                out[l, slot, b] = v_all[r * NTOK + b * 512 + lt * 128: r * NTOK + b * 512 + (lt + 1) * 128]
    return np.ascontiguousarray(out.reshape(LTILES * 4 * 512, D))

def build_s1b():
    nc = bass.Bass("TRN2", target_bir_lowering=False, debug=False)
    p = P()
    ctx = Ctx(nc, p)
    x_in = nc.dram_tensor("x", [NTOK, D], F32, kind="ExternalInput").ap()
    qT_in = nc.dram_tensor("qT", [D, NTOK], BF16, kind="ExternalInput").ap()
    kT_in = nc.dram_tensor("kTall", [NCORES * D, NTOK], BF16, kind="ExternalInput").ap()
    v_in = nc.dram_tensor("vall", [NCORES * NTOK, D], BF16, kind="ExternalInput").ap()
    x_out = nc.dram_tensor("xo", [NTOK, D], F32, kind="ExternalOutput").ap()

    with ExitStack() as es:
        sb = lambda name, shape, dt: es.enter_context(nc.sbuf_tensor(name, shape, dt))
        psa = lambda name, shape, dt: es.enter_context(nc.psum_tensor(name, shape, dt))
        alloc_common(ctx, es)
        XT = sb("XT", [128, NTILE * 1024], F32)
        ZAC = sb("ZAC", [128, 4 * 1024], F32)
        KCH = sb("KCH", [128, 2 * 4096], BF16)
        VCH = sb("VCH", [128, 2 * 4096], BF16)
        QTL = sb("QTL", [128, 4096], BF16)
        PB = sb("PB", [128, 1024], BF16)
        PBT = sb("PBT", [128, 1024], BF16)
        SMALL = sb("SMALL", [128, 24], F32)
        IDN = sb("IDN", [128, 128], BF16)
        ctx.ps["ptA"] = psa("ptA", [128, 128], BF16)
        ctx.ps["ptB"] = psa("ptB", [128, 128], BF16)
        PSS = [psa(f"psS{k}", [128, 512], F32) for k in range(2)]
        PSZ = [psa(f"psZ{k}", [128, 1024], F32) for k in range(2)]

        ident_in = nc.dram_tensor("ident", [128, 128], BF16, kind="ExternalInput").ap()
        ctx.dma("scalar", IDN[:, :], ident_in, "dC")

        x_res = x_in.rearrange("(i p) e -> p i e", p=128)
        ctx.dma("sync", XT[:, :].rearrange("p (i e) -> p i e", i=NTILE), x_res, "dS")

        ctx.wait("tensor", "dC", p.val("dC"))
        ctx.wait("vector", "dS", p.val("dS"))
        kv_consumed = [None, None]
        exp_done = [None, None]
        tp_done = [None, None]
        zadd_done = [None, None]
        for b in range(B):
            qa = bass.AP(qT_in.tensor, b * 512,
                         [[NTOK, 128], [128 * NTOK, 8], [128, 4], [1, 128]])
            qs = QTL[:, :].rearrange("p (c l j) -> p c l j", c=8, l=4)
            ctx.wait("sync", "st", p.val("st"))
            ctx.wait("scalar", "st", p.val("st"))
            dq = ctx.dma("sync", qs, qa, "dQ")
            ctx.op("vector", "memset", SMALL[:, 0:4], -3.0e38)
            ctx.op("vector", "memset", SMALL[:, 4:8], 0.0)
            ctx.op("vector", "memset", ZAC[:, :], 0.0)
            zinit = ctx.sig("vector", "sv")
            ctx.wait("tensor", "dQ", dq)
            kv_dma = [None, None]
            for kc in range(8):
                buf = kc % 2
                if kv_consumed[buf] is not None:
                    ctx.wait("sync", "st", kv_consumed[buf])
                    ctx.wait("scalar", "st", kv_consumed[buf])
                ksem = "dKa" if buf == 0 else "dKb"
                vsem = "dVa" if buf == 0 else "dVb"
                for gs in range(4):
                    g = kc * 4 + gs
                    r, l = g % 8, g // 8
                    ka = bass.AP(kT_in.tensor, (r * D) * NTOK + b * 512 + l * 128,
                                 [[NTOK, 128], [128 * NTOK, 8], [1, 128]])
                    ko = KCH[:, buf * 4096: buf * 4096 + 4096].rearrange(
                        "p (c j) -> p c j", c=8)[:, :, gs * 128:(gs + 1) * 128]
                    ctx.dma("sync", ko, ka, ksem)
                    va = bass.AP(v_in.tensor, (r * NTOK + b * 512 + l * 128) * D,
                                 [[D, 128], [1, D]])
                    ctx.dma("scalar", VCH[:, buf * 4096 + gs * 1024: buf * 4096 + (gs + 1) * 1024],
                            va, vsem)
                kv_dma[buf] = (p.val(ksem), p.val(vsem))
                for l in range(4):
                    lp = l % 2
                    sb8 = 8 + lp * 8
                    if l == 0:
                        ctx.wait("tensor", ksem, kv_dma[buf][0])
                    if exp_done[lp] is not None:
                        ctx.wait("tensor", "ss", exp_done[lp])
                    for c in range(8):
                        ctx.op("tensor", "matmul", PSS[lp][:, :],
                               QTL[:, c * 512 + l * 128: c * 512 + l * 128 + 128],
                               KCH[:, buf * 4096 + c * 512: buf * 4096 + (c + 1) * 512],
                               start=(c == 0), stop=(c == 7))
                    tsc = ctx.sig("tensor", "st")
                    if l == 3:
                        kv_consumed[buf] = None
                    ctx.wait("vector", "st", tsc)
                    if exp_done[lp] is not None:
                        ctx.wait("vector", "ss", exp_done[lp])
                    ctx.op("vector", "reduce_max", SMALL[:, sb8:sb8 + 1], PSS[lp][:, :], axis=AX.X)
                    ctx.drain("vector")
                    ctx.op("vector", "tensor_max", SMALL[:, sb8 + 1:sb8 + 2], SMALL[:, l:l + 1], SMALL[:, sb8:sb8 + 1])
                    ctx.drain("vector")
                    ctx.op("vector", "tensor_sub", SMALL[:, sb8 + 2:sb8 + 3], SMALL[:, l:l + 1], SMALL[:, sb8 + 1:sb8 + 2])
                    ctx.drain("vector")
                    ctx.op("vector", "tensor_scalar_mul", SMALL[:, sb8 + 2:sb8 + 3], SMALL[:, sb8 + 2:sb8 + 3], 1.0 / 32.0)
                    ctx.op("vector", "tensor_scalar_mul", SMALL[:, sb8 + 3:sb8 + 4], SMALL[:, sb8 + 1:sb8 + 2], -1.0 / 32.0)
                    ctx.op("vector", "tensor_copy", SMALL[:, l:l + 1], SMALL[:, sb8 + 1:sb8 + 2])
                    vst = ctx.sig("vector", "sv")
                    ctx.wait("scalar", "sv", vst)
                    if tp_done[lp] is not None:
                        ctx.wait("scalar", "st", tp_done[lp])
                    ctx.op("scalar", "activation", SMALL[:, sb8 + 4:sb8 + 5], SMALL[:, sb8 + 2:sb8 + 3], AF.Exp)
                    ctx.drain("scalar")
                    ctx.op("scalar", "activation", PB[:, lp * 512:(lp + 1) * 512], PSS[lp][:, :], AF.Exp,
                           bias=SMALL[:, sb8 + 3:sb8 + 4], scale=1.0 / 32.0, accum_out=SMALL[:, sb8 + 5:sb8 + 6])
                    sse = ctx.sig("scalar", "ss")
                    exp_done[lp] = sse
                    ctx.wait("vector", "ss", sse)
                    ctx.op("vector", "scalar_tensor_tensor", SMALL[:, 4 + l:5 + l],
                           SMALL[:, 4 + l:5 + l], SMALL[:, sb8 + 4:sb8 + 5], SMALL[:, sb8 + 5:sb8 + 6],
                           op0=ALU.mult, op1=ALU.add)
                    ctx.op("vector", "tensor_scalar_mul", ZAC[:, l * 1024:(l + 1) * 1024],
                           ZAC[:, l * 1024:(l + 1) * 1024], SMALL[:, sb8 + 4:sb8 + 5])
                    vrs = ctx.sig("vector", "sv")
                    pst = [ctx.ps["ptA"], ctx.ps["ptB"]]
                    ctx.wait("tensor", "ss", sse)
                    for t in range(4):
                        ctx.wait("tensor", "ss", p.val("ss") - 1 if t >= 2 else p.val("ss"))
                        ctx.op("tensor", "transpose", pst[t % 2][:, :],
                               PB[:, lp * 512 + t * 128: lp * 512 + (t + 1) * 128], IDN[:, :])
                        tv = ctx.sig("tensor", "st")
                        ctx.wait("scalar", "st", tv)
                        ctx.op("scalar", "copy", PBT[:, lp * 512 + t * 128: lp * 512 + (t + 1) * 128], pst[t % 2][:, :])
                        ctx.sig("scalar", "ss")
                    tp_done[lp] = p.val("st")
                    ctx.wait("tensor", "ss", p.val("ss"))
                    if l == 0:
                        ctx.wait("tensor", vsem, kv_dma[buf][1])
                    if zadd_done[lp] is not None:
                        ctx.wait("tensor", "sv", zadd_done[lp])
                    for j in range(2):
                        for t in range(4):
                            ctx.op("tensor", "matmul", PSZ[lp][:, j * 512:(j + 1) * 512],
                                   PBT[:, lp * 512 + t * 128: lp * 512 + (t + 1) * 128],
                                   VCH[:, buf * 4096 + t * 1024 + j * 512: buf * 4096 + t * 1024 + (j + 1) * 512],
                                   start=(t == 0), stop=(t == 3))
                    tz = ctx.sig("tensor", "st")
                    kv_consumed[buf] = tz
                    ctx.wait("vector", "st", tz)
                    ctx.wait("vector", "sv", vrs)
                    ctx.op("vector", "tensor_add", ZAC[:, l * 1024:(l + 1) * 1024],
                           ZAC[:, l * 1024:(l + 1) * 1024], PSZ[lp][:, :])
                    zadd_done[lp] = ctx.sig("vector", "sv")
            for l in range(4):
                i = b * 4 + l
                ctx.op("vector", "reciprocal", SMALL[:, 14:15], SMALL[:, 4 + l:5 + l])
                ctx.drain("vector")
                ctx.op("vector", "scalar_tensor_tensor", XT[:, i * 1024:(i + 1) * 1024],
                       ZAC[:, l * 1024:(l + 1) * 1024], SMALL[:, 14:15],
                       XT[:, i * 1024:(i + 1) * 1024], op0=ALU.mult, op1=ALU.add)
                ctx.drain("vector")
            ctx.sig("vector", "sv")

        out_ready = p.val("sv")
        ctx.wait("sync", "sv", out_ready)
        ctx.dma("sync", x_out.rearrange("(i p) e -> p i e", p=128),
                XT[:, :].rearrange("p (i e) -> p i e", i=NTILE), "dO")
        ctx.wait("sync", "dO", p.val("dO"))
        finish(ctx)
    return nc


def run_s1(x_full, Wq, bq, Wk, bk, Wv, bv):
    nc_a = build_s1a()
    in_maps = []
    for c in range(NCORES):
        in_maps.append({
            "x": shard_tokens(x_full, c),
            "wq": np.asarray(Wq, np.float32), "wk": np.asarray(Wk, np.float32),
            "wv": np.asarray(Wv, np.float32),
            "bq": np.asarray(bq, np.float32).reshape(1, D),
            "bk": np.asarray(bk, np.float32).reshape(1, D),
            "bv": np.asarray(bv, np.float32).reshape(1, D),
            "ident": IDENT_BF,
        })
    res = run_bass_kernel_spmd(nc_a, in_maps, core_ids=list(range(NCORES)))
    qT = [res.results[c]["qT"] for c in range(NCORES)]
    kT_all = np.concatenate([res.results[c]["kT"] for c in range(NCORES)], axis=0)
    v_all = np.concatenate([res.results[c]["v"] for c in range(NCORES)], axis=0)

    nc_b = build_s1b()
    in_maps = []
    for c in range(NCORES):
        in_maps.append({
            "x": shard_tokens(x_full, c),
            "qT": qT[c], "kTall": kT_all, "vall": v_all,
            "ident": IDENT_BF,
        })
    res = run_bass_kernel_spmd(nc_b, in_maps, core_ids=list(range(NCORES)))
    return unshard_tokens([res.results[c]["xo"] for c in range(NCORES)])



def build_s7b():
    nc = bass.Bass("TRN2", target_bir_lowering=False, debug=False)
    p = P()
    ctx = Ctx(nc, p)
    x_in = nc.dram_tensor("x", [NTOK, D], F32, kind="ExternalInput").ap()
    kT_in = nc.dram_tensor("kT", [D, NTOK], BF16, kind="ExternalInput").ap()
    qT_in = nc.dram_tensor("qTall", [NCORES * D, NTOK], BF16, kind="ExternalInput").ap()
    v_in = nc.dram_tensor("vall", [NCORES * NTOK, D], BF16, kind="ExternalInput").ap()
    mq_in = nc.dram_tensor("maskq", [4 * 128, T], BF16, kind="ExternalInput").ap()
    ident_in = nc.dram_tensor("ident", [128, 128], BF16, kind="ExternalInput").ap()
    x_out = nc.dram_tensor("xo", [NTOK, D], F32, kind="ExternalOutput").ap()

    with ExitStack() as es:
        sb = lambda name, shape, dt: es.enter_context(nc.sbuf_tensor(name, shape, dt))
        psa = lambda name, shape, dt: es.enter_context(nc.psum_tensor(name, shape, dt))
        alloc_common(ctx, es)
        XT = sb("XT", [128, NTILE * 1024], F32)
        ZAC = sb("ZAC", [128, 4 * 1024], F32)
        QCH = sb("QCH", [128, 2 * 4096], BF16)
        VCH = sb("VCH", [128, 2 * 4096], BF16)
        KTL = sb("KTL", [128, 4096], BF16)
        MQ = sb("MQ", [128, 4 * 4096], BF16)
        PB = sb("PB", [128, 1024], BF16)
        PBM = sb("PBM", [128, 1024], BF16)
        PBT = sb("PBT", [128, 1024], BF16)
        SMALL = sb("SMALL", [128, 24], F32)
        IDN = sb("IDN", [128, 128], BF16)
        ctx.ps["ptA"] = psa("ptA", [128, 128], BF16)
        ctx.ps["ptB"] = psa("ptB", [128, 128], BF16)
        PSS = [psa(f"psS{k}", [128, 512], F32) for k in range(2)]
        PSZ = [psa(f"psZ{k}", [128, 1024], F32) for k in range(2)]

        ctx.dma("scalar", IDN[:, :], ident_in, "dC")
        x_res = x_in.rearrange("(i p) e -> p i e", p=128)
        ctx.dma("sync", XT[:, :].rearrange("p (i e) -> p i e", i=NTILE), x_res, "dS")
        ctx.dma("sync", MQ[:, :].rearrange("p (m t) -> p m t", m=4),
                mq_in.rearrange("(m p) t -> p m t", p=128), "dS")

        ctx.wait("tensor", "dC", p.val("dC"))
        ctx.wait("vector", "dS", p.val("dS"))
        kv_consumed = [None, None]
        exp_done = [None, None]
        tp_done = [None, None]
        zadd_done = [None, None]
        for b in range(B):
            ka = bass.AP(kT_in.tensor, b * 512,
                         [[NTOK, 128], [128 * NTOK, 8], [128, 4], [1, 128]])
            ks = KTL[:, :].rearrange("p (c m j) -> p c m j", c=8, m=4)
            ctx.wait("sync", "st", p.val("st"))
            ctx.wait("scalar", "st", p.val("st"))
            dk = ctx.dma("sync", ks, ka, "dQ")
            ctx.op("vector", "memset", SMALL[:, 0:4], -3.0e38)
            ctx.op("vector", "memset", SMALL[:, 4:8], 0.0)
            ctx.op("vector", "memset", ZAC[:, :], 0.0)
            ctx.sig("vector", "sv")
            ctx.wait("tensor", "dQ", dk)
            kv_dma = [None, None]
            for kc in range(8):
                buf = kc % 2
                if kv_consumed[buf] is not None:
                    ctx.wait("sync", "st", kv_consumed[buf])
                    ctx.wait("scalar", "st", kv_consumed[buf])
                ksem = "dKa" if buf == 0 else "dKb"
                vsem = "dVa" if buf == 0 else "dVb"
                for gs in range(4):
                    g = kc * 4 + gs
                    r, l = g % 8, g // 8
                    qa = bass.AP(qT_in.tensor, (r * D) * NTOK + b * 512 + l * 128,
                                 [[NTOK, 128], [128 * NTOK, 8], [1, 128]])
                    qo = QCH[:, buf * 4096: buf * 4096 + 4096].rearrange(
                        "p (c j) -> p c j", c=8)[:, :, gs * 128:(gs + 1) * 128]
                    ctx.dma("sync", qo, qa, ksem)
                    va = bass.AP(v_in.tensor, (r * NTOK + b * 512 + l * 128) * D,
                                 [[D, 128], [1, D]])
                    ctx.dma("scalar", VCH[:, buf * 4096 + gs * 1024: buf * 4096 + (gs + 1) * 1024],
                            va, vsem)
                kv_dma[buf] = (p.val(ksem), p.val(vsem))
                for m in range(4):
                    lp = m % 2
                    sb8 = 8 + lp * 8
                    if m == 0:
                        ctx.wait("tensor", ksem, kv_dma[buf][0])
                    if exp_done[lp] is not None:
                        ctx.wait("tensor", "ss", exp_done[lp])
                    for c in range(8):
                        ctx.op("tensor", "matmul", PSS[lp][:, :],
                               KTL[:, c * 512 + m * 128: c * 512 + m * 128 + 128],
                               QCH[:, buf * 4096 + c * 512: buf * 4096 + (c + 1) * 512],
                               start=(c == 0), stop=(c == 7))
                    tsc = ctx.sig("tensor", "st")
                    ctx.wait("vector", "st", tsc)
                    if exp_done[lp] is not None:
                        ctx.wait("vector", "ss", exp_done[lp])
                    ctx.op("vector", "reduce_max", SMALL[:, sb8:sb8 + 1], PSS[lp][:, :], axis=AX.X)
                    ctx.drain("vector")
                    ctx.op("vector", "tensor_max", SMALL[:, sb8 + 1:sb8 + 2], SMALL[:, m:m + 1], SMALL[:, sb8:sb8 + 1])
                    ctx.drain("vector")
                    ctx.op("vector", "tensor_sub", SMALL[:, sb8 + 2:sb8 + 3], SMALL[:, m:m + 1], SMALL[:, sb8 + 1:sb8 + 2])
                    ctx.drain("vector")
                    ctx.op("vector", "tensor_scalar_mul", SMALL[:, sb8 + 2:sb8 + 3], SMALL[:, sb8 + 2:sb8 + 3], 1.0 / 32.0)
                    ctx.op("vector", "tensor_scalar_mul", SMALL[:, sb8 + 3:sb8 + 4], SMALL[:, sb8 + 1:sb8 + 2], -1.0 / 32.0)
                    ctx.op("vector", "tensor_copy", SMALL[:, m:m + 1], SMALL[:, sb8 + 1:sb8 + 2])
                    vst = ctx.sig("vector", "sv")
                    ctx.wait("scalar", "sv", vst)
                    if tp_done[lp] is not None:
                        ctx.wait("scalar", "st", tp_done[lp])
                    ctx.op("scalar", "activation", SMALL[:, sb8 + 4:sb8 + 5], SMALL[:, sb8 + 2:sb8 + 3], AF.Exp)
                    ctx.drain("scalar")
                    ctx.op("scalar", "activation", PB[:, lp * 512:(lp + 1) * 512], PSS[lp][:, :], AF.Exp,
                           bias=SMALL[:, sb8 + 3:sb8 + 4], scale=1.0 / 32.0, accum_out=SMALL[:, sb8 + 5:sb8 + 6])
                    sse = ctx.sig("scalar", "ss")
                    exp_done[lp] = sse
                    ctx.wait("vector", "ss", sse)
                    ctx.op("vector", "scalar_tensor_tensor", SMALL[:, 4 + m:5 + m],
                           SMALL[:, 4 + m:5 + m], SMALL[:, sb8 + 4:sb8 + 5], SMALL[:, sb8 + 5:sb8 + 6],
                           op0=ALU.mult, op1=ALU.add)
                    ctx.op("vector", "tensor_scalar_mul", ZAC[:, m * 1024:(m + 1) * 1024],
                           ZAC[:, m * 1024:(m + 1) * 1024], SMALL[:, sb8 + 4:sb8 + 5])
                    ctx.drain("vector")
                    ctx.op("vector", "tensor_mul", PBM[:, lp * 512:(lp + 1) * 512], PB[:, lp * 512:(lp + 1) * 512],
                           MQ[:, m * 4096 + kc * 512: m * 4096 + (kc + 1) * 512])
                    vrs = ctx.sig("vector", "sv")
                    pst = [ctx.ps["ptA"], ctx.ps["ptB"]]
                    ctx.wait("tensor", "sv", vrs)
                    for t in range(4):
                        ctx.wait("tensor", "ss", p.val("ss") - 1 if t >= 2 else p.val("ss"))
                        ctx.op("tensor", "transpose", pst[t % 2][:, :],
                               PBM[:, lp * 512 + t * 128: lp * 512 + (t + 1) * 128], IDN[:, :])
                        tv = ctx.sig("tensor", "st")
                        ctx.wait("scalar", "st", tv)
                        ctx.op("scalar", "copy", PBT[:, lp * 512 + t * 128: lp * 512 + (t + 1) * 128], pst[t % 2][:, :])
                        ctx.sig("scalar", "ss")
                    tp_done[lp] = p.val("st")
                    ctx.wait("tensor", "ss", p.val("ss"))
                    if m == 0:
                        ctx.wait("tensor", vsem, kv_dma[buf][1])
                    if zadd_done[lp] is not None:
                        ctx.wait("tensor", "sv", zadd_done[lp])
                    for j in range(2):
                        for t in range(4):
                            ctx.op("tensor", "matmul", PSZ[lp][:, j * 512:(j + 1) * 512],
                                   PBT[:, lp * 512 + t * 128: lp * 512 + (t + 1) * 128],
                                   VCH[:, buf * 4096 + t * 1024 + j * 512: buf * 4096 + t * 1024 + (j + 1) * 512],
                                   start=(t == 0), stop=(t == 3))
                    tz = ctx.sig("tensor", "st")
                    kv_consumed[buf] = tz
                    ctx.wait("vector", "st", tz)
                    ctx.op("vector", "tensor_add", ZAC[:, m * 1024:(m + 1) * 1024],
                           ZAC[:, m * 1024:(m + 1) * 1024], PSZ[lp][:, :])
                    zadd_done[lp] = ctx.sig("vector", "sv")
            for m in range(4):
                i = b * 4 + m
                ctx.op("vector", "reciprocal", SMALL[:, 14:15], SMALL[:, 4 + m:5 + m])
                ctx.drain("vector")
                ctx.op("vector", "scalar_tensor_tensor", XT[:, i * 1024:(i + 1) * 1024],
                       ZAC[:, m * 1024:(m + 1) * 1024], SMALL[:, 14:15],
                       XT[:, i * 1024:(i + 1) * 1024], op0=ALU.mult, op1=ALU.add)
                ctx.drain("vector")
            ctx.sig("vector", "sv")

        out_ready = p.val("sv")
        ctx.wait("sync", "sv", out_ready)
        ctx.dma("sync", x_out.rearrange("(i p) e -> p i e", p=128),
                XT[:, :].rearrange("p (i e) -> p i e", i=NTILE), "dO")
        ctx.wait("sync", "dO", p.val("dO"))
        finish(ctx)
    return nc


def host_s7_maskq(core):
    out = np.zeros((LTILES, 128, T), np.float32)
    for m in range(LTILES):
        gk = m * 8 + core
        kk = gk * 128 + np.arange(128)[:, None]
        qq = np.arange(T)[None, :]
        out[m] = (qq <= kk).astype(np.float32)
    return out.reshape(LTILES * 128, T).astype(BF16NP)


def run_s7(x_full, wq, wk, wv):
    nc_a = build_s1a()
    zb = np.zeros((1, D), np.float32)
    in_maps = []
    for c in range(NCORES):
        in_maps.append({
            "x": shard_tokens(x_full, c),
            "wq": np.asarray(wq, np.float32), "wk": np.asarray(wk, np.float32),
            "wv": np.asarray(wv, np.float32),
            "bq": zb, "bk": zb, "bv": zb,
            "ident": IDENT_BF,
        })
    res = run_bass_kernel_spmd(nc_a, in_maps, core_ids=list(range(NCORES)))
    qT_all = np.concatenate([res.results[c]["qT"] for c in range(NCORES)], axis=0)
    kT = [res.results[c]["kT"] for c in range(NCORES)]
    v_all = np.concatenate([res.results[c]["v"] for c in range(NCORES)], axis=0)

    nc_b = build_s7b()
    in_maps = []
    for c in range(NCORES):
        in_maps.append({
            "x": shard_tokens(x_full, c),
            "kT": kT[c], "qTall": qT_all, "vall": v_all,
            "maskq": host_s7_maskq(c),
            "ident": IDENT_BF,
        })
    res = run_bass_kernel_spmd(nc_b, in_maps, core_ids=list(range(NCORES)))
    return unshard_tokens([res.results[c]["xo"] for c in range(NCORES)])



def build_moe_prep():
    nc = bass.Bass("TRN2", target_bir_lowering=False, debug=False)
    p = P()
    ctx = Ctx(nc, p)
    H = 4096
    w1_in = nc.dram_tensor("w1v", [H, D], F32, kind="ExternalInput").ap()
    w1g_in = nc.dram_tensor("w1gc", [128, 32], F32, kind="ExternalInput").ap()
    w2_in = nc.dram_tensor("w2v", [D, H], F32, kind="ExternalInput").ap()
    w2g_in = nc.dram_tensor("w2gc", [128, 8], F32, kind="ExternalInput").ap()
    ident_in = nc.dram_tensor("ident", [128, 128], BF16, kind="ExternalInput").ap()
    w1t_out = nc.dram_tensor("w1nT", [D, H], BF16, kind="ExternalOutput").ap()
    w2t_out = nc.dram_tensor("w2nT", [H, D], BF16, kind="ExternalOutput").ap()

    EPS = 1e-12
    with ExitStack() as es:
        sb = lambda name, shape, dt: es.enter_context(nc.sbuf_tensor(name, shape, dt))
        psa = lambda name, shape, dt: es.enter_context(nc.psum_tensor(name, shape, dt))
        alloc_common(ctx, es)
        WS = sb("WS", [128, 4096], F32)
        SQ2 = sb("SQ2", [128, 4096], F32)
        WNB = sb("WNB", [128, 4096], BF16)
        WTB = sb("WTB", [128, 4096], BF16)
        G1C = sb("G1C", [128, 32], F32)
        G2C = sb("G2C", [128, 8], F32)
        NRM = sb("NRM", [128, 8], F32)
        IDN = sb("IDN", [128, 128], BF16)
        ctx.ps["ptA"] = psa("ptA", [128, 128], BF16)
        ctx.ps["ptB"] = psa("ptB", [128, 128], BF16)

        ctx.dma("scalar", IDN[:, :], ident_in, "dC")
        ctx.dma("scalar", G1C[:, :], w1g_in, "dC")
        ctx.dma("scalar", G2C[:, :], w2g_in, "dC")
        gl = p.val("dC")
        ctx.wait("vector", "dC", gl)
        ctx.wait("tensor", "dC", gl)

        def norm_scale(nsq_ap, g_ap, out_ap, width):
            v = ctx.sig("vector", "sv")
            ctx.wait("scalar", "sv", v)
            ctx.op("scalar", "activation", nsq_ap, nsq_ap, AF.Sqrt)
            sv = ctx.sig("scalar", "ss")
            ctx.wait("vector", "ss", sv)
            ctx.op("vector", "tensor_scalar_max", nsq_ap, nsq_ap, EPS)
            ctx.drain("vector")
            ctx.op("vector", "reciprocal", nsq_ap, nsq_ap)
            ctx.drain("vector")
            ctx.op("vector", "tensor_mul", out_ap, nsq_ap, g_ap)
            ctx.drain("vector")

        for hs in range(8):
            dv = ctx.dma("sync",
                         WS[:, :].rearrange("p (s e) -> p s e", s=4),
                         bass.AP(w1_in.tensor, hs * 512 * D,
                                 [[D, 128], [128 * D, 4], [1, D]]), "dQ")
            ctx.wait("vector", "dQ", dv)
            ctx.op("vector", "tensor_mul", SQ2[:, :], WS[:, :], WS[:, :])
            ctx.drain("vector")
            for s4 in range(4):
                ctx.op("vector", "reduce_sum", NRM[:, s4:s4 + 1],
                       SQ2[:, s4 * 1024:(s4 + 1) * 1024], axis=AX.X)
            ctx.drain("vector")
            norm_scale(NRM[:, 0:4], G1C[:, hs * 4:(hs + 1) * 4], NRM[:, 4:8], 4)
            for s4 in range(4):
                ctx.op("vector", "tensor_scalar_mul", WNB[:, s4 * 1024:(s4 + 1) * 1024],
                       WS[:, s4 * 1024:(s4 + 1) * 1024], NRM[:, 4 + s4:5 + s4])
            wnb = ctx.sig("vector", "sv")
            pst = [ctx.ps["ptA"], ctx.ps["ptB"]]
            ctx.wait("tensor", "sv", wnb)
            ctx.wait("scalar", "dOa", p.val("dOa"))
            ctx.wait("scalar", "dOb", p.val("dOb"))
            base_ss = p.val("ss")
            k = 0
            for s4 in range(4):
                for ec in range(8):
                    pt = pst[k % 2][:, :]
                    ctx.wait("tensor", "ss", base_ss + k - 1 if k >= 2 else base_ss)
                    ctx.op("tensor", "transpose", pt,
                           WNB[:, s4 * 1024 + ec * 128: s4 * 1024 + ec * 128 + 128], IDN[:, :])
                    tv = ctx.sig("tensor", "st")
                    ctx.wait("scalar", "st", tv)
                    ctx.op("scalar", "copy",
                           WTB[:, ec * 512 + s4 * 128: ec * 512 + s4 * 128 + 128], pt)
                    ctx.sig("scalar", "ss")
                    k += 1
            sv = p.val("ss")
            ctx.wait("sync", "ss", sv)
            dma_par = "dOa" if hs % 2 == 0 else "dOb"
            ctx.wait("sync", dma_par, p.val(dma_par))
            ctx.dma("sync", bass.AP(w1t_out.tensor, hs * 512,
                                    [[H, 128], [128 * H, 8], [1, 512]]),
                    WTB[:, :].rearrange("p (c h) -> p c h", c=8), dma_par)
            ctx.wait("sync", "st", p.val("st"))
        for os_ in range(8):
            dv = ctx.dma("sync", WS[:, :],
                         bass.AP(w2_in.tensor, os_ * 128 * H, [[H, 128], [1, H]]), "dQ")
            ctx.wait("vector", "dQ", dv)
            ctx.wait("vector", "st", p.val("st"))
            ctx.op("vector", "tensor_mul", SQ2[:, :], WS[:, :], WS[:, :])
            ctx.drain("vector")
            ctx.op("vector", "reduce_sum", NRM[:, 0:1], SQ2[:, :], axis=AX.X)
            ctx.drain("vector")
            norm_scale(NRM[:, 0:1], G2C[:, os_:os_ + 1], NRM[:, 4:5], 1)
            ctx.op("vector", "tensor_scalar_mul", WNB[:, :], WS[:, :], NRM[:, 4:5])
            wnb = ctx.sig("vector", "sv")
            pst = [ctx.ps["ptA"], ctx.ps["ptB"]]
            ctx.wait("tensor", "sv", wnb)
            ctx.wait("scalar", "dOa", p.val("dOa"))
            ctx.wait("scalar", "dOb", p.val("dOb"))
            base_ss = p.val("ss")
            for hc in range(32):
                pt = pst[hc % 2][:, :]
                ctx.wait("tensor", "ss", base_ss + hc - 1 if hc >= 2 else base_ss)
                ctx.op("tensor", "transpose", pt, WNB[:, hc * 128:(hc + 1) * 128], IDN[:, :])
                tv = ctx.sig("tensor", "st")
                ctx.wait("scalar", "st", tv)
                ctx.op("scalar", "copy", WTB[:, hc * 128:(hc + 1) * 128], pt)
                ctx.sig("scalar", "ss")
            sv = p.val("ss")
            ctx.wait("sync", "ss", sv)
            dma_par = "dOa" if os_ % 2 == 0 else "dOb"
            ctx.wait("sync", dma_par, p.val(dma_par))
            ctx.dma("sync", bass.AP(w2t_out.tensor, os_ * 128,
                                    [[D, 128], [128 * D, 32], [1, 128]]),
                    WTB[:, :].rearrange("p (c h) -> p c h", c=32), dma_par)
            ctx.wait("sync", "st", p.val("st"))
        ctx.wait("sync", "dOa", p.val("dOa"))
        ctx.wait("sync", "dOb", p.val("dOb"))
        finish(ctx)
    return nc


def build_moe():
    nc = bass.Bass("TRN2", target_bir_lowering=False, debug=False)
    p = P()
    ctx = Ctx(nc, p)
    H = 4096
    x_in = nc.dram_tensor("x", [NTOK, D], F32, kind="ExternalInput").ap()
    w1t_in = nc.dram_tensor("w1nT", [8 * 8 * 128, H], BF16, kind="ExternalInput").ap()
    w2t_in = nc.dram_tensor("w2nT", [8 * 2 * 8 * 128, 2048], BF16, kind="ExternalInput").ap()
    gv_in = nc.dram_tensor("gatev", [8, D], F32, kind="ExternalInput").ap()
    gvt_in = nc.dram_tensor("gatevT", [D, 8], F32, kind="ExternalInput").ap()
    gg_in = nc.dram_tensor("gateg", [8, 1], F32, kind="ExternalInput").ap()
    gb_in = nc.dram_tensor("gateb", [1, 8], F32, kind="ExternalInput").ap()
    b1c_in = nc.dram_tensor("b1c", [8 * 128, 32], F32, kind="ExternalInput").ap()
    b2_in = nc.dram_tensor("b2r", [8, D], F32, kind="ExternalInput").ap()
    ident_in = nc.dram_tensor("ident", [128, 128], BF16, kind="ExternalInput").ap()
    x_out = nc.dram_tensor("xo", [NTOK, D], F32, kind="ExternalOutput").ap()

    EPS = 1e-12
    with ExitStack() as es:
        sb = lambda name, shape, dt: es.enter_context(nc.sbuf_tensor(name, shape, dt))
        psa = lambda name, shape, dt: es.enter_context(nc.psum_tensor(name, shape, dt))
        alloc_common(ctx, es)
        XT = sb("XT", [128, NTILE * 1024], F32)
        ABF = sb("ABF", [128, 61440], BF16)
        XLN = ABF[:, 0:16384]
        XLF = ABF[:, 16384:32768]
        H1F = ABF[:, 32768:49152]
        W1TS = [ABF[:, 49152:53248], ABF[:, 53248:57344]]
        W2TS = [ABF[:, 57344:59392], ABF[:, 59392:61440]]
        STATS = sb("STATS", [128, 64], F32)
        SQ = sb("SQ", [128, 1024], F32)
        GVN = sb("GVN", [8, 1024], F32)
        GVT = sb("GVT", [128, 64], BF16)
        GROWS = sb("GROWS", [1, 32], F32)
        SC8 = sb("SC8", [128, 8], F32)
        GB8 = sb("GB8", [128, 8], F32)
        MASKT = sb("MASKT", [128, NTILE * 8], F32)
        B1C = sb("B1C", [128, 32], F32)
        B2R = sb("B2R", [1, 1024], BF16)
        VROW = sb("VROW", [1, 1024], F32)
        GSM = sb("GSM", [128, 8], F32)
        IDN = sb("IDN", [128, 128], BF16)
        ctx.ps["ptA"] = psa("ptA", [128, 128], BF16)
        ctx.ps["ptB"] = psa("ptB", [128, 128], BF16)
        PSY = [psa(f"psy{t}", [128, 512], F32) for t in range(6)]

        x_res = x_in.rearrange("(i p) e -> p i e", p=128)
        ctx.dma("sync", XT[:, :].rearrange("p (i e) -> p i e", i=NTILE), x_res, "dS")
        ctx.dma("scalar", IDN[:, :], ident_in, "dC")
        ctx.dma("scalar", GVN[:, :], gv_in, "dC")
        dgvt = ctx.dma("gpsimd", GVT[:, :].rearrange("p (c e) -> p c e", c=8),
                       gvt_in.rearrange("(c p) e -> p c e", p=128), "dG")

        ctx.wait("vector", "dS", p.val("dS"))
        ln_done = emit_ln(ctx, XT, XLN, STATS, SQ)
        fm_done = emit_tm_to_fm(ctx, XLN, XLF, IDN[:, :], ln_done, copy_eng="scalar")

        ctx.wait("vector", "dC", p.val("dC"))
        ctx.op("vector", "tensor_mul", GVN[0:8, :], GVN[0:8, :], GVN[0:8, :])
        ctx.drain("vector")
        ctx.op("vector", "reduce_sum", GSM[0:8, 0:1], GVN[0:8, :], axis=AX.X)
        v = ctx.sig("vector", "sv")
        ctx.wait("scalar", "sv", v)
        ctx.op("scalar", "activation", GSM[0:8, 1:2], GSM[0:8, 0:1], AF.Sqrt)
        sv = ctx.sig("scalar", "ss")
        ctx.wait("vector", "ss", sv)
        ctx.op("vector", "tensor_scalar_max", GSM[0:8, 1:2], GSM[0:8, 1:2], EPS)
        ctx.drain("vector")
        ctx.op("vector", "reciprocal", GSM[0:8, 2:3], GSM[0:8, 1:2])
        gr = ctx.sig("vector", "sv")
        dgg = ctx.dma("scalar", GSM[0:8, 4:5], gg_in, "dC")
        dgb = ctx.dma("scalar", GROWS[0:1, 8:16], gb_in, "dC")
        ctx.wait("vector", "dC", p.val("dC"))
        ctx.wait("vector", "sv", gr)
        ctx.op("vector", "tensor_mul", GSM[0:8, 3:4], GSM[0:8, 2:3], GSM[0:8, 4:5])
        sdone = ctx.sig("vector", "sv")
        ctx.wait("gpsimd", "sv", sdone)
        dsc = ctx.dma("gpsimd", GROWS[0:1, 0:8],
                      bass.AP(GSM[0:8, :].tensor, 3, [[8, 8], [1, 1]]), "dG")
        ctx.wait("tensor", "dG", p.val("dG"))
        ctx.wait("tensor", "dC", p.val("dC"))
        ctx.wait("tensor", "sv", 1)
        ctx.op("tensor", "matmul", PSY[0][:, 0:8], ctx.sb["ones"][:, :], GROWS[0:1, 0:8],
               start=True, stop=True)
        ctx.op("tensor", "matmul", PSY[1][:, 0:8], ctx.sb["ones"][:, :], GROWS[0:1, 8:16],
               start=True, stop=True)
        tb = ctx.sig("tensor", "st")
        ctx.wait("scalar", "st", tb)
        ctx.op("scalar", "copy", SC8[:, :], PSY[0][:, 0:8])
        ctx.op("scalar", "copy", GB8[:, :], PSY[1][:, 0:8])
        rows_done = ctx.sig("scalar", "ss")

        ctx.wait("tensor", "ss", rows_done)
        ctx.wait("tensor", "ss", fm_done)
        for i in range(NTILE):
            ctx.wait("tensor", "sv", p.val("sv"))
            for c in range(8):
                ctx.op("tensor", "matmul", PSY[0][:, 0:8],
                       XLF[:, c * 2048 + i * 128: c * 2048 + i * 128 + 128],
                       GVT[:, c * 8:(c + 1) * 8], start=(c == 0), stop=(c == 7))
            tg = ctx.sig("tensor", "st")
            ctx.wait("vector", "st", tg)
            ctx.op("vector", "tensor_mul", GSM[:, 0:8], PSY[0][:, 0:8], SC8[:, :])
            ctx.drain("vector")
            ctx.op("vector", "tensor_add", GSM[:, 0:8], GSM[:, 0:8], GB8[:, :])
            ctx.drain("vector")
            ctx.op("vector", "reduce_max", STATS[:, 0:1], GSM[:, 0:8], axis=AX.X)
            ctx.drain("vector")
            ctx.op("vector", "tensor_single_scalar", SQ[:, 0:8], GSM[:, 0:8],
                   STATS[:, 0:1], op=ALU.is_ge)
            ctx.drain("vector")
            ctx.op("vector", "scalar_tensor_tensor", SQ[:, 8:16], SQ[:, 0:8],
                   -1.0e30, GSM[:, 0:8], op0=ALU.mult, op1=ALU.add)
            ctx.drain("vector")
            ctx.op("vector", "reduce_max", STATS[:, 1:2], SQ[:, 8:16], axis=AX.X)
            ctx.drain("vector")
            ctx.op("vector", "tensor_single_scalar", MASKT[:, i * 8:(i + 1) * 8],
                   GSM[:, 0:8], STATS[:, 1:2], op=ALU.is_ge)
            ctx.sig("vector", "sv")

        w1_consumed = [None, None]
        w2_consumed = [None, None]
        y2_evac_a = [None]
        ctx.wait("tensor", "sv", p.val("sv"))
        for e in range(8):
            ctx.wait("scalar", "st", p.val("st"))
            ctx.wait("vector", "st", p.val("st"))
            dv = ctx.dma("scalar", B1C[:, :],
                         bass.AP(b1c_in.tensor, e * 128 * 32, [[32, 128], [1, 32]]), "dC")
            ctx.wait("scalar", "dC", dv)
            dv2 = ctx.dma("scalar", VROW[:, :],
                          bass.AP(b2_in.tensor, e * D, [[D, 1], [1, D]]), "dC")
            ctx.wait("vector", "dC", dv2)
            ctx.op("vector", "tensor_copy", B2R[0:1, :], VROW[0:1, :])
            b2done = ctx.sig("vector", "sv")
            w1_par = ["dKa", "dKb"]
            w2_par = ["dVa", "dVb"]
            w1_gpar = ["dGa", "dGb"]
            w2_gpar = ["dGc", "dGd"]
            for q in range(4):
                for hs in range(8):
                    buf = hs % 2
                    if w1_consumed[buf] is not None:
                        ctx.wait("sync", "st", w1_consumed[buf])
                        ctx.wait("gpsimd", "st", w1_consumed[buf])
                    wa1 = bass.AP(w1t_in.tensor, (e * 8 + hs) * 128 * H, [[H, 128], [1, 2048]])
                    wa2 = bass.AP(w1t_in.tensor, (e * 8 + hs) * 128 * H + 2048, [[H, 128], [1, 2048]])
                    ctx.dma("sync", W1TS[buf][:, 0:2048], wa1, w1_par[buf])
                    ctx.dma("gpsimd", W1TS[buf][:, 2048:4096], wa2, w1_gpar[buf])
                    w1_lastd = p.val(w1_par[buf])
                    ctx.wait("tensor", w1_par[buf], w1_lastd)
                    ctx.wait("tensor", w1_gpar[buf], p.val(w1_gpar[buf]))
                    ctx.wait("tensor", "ss", p.val("ss"))
                    for hsub in range(4):
                        bank = PSY[[4, 5, 0, 1][hsub]]
                        if hsub == 2:
                            if y2_evac_a[0] is not None:
                                ctx.wait("tensor", "sv", y2_evac_a[0])
                        for c in range(8):
                            ctx.op("tensor", "matmul", bank[:, 0:512],
                                   W1TS[buf][:, c * 512 + hsub * 128: c * 512 + hsub * 128 + 128],
                                   XLF[:, c * 2048 + q * 512: c * 2048 + (q + 1) * 512],
                                   start=(c == 0), stop=(c == 7))
                    th = ctx.sig("tensor", "st")
                    w1_consumed[buf] = th
                    ctx.wait("scalar", "st", th)
                    for hsub in range(4):
                        ctx.op("scalar", "activation",
                               H1F[:, (hs * 4 + hsub) * 512: (hs * 4 + hsub + 1) * 512],
                               PSY[[4, 5, 0, 1][hsub]][:, 0:512], AF.Silu,
                               bias=B1C[:, hs * 4 + hsub: hs * 4 + hsub + 1], scale=1.0)
                    ctx.sig("scalar", "ss")
                h1done = p.val("ss")
                for j in range(2):
                    for hs2 in range(8):
                        buf = hs2 % 2
                        if w2_consumed[buf] is not None:
                            ctx.wait("scalar", "st", w2_consumed[buf])
                            ctx.wait("gpsimd", "st", w2_consumed[buf])
                        wb1 = bass.AP(w2t_in.tensor,
                                      ((e * 2 + j) * 8 + hs2) * 128 * 2048,
                                      [[2048, 128], [1, 1024]])
                        wb2 = bass.AP(w2t_in.tensor,
                                      ((e * 2 + j) * 8 + hs2) * 128 * 2048 + 1024,
                                      [[2048, 128], [1, 1024]])
                        ctx.dma("scalar", W2TS[buf][:, 0:1024], wb1, w2_par[buf])
                        ctx.dma("gpsimd", W2TS[buf][:, 1024:2048], wb2, w2_gpar[buf])
                        w2_lastd = p.val(w2_par[buf])
                        ctx.wait("tensor", w2_par[buf], w2_lastd)
                        ctx.wait("tensor", w2_gpar[buf], p.val(w2_gpar[buf]))
                        if hs2 == 0:
                            ctx.wait("tensor", "ss", h1done)
                            ctx.wait("tensor", "sv", p.val("sv"))
                        for ti in range(4):
                            for hsub in range(4):
                                ctx.op("tensor", "matmul", PSY[ti][:, :],
                                       H1F[:, (hs2 * 4 + hsub) * 512 + ti * 128: (hs2 * 4 + hsub) * 512 + ti * 128 + 128],
                                       W2TS[buf][:, hsub * 512:(hsub + 1) * 512],
                                       start=(hs2 == 0 and hsub == 0), stop=False)
                        w2_consumed[buf] = ctx.sig("tensor", "st")
                    ctx.wait("tensor", "sv", b2done)
                    for ti in range(4):
                        ctx.op("tensor", "matmul", PSY[ti][:, :],
                               ctx.sb["ones_bf"][0:1, 0:128],
                               B2R[0:1, j * 512:(j + 1) * 512], start=False, stop=True)
                    ty = ctx.sig("tensor", "st")
                    ctx.wait("vector", "st", ty)
                    for ti in range(4):
                        i = q * 4 + ti
                        sl = slice(i * 1024 + j * 512, i * 1024 + (j + 1) * 512)
                        ctx.op("vector", "scalar_tensor_tensor", XT[:, sl],
                               PSY[ti][:, :],
                               MASKT[:, i * 8 + e: i * 8 + e + 1],
                               XT[:, sl], op0=ALU.mult, op1=ALU.add)
                        if ti == 1:
                            y2_evac_a[0] = ctx.sig("vector", "sv")
                    ctx.sig("vector", "sv")

        out_ready = p.val("sv")
        ctx.wait("sync", "sv", out_ready)
        ctx.dma("sync", x_out.rearrange("(i p) e -> p i e", p=128),
                XT[:, :].rearrange("p (i e) -> p i e", i=NTILE), "dO")
        ctx.wait("sync", "dO", p.val("dO"))
        finish(ctx)
    return nc


_MOE_PREP_CACHE = {}


def run_moe_prep(w1_v, w1_g, w2_v, w2_g):
    key = id(w1_v)
    nc = build_moe_prep()
    in_maps = []
    for c in range(NCORES):
        in_maps.append({
            "w1v": np.ascontiguousarray(np.asarray(w1_v[c], np.float32)),
            "w1gc": np.ascontiguousarray(np.asarray(w1_g[c], np.float32).reshape(32, 128).T),
            "w2v": np.ascontiguousarray(np.asarray(w2_v[c], np.float32)),
            "w2gc": np.ascontiguousarray(np.asarray(w2_g[c], np.float32).reshape(8, 128).T),
            "ident": IDENT_BF,
        })
    res = run_bass_kernel_spmd(nc, in_maps, core_ids=list(range(NCORES)))
    w1nT = np.concatenate([res.results[c]["w1nT"] for c in range(NCORES)], axis=0)
    w2nT = np.concatenate([res.results[c]["w2nT"] for c in range(NCORES)], axis=0)
    return w1nT, w2nT


def relayout_moe_weights(w1nT, w2nT):
    w1 = w1nT.reshape(8, 8, 128, 8, 512)
    w1R = np.ascontiguousarray(w1.transpose(0, 3, 2, 1, 4)).reshape(8 * 8 * 128, 8 * 512)
    w2 = w2nT.reshape(8, 8, 4, 128, 2, 512)
    w2R = np.ascontiguousarray(w2.transpose(0, 4, 1, 3, 2, 5)).reshape(8 * 2 * 8 * 128, 4 * 512)
    return w1R, w2R


def run_moe(x_full, w1nT, w2nT, gate_v, gate_g, gate_b, b1, b2):
    w1nT, w2nT = relayout_moe_weights(w1nT, w2nT)
    nc = build_moe()
    b1c = np.ascontiguousarray(np.asarray(b1, np.float32).reshape(8, 32, 128)
                               .transpose(0, 2, 1).reshape(8 * 128, 32))
    in_maps = []
    for c in range(NCORES):
        in_maps.append({
            "x": shard_tokens(x_full, c),
            "w1nT": w1nT, "w2nT": w2nT,
            "gatev": np.asarray(gate_v, np.float32),
            "gatevT": np.ascontiguousarray(np.asarray(gate_v, np.float32).T),
            "gateg": np.asarray(gate_g, np.float32).reshape(8, 1),
            "gateb": np.asarray(gate_b, np.float32).reshape(1, 8),
            "b1c": b1c,
            "b2r": np.asarray(b2, np.float32),
            "ident": IDENT_BF,
        })
    res = run_bass_kernel_spmd(nc, in_maps, core_ids=list(range(NCORES)))
    return unshard_tokens([res.results[c]["xo"] for c in range(NCORES)])



def build_s5a():
    nc = bass.Bass("TRN2", target_bir_lowering=False, debug=False)
    p = P()
    ctx = Ctx(nc, p)
    x_in = nc.dram_tensor("x", [NTOK, D], F32, kind="ExternalInput").ap()
    w_in = nc.dram_tensor("wqkv", [3 * D, D], F32, kind="ExternalInput").ap()
    b_in = nc.dram_tensor("bqkv", [1, 3 * D], F32, kind="ExternalInput").ap()
    ident_in = nc.dram_tensor("ident", [128, 128], BF16, kind="ExternalInput").ap()
    q_out = nc.dram_tensor("q", [NTOK, D], BF16, kind="ExternalOutput").ap()
    k_out = nc.dram_tensor("k", [NTOK, D], BF16, kind="ExternalOutput").ap()
    v_out = nc.dram_tensor("v", [NTOK, D], BF16, kind="ExternalOutput").ap()

    with ExitStack() as es:
        sb = lambda name, shape, dt: es.enter_context(nc.sbuf_tensor(name, shape, dt))
        psa = lambda name, shape, dt: es.enter_context(nc.psum_tensor(name, shape, dt))
        alloc_common(ctx, es)
        XT = sb("XT", [128, NTILE * 1024], F32)
        ABF = sb("ABF", [128, 40960], BF16)
        XLN = ABF[:, 0:16384]
        XLF = ABF[:, 16384:32768]
        WNAT = ABF[:, 0:8192]
        WT = ABF[:, 8192:16384]
        OBUF = ABF[:, 32768:36864]
        IDN = sb("IDN", [128, 128], BF16)
        STATS = sb("STATS", [128, 64], F32)
        SQ = sb("SQ", [128, 1024], F32)
        VROW = sb("VROW", [1, 1024], F32)
        BROWS = sb("BROWS", [1, 3 * 1024], BF16)
        ctx.ps["ptA"] = psa("ptA", [128, 128], BF16)
        ctx.ps["ptB"] = psa("ptB", [128, 128], BF16)
        PSA = [psa(f"psa{t}", [128, 512], F32) for t in range(2)]

        x_res = x_in.rearrange("(i p) e -> p i e", p=128)
        ctx.dma("sync", XT[:, :].rearrange("p (i e) -> p i e", i=NTILE), x_res, "dS")
        ctx.dma("scalar", IDN[:, :], ident_in, "dC")
        for bi in range(3):
            dv = ctx.dma("scalar", VROW[:, :],
                         bass.AP(b_in.tensor, bi * 1024, [[3072, 1], [1, 1024]]), "dC")
            ctx.wait("vector", "dC", dv)
            ctx.op("vector", "tensor_copy", BROWS[0:1, bi * 1024:(bi + 1) * 1024], VROW[0:1, :])
            bv_done = ctx.sig("vector", "sv")
            if bi < 2:
                ctx.wait("scalar", "sv", bv_done)

        ctx.wait("vector", "dS", 16)
        ln_done = emit_ln(ctx, XT, XLN, STATS, SQ)
        fm_done = emit_tm_to_fm(ctx, XLN, XLF, IDN[:, :], ln_done)

        outs = [q_out, k_out, v_out]
        for part in range(3):
            ctx.wait("gpsimd", "st", p.val("st"))
            wa = bass.AP(w_in.tensor, part * 1024 * D, [[D, 128], [128 * D, 8], [1, D]])
            dv = ctx.dma("gpsimd", WNAT.rearrange("p (c e) -> p c e", c=8), wa, "dG")
            pst = [ctx.ps["ptA"], ctx.ps["ptB"]]
            ctx.wait("tensor", "dG", dv)
            ctx.wait("tensor", "dC", p.val("dC"))
            base_ss = p.val("ss")
            k = 0
            for oc in range(8):
                for ec in range(8):
                    pt = pst[k % 2][:, :]
                    ctx.wait("tensor", "ss", base_ss + k - 1 if k >= 2 else base_ss)
                    ctx.op("tensor", "transpose", pt,
                           WNAT[:, oc * 1024 + ec * 128: oc * 1024 + ec * 128 + 128], IDN[:, :])
                    tv = ctx.sig("tensor", "st")
                    ctx.wait("scalar", "st", tv)
                    ctx.op("scalar", "copy",
                           WT[:, ec * 1024 + oc * 128: ec * 1024 + oc * 128 + 128], pt)
                    ctx.sig("scalar", "ss")
                    k += 1
            wt_done = p.val("ss")
            emit_proj_tm(ctx, WT, BROWS[:, part * 1024:(part + 1) * 1024], XLF, PSA,
                         OBUF, outs[part], [("ss", wt_done)])
        ctx.wait("sync", "dOa", p.val("dOa"))
        ctx.wait("sync", "dOb", p.val("dOb"))
        finish(ctx)
    return nc


def build_s5b():
    nc = bass.Bass("TRN2", target_bir_lowering=False, debug=False)
    p = P()
    ctx = Ctx(nc, p)
    x_in = nc.dram_tensor("x", [NTOK, D], F32, kind="ExternalInput").ap()
    q_in = nc.dram_tensor("q", [NTOK, D], BF16, kind="ExternalInput").ap()
    k_in = nc.dram_tensor("k", [NTOK, D], BF16, kind="ExternalInput").ap()
    v_in = nc.dram_tensor("v", [NTOK, D], BF16, kind="ExternalInput").ap()
    wo_in = nc.dram_tensor("wo", [D, D], F32, kind="ExternalInput").ap()
    bo_in = nc.dram_tensor("bo", [1, D], F32, kind="ExternalInput").ap()
    ident_in = nc.dram_tensor("ident", [128, 128], BF16, kind="ExternalInput").ap()
    x_out = nc.dram_tensor("xo", [NTOK, D], F32, kind="ExternalOutput").ap()

    with ExitStack() as es:
        sb = lambda name, shape, dt: es.enter_context(nc.sbuf_tensor(name, shape, dt))
        psa = lambda name, shape, dt: es.enter_context(nc.psum_tensor(name, shape, dt))
        alloc_common(ctx, es)
        ABF = sb("ABF", [128, 60416], BF16)
        QTM = ABF[:, 0:16384]
        KTM = ABF[:, 16384:32768]
        VTM = ABF[:, 32768:49152]
        EXPC = [ABF[:, 49152:53248], ABF[:, 53248:57344]]
        TSTG = ABF[:, 57344:59392]
        BROW = ABF[:, 59392:59904]
        WNAT = KTM[:, 0:8192]
        WOT = KTM[:, 8192:16384]
        F32B = sb("F32B", [128, 16384], F32)
        XT = F32B[:, 0:16384]
        AB2H = nc.virtual_sbuf_tensor("AB2", [128, 32768], BF16)
        nc._tensor(AB2H.name, AB2H.shape, AB2H.dtype, type="SB")
        _m = nc.lookup_mloc(AB2H)
        _m.allocated = True
        _m.addr = nc.lookup_mloc(F32B).addr
        AB2 = AB2H
        CCs = [AB2[:, 0:4096], AB2[:, 4096:8192]]
        Es = [AB2[:, 8192:12288], AB2[:, 12288:16384]]
        ECB = AB2[:, 16384:20480]
        PRD = AB2[:, 20480:24576]
        TMPP = AB2[:, 24576:28672]
        PREPF = F32B[:, 0:8192]
        NZ = sb("NZ", [128, 384], F32)
        IDN = sb("IDN", [128, 128], BF16)
        VROW = sb("VROW", [1, 1024], F32)
        ctx.ps["ptA"] = psa("ptA", [128, 128], BF16)
        ctx.ps["ptB"] = psa("ptB", [128, 128], BF16)
        PSB = [psa(f"psb{j}", [128, 512], F32) for j in range(2)]

        ctx.dma("scalar", IDN[:, :], ident_in, "dC")
        for nm, dst in ((q_in, QTM), (k_in, KTM), (v_in, VTM)):
            ctx.dma("sync", dst.rearrange("p (i e) -> p i e", i=NTILE),
                    nm.rearrange("(i p) e -> p i e", p=128), "dS")
        dv = ctx.dma("scalar", VROW[:, :], bo_in, "dC")
        ctx.wait("vector", "dC", dv)
        ctx.op("vector", "tensor_copy", BROW[0:1, 0:512], VROW[0:1, 0:512])
        ctx.op("vector", "tensor_copy", BROW[0:1, 512:1024]
               if False else ABF[0:1, 59904:60416], VROW[0:1, 512:1024])
        ctx.sig("vector", "sv")
        BROW2 = ABF[:, 59904:60416]

        ctx.wait("vector", "dS", p.val("dS"))
        ctx.wait("gpsimd", "dS", p.val("dS"))

        def s_ap(base_col, t, h, kind, width=64):
            pass

        engs = ["vector", "gpsimd"]
        sigs = ["sv", "sg"]
        for st in range(LTILES):
            ctx.wait("vector", "sg", p.val("sg"))
            for src in (QTM, KTM):
                tiles = [(t * 4 + st) * 1024 for t in range(4)]
                MEAN = PREPF[:, 4096:5120]
                T1 = PREPF[:, 5120:6144]
                T2 = PREPF[:, 6144:7168]
                XC = PREPF[:, 0:4096]
                ctx.op("vector", "tensor_add", T1, src[:, tiles[0]:tiles[0] + 1024],
                       src[:, tiles[1]:tiles[1] + 1024])
                ctx.op("vector", "tensor_add", T2, src[:, tiles[2]:tiles[2] + 1024],
                       src[:, tiles[3]:tiles[3] + 1024])
                ctx.drain("vector")
                ctx.op("vector", "tensor_add", MEAN, T1, T2)
                ctx.drain("vector")
                ctx.op("vector", "tensor_scalar_mul", MEAN, MEAN, 0.25)
                ctx.drain("vector")
                for t in range(4):
                    ctx.op("vector", "tensor_sub", XC[:, t * 1024:(t + 1) * 1024],
                           src[:, tiles[t]:tiles[t] + 1024], MEAN)
                ctx.drain("vector")
                SX = PREPF[:, 4096:5120]
                SQT = PREPF[:, 5120:6144]
                ctx.op("vector", "tensor_mul", SX, XC[:, 0:1024], XC[:, 0:1024])
                ctx.drain("vector")
                for t in range(1, 4):
                    ctx.op("vector", "tensor_mul", SQT, XC[:, t * 1024:(t + 1) * 1024],
                           XC[:, t * 1024:(t + 1) * 1024])
                    ctx.drain("vector")
                    ctx.op("vector", "tensor_add", SX, SX, SQT)
                    ctx.drain("vector")
                vsx = ctx.sig("vector", "sv")
                ctx.wait("scalar", "sv", vsx)
                ctx.op("scalar", "activation", SQT, SX, AF.Sqrt)
                ssq = ctx.sig("scalar", "ss")
                ctx.wait("vector", "ss", ssq)
                ctx.op("vector", "reciprocal", SQT, SQT)
                ctx.drain("vector")
                for t in range(4):
                    ctx.op("vector", "tensor_mul", src[:, tiles[t]:tiles[t] + 1024],
                           XC[:, t * 1024:(t + 1) * 1024], SQT)
                ctx.drain("vector")
            prep_done = ctx.sig("vector", "sv")
            ctx.wait("gpsimd", "sv", prep_done)

            for h in range(16):
                pu = h % 2
                CC = CCs[pu]
                EX = Es[pu]
                Z = NZ[:, 0:64]
                RZ = NZ[:, 320:384]
                NT = [NZ[:, 64 + t * 64: 128 + t * 64] for t in range(4)]
                abf_row = ABF[:, :].ap[0][0]
                ab2_row = AB2[:, :].ap[0][0]

                def xap(t, h=h, st=st, abf_row=abf_row):
                    off = (t * 4 + st) * 1024 + h * 64
                    return bass.AP(ABF[:, :].tensor, off, [[abf_row, 128], [0, 64], [1, 64]])

                def yap(t, h=h, st=st, abf_row=abf_row):
                    off = 16384 + (t * 4 + st) * 1024 + h * 64
                    return bass.AP(ABF[:, :].tensor, off, [[abf_row, 128], [1, 64], [0, 64]])

                def vap(t, h=h, st=st, abf_row=abf_row):
                    off = 32768 + (t * 4 + st) * 1024 + h * 64
                    return bass.AP(ABF[:, :].tensor, off, [[abf_row, 128], [0, 64], [1, 64]])

                cc3 = bass.AP(AB2[:, :].tensor, CC.offset, [[ab2_row, 128], [64, 64], [1, 64]])
                tp3 = bass.AP(AB2[:, :].tensor, TMPP.offset, [[ab2_row, 128], [64, 64], [1, 64]])
                ec3 = bass.AP(AB2[:, :].tensor, ECB.offset, [[ab2_row, 128], [64, 64], [1, 64]])
                pr3 = bass.AP(AB2[:, :].tensor, PRD.offset, [[ab2_row, 128], [64, 64], [1, 64]])

                ctx.wait("gpsimd", "ss", p.val("ss"))
                ctx.op("gpsimd", "tensor_tensor", cc3, xap(0), yap(0), op=ALU.mult)
                ctx.drain("gpsimd")
                ctx.op("gpsimd", "tensor_tensor", tp3, xap(1), yap(1), op=ALU.mult)
                ctx.drain("gpsimd")
                ctx.op("gpsimd", "tensor_add", CC[:, :], CC[:, :], TMPP[:, 0:4096])
                vc = ctx.sig("gpsimd", "sg")
                ctx.op("vector", "tensor_tensor", pr3, xap(2), yap(2), op=ALU.mult)
                ctx.drain("vector")
                ctx.op("vector", "tensor_tensor", ec3, xap(3), yap(3), op=ALU.mult)
                ctx.drain("vector")
                ctx.op("vector", "tensor_add", PRD[:, :], PRD[:, :], ECB[:, :])
                ctx.drain("vector")
                ctx.wait("vector", "sg", vc)
                ctx.op("vector", "tensor_add", CC[:, :], CC[:, :], PRD[:, :])
                vv = ctx.sig("vector", "sv")
                ctx.wait("scalar", "sv", vv)
                ctx.op("scalar", "activation", EX[:, :], CC[:, :], AF.Exp)
                se = ctx.sig("scalar", "ss")
                ctx.wait("vector", "ss", se)
                ctx.op("vector", "tensor_scalar_max", ECB[:, :], EX[:, :], 1.0)
                ctx.drain("vector")
                ctx.op("vector", "reduce_sum", Z, ec3, axis=AX.X)
                for t in range(4):
                    ctx.drain("vector")
                    ctx.op("vector", "tensor_tensor", pr3, ec3, vap(t), op=ALU.mult)
                    ctx.drain("vector")
                    ctx.op("vector", "reduce_sum", NT[t], pr3, axis=AX.X)
                ctx.drain("vector")
                ctx.op("vector", "reciprocal", RZ, Z)
                ctx.drain("vector")
                for t in range(4):
                    off = (t * 4 + st) * 1024 + h * 64
                    ctx.op("vector", "tensor_mul", ABF[:, off:off + 64], NT[t], RZ)
                ctx.drain("vector")
                ctx.sig("vector", "sv")

        ctx.wait("gpsimd", "sv", p.val("sv"))
        ctx.wait("gpsimd", "sg", p.val("sg"))
        wa = bass.AP(wo_in.tensor, 0, [[D, 128], [128 * D, 8], [1, D]])
        dv = ctx.dma("gpsimd", WNAT.rearrange("p (c e) -> p c e", c=8), wa, "dG")
        pst = [ctx.ps["ptA"], ctx.ps["ptB"]]
        ctx.wait("tensor", "dG", dv)
        ctx.wait("tensor", "dC", p.val("dC"))
        ctx.wait("tensor", "sg", p.val("sg"))
        ctx.wait("tensor", "sv", p.val("sv"))
        base_ss = p.val("ss")
        k = 0
        for oc in range(8):
            for ec in range(8):
                pt = pst[k % 2][:, :]
                ctx.wait("tensor", "ss", base_ss + k - 1 if k >= 2 else base_ss)
                ctx.op("tensor", "transpose", pt,
                       WNAT[:, oc * 1024 + ec * 128: oc * 1024 + ec * 128 + 128], IDN[:, :])
                tv = ctx.sig("tensor", "st")
                ctx.wait("scalar", "st", tv)
                ctx.op("scalar", "copy",
                       WOT[:, ec * 1024 + oc * 128: ec * 1024 + oc * 128 + 128], pt)
                ctx.sig("scalar", "ss")
                k += 1
        wt_done = p.val("ss")

        x_res = x_in.rearrange("(i p) e -> p i e", p=128)
        ctx.wait("sync", "sv", p.val("sv"))
        ctx.wait("sync", "sg", p.val("sg"))
        dx = ctx.dma("sync", XT.rearrange("p (i e) -> p i e", i=NTILE), x_res, "dQ")
        ctx.wait("vector", "dQ", dx)

        ctx.wait("tensor", "ss", wt_done)
        for i in range(NTILE):
            ctx.wait("tensor", "sv", p.val("sv"))
            for c in range(8):
                pt = pst[c % 2][:, :]
                ctx.wait("tensor", "ss", p.val("ss") - 1 if c >= 2 else p.val("ss"))
                ctx.op("tensor", "transpose", pt,
                       QTM[:, i * 1024 + c * 128: i * 1024 + c * 128 + 128], IDN[:, :])
                tv = ctx.sig("tensor", "st")
                ctx.wait("scalar", "st", tv)
                ctx.op("scalar", "copy", TSTG[:, c * 128:(c + 1) * 128], pt)
                ctx.sig("scalar", "ss")
            ctx.wait("tensor", "ss", p.val("ss"))
            for j in range(2):
                for c in range(8):
                    ctx.op("tensor", "matmul", PSB[j][:, :],
                           TSTG[:, c * 128:(c + 1) * 128],
                           WOT[:, c * 1024 + j * 512: c * 1024 + (j + 1) * 512],
                           start=(c == 0), stop=False)
                ctx.op("tensor", "matmul", PSB[j][:, :],
                       ctx.sb["ones_bf"][0:1, 0:128],
                       (BROW if j == 0 else BROW2)[0:1, 0:512], start=False, stop=True)
            ty = ctx.sig("tensor", "st")
            ctx.wait("vector", "st", ty)
            for j in range(2):
                sl = slice(i * 1024 + j * 512, i * 1024 + (j + 1) * 512)
                ctx.op("vector", "tensor_add", XT[:, sl], XT[:, sl], PSB[j][:, :])
            ctx.sig("vector", "sv")

        out_ready = p.val("sv")
        ctx.wait("sync", "sv", out_ready)
        ctx.dma("sync", x_out.rearrange("(i p) e -> p i e", p=128),
                XT.rearrange("p (i e) -> p i e", i=NTILE), "dO")
        ctx.wait("sync", "dO", p.val("dO"))
        finish(ctx)
    return nc


def run_s5(x_full, Wqkv, bqkv, Wout, bout):
    nc_a = build_s5a()
    in_maps = []
    for c in range(NCORES):
        in_maps.append({
            "x": shard_tokens(x_full, c),
            "wqkv": np.asarray(Wqkv, np.float32),
            "bqkv": np.asarray(bqkv, np.float32).reshape(1, 3 * D),
            "ident": IDENT_BF,
        })
    res = run_bass_kernel_spmd(nc_a, in_maps, core_ids=list(range(NCORES)))
    nc_b = build_s5b()
    in_maps = []
    for c in range(NCORES):
        in_maps.append({
            "x": shard_tokens(x_full, c),
            "q": res.results[c]["q"], "k": res.results[c]["k"], "v": res.results[c]["v"],
            "wo": np.asarray(Wout, np.float32),
            "bo": np.asarray(bout, np.float32).reshape(1, D),
            "ident": IDENT_BF,
        })
    res2 = run_bass_kernel_spmd(nc_b, in_maps, core_ids=list(range(NCORES)))
    return unshard_tokens([res2.results[c]["xo"] for c in range(NCORES)])



def shard_tokens(x, core):
    xr = x.reshape(B, T // 128, 128, x.shape[-1])
    sl = xr[:, core::8]
    return np.ascontiguousarray(sl.reshape(B * 512, x.shape[-1]))


def unshard_tokens(slices):
    Dd = slices[0].shape[-1]
    out = np.empty((B, T // 128, 128, Dd), dtype=slices[0].dtype)
    for c in range(8):
        out[:, c::8] = slices[c].reshape(B, LTILES, 128, Dd)
    return out.reshape(B, T, Dd)


IDENT_BF = np.eye(128, dtype=BF16NP)


def run_s2(x_full, W, alpha, bias):
    nc = build_s2(0)
    in_maps = []
    for c in range(NCORES):
        in_maps.append({
            "x": shard_tokens(x_full, c),
            "w": W,
            "alpha": np.asarray(alpha, np.float32).reshape(1, 1),
            "bias": np.asarray(bias, np.float32).reshape(1, D),
            "ident": IDENT_BF,
        })
    res = run_bass_kernel_spmd(nc, in_maps, core_ids=list(range(NCORES)))
    return unshard_tokens([res.results[c]["xo"] for c in range(NCORES)])


def kernel(**inputs):
    i = inputs
    x = np.asarray(i["L0_attn_a_x"], np.float32)
    x = run_s1(x, i["L0_attn_a_Wq"], i["L0_attn_a_bq"], i["L0_attn_a_Wk"],
               i["L0_attn_a_bk"], i["L0_attn_a_Wv"], i["L0_attn_a_bv"])
    x = run_s2(x, i["L0_ffn_weight"], i["L0_ffn_alpha"], i["L0_ffn_bias"])
    x = run_s3(x, i["L0_attn_b_q_w"], i["L0_attn_b_q_b"], i["L0_attn_b_k_w"],
               i["L0_attn_b_k_b"], i["L0_attn_b_v_w"], i["L0_attn_b_v_b"],
               i["L0_attn_b_o_w"], i["L0_attn_b_o_b"])
    x = run_s2(x, i["L0_ffn_weight"], i["L0_ffn_alpha"], i["L0_ffn_bias"])
    x = run_s5(x, i["L1_attn_a_Wqkv"], i["L1_attn_a_bqkv"], i["L1_attn_a_Wout"],
               i["L1_attn_a_bout"])
    w1nT, w2nT = run_moe_prep(i["L1_ffn_w1_v"], i["L1_ffn_w1_g"],
                              i["L1_ffn_w2_v"], i["L1_ffn_w2_g"])
    x = run_moe(x, w1nT, w2nT, i["L1_ffn_gate_v"], i["L1_ffn_gate_g"],
                i["L1_ffn_gate_b"], i["L1_ffn_b1"], i["L1_ffn_b2"])
    x = run_s7(x, i["L1_attn_b_wq"], i["L1_attn_b_wk"], i["L1_attn_b_wv"])
    x = run_moe(x, w1nT, w2nT, i["L1_ffn_gate_v"], i["L1_ffn_gate_g"],
                i["L1_ffn_gate_b"], i["L1_ffn_b1"], i["L1_ffn_b2"])
    return np.ascontiguousarray(x.astype(np.float32))


# version 06
# speedup vs baseline: 1.5726x; 1.0094x over previous
import numpy as np
import ml_dtypes
from contextlib import ExitStack
import concourse.bass as bass
import concourse.mybir as mybir
from concourse.bass_utils import run_bass_kernel_spmd

F32 = mybir.dt.float32
BF16 = mybir.dt.bfloat16
AF = mybir.ActivationFunctionType
ALU = mybir.AluOpType
AX = mybir.AxisListType
BF16NP = ml_dtypes.bfloat16

NCORES = 8
B = 4
T = 4096
D = 1024
NTOK = 2048
NTILE = 16
LTILES = 4

ENGINES = ("sync", "scalar", "vector", "tensor", "gpsimd")


class P:

    def __init__(self):
        self.ops = {e: [] for e in ENGINES}
        self.cnt = {}

    def add(self, eng, fn):
        self.ops[eng].append(fn)

    def bump(self, sem_name, n=1):
        v = self.cnt.get(sem_name, 0) + n
        self.cnt[sem_name] = v
        return v

    def val(self, sem_name):
        return self.cnt.get(sem_name, 0)


class Ctx:

    def __init__(self, nc, p):
        self.nc = nc
        self.p = p
        self.sb = {}
        self.ps = {}
        self.sem = {}
        self.dram = {}

    def wait(self, eng, sem_name, val):
        self.p.add(eng, lambda e, s=self.sem[sem_name], v=val: e.wait_ge(s, v))

    def wait_cur(self, eng, sem_name):
        self.wait(eng, sem_name, self.p.val(sem_name))

    def sig(self, eng, sem_name, n=1):
        v = self.p.bump(sem_name, n)
        self.p.add(eng, lambda e, s=self.sem[sem_name], k=n: e.drain().then_inc(s, k))
        return v

    def drain(self, eng):
        self.p.add(eng, lambda e: e.drain())

    def op(self, eng, method, *args, **kw):
        def fn(e, m=method, a=args, k=kw):
            getattr(e, m)(*a, **k)
        self.p.add(eng, fn)

    def dma(self, queue, out_ap, in_ap, sem_name, **kw):
        v = self.p.bump(sem_name, 16)
        def fn(e, o=out_ap, i=in_ap, s=self.sem[sem_name], k=kw):
            e.dma_start(out=o, in_=i, **k).then_inc(s, 16)
        self.p.add(queue, fn)
        return v


def finish(ctx):
    nc, p = ctx.nc, ctx.p
    with nc.Block() as block:
        for eng in ENGINES:
            ops = p.ops[eng]
            if not ops:
                continue
            def runner(e, ops=ops):
                for f in ops:
                    f(e)
            getattr(block, eng)(runner)



def alloc_common(ctx, es):
    nc = ctx.nc
    for name in ("dS", "dG", "dC", "dO", "dOa", "dOb",
                 "dKa", "dKb", "dVa", "dVb", "dQ", "dGa", "dGb", "dGc", "dGd",
                 "sv", "ss", "st", "sg", "sp"):
        ctx.sem[name] = es.enter_context(nc.semaphore(name))
    ctx.sb["ones"] = es.enter_context(nc.sbuf_tensor("ones_c", [1, 128], F32))
    ctx.sb["ones_bf"] = es.enter_context(nc.sbuf_tensor("ones_bf", [1, 1024], BF16))
    ctx.op("vector", "memset", ctx.sb["ones"][:, :], 1.0)
    ctx.op("vector", "memset", ctx.sb["ones_bf"][:, :], 1.0)
    ctx.sig("vector", "sv")


def bcast_rows(ctx, vec_sb_ap, out_sb_ap, psum_ap, n, wait_list=(), dt=F32):
    for (eng, sem, val) in wait_list:
        ctx.wait(eng, sem, val)
    ctx.wait("tensor", "sv", 1)
    ctx.op("tensor", "matmul", psum_ap, ctx.sb["ones"][:, :].nop
           if False else ctx.sb["ones"][:, :], vec_sb_ap, start=True, stop=True)
    tv = ctx.sig("tensor", "st")
    ctx.wait("scalar", "st", tv)
    ctx.op("scalar", "copy", out_sb_ap, psum_ap)
    return ctx.sig("scalar", "ss")


def emit_ln(ctx, xt, xln, stats, sq, eps=1e-5):
    for i in range(NTILE):
        sl = slice(i * 1024, (i + 1) * 1024)
        ctx.op("vector", "reduce_sum", stats[:, i:i + 1], xt[:, sl], axis=AX.X)
        ctx.op("vector", "tensor_mul", sq[:, :], xt[:, sl], xt[:, sl])
        ctx.drain("vector")
        ctx.op("vector", "reduce_sum", stats[:, 16 + i:17 + i], sq[:, :], axis=AX.X)
        ctx.drain("vector")
    ctx.drain("vector")
    ctx.op("vector", "tensor_scalar_mul", stats[:, 32:48], stats[:, 0:16], 1.0 / 1024.0)
    ctx.drain("vector")
    ctx.op("vector", "tensor_scalar_mul", stats[:, 48:64], stats[:, 16:32], 1.0 / 1024.0)
    ctx.drain("vector")
    ctx.op("vector", "tensor_mul", sq[:, 0:16], stats[:, 32:48], stats[:, 32:48])
    ctx.drain("vector")
    ctx.op("vector", "tensor_sub", stats[:, 48:64], stats[:, 48:64], sq[:, 0:16])
    ctx.drain("vector")
    ctx.op("vector", "tensor_scalar_add", stats[:, 48:64], stats[:, 48:64], eps)
    v = ctx.sig("vector", "sv")
    ctx.wait("scalar", "sv", v)
    ctx.op("scalar", "activation", stats[:, 48:64], stats[:, 48:64], AF.Sqrt)
    sv2 = ctx.sig("scalar", "ss")
    ctx.wait("vector", "ss", sv2)
    ctx.op("vector", "reciprocal", stats[:, 48:64], stats[:, 48:64])
    ctx.drain("vector")
    for i in range(NTILE):
        sl = slice(i * 1024, (i + 1) * 1024)
        ctx.op("vector", "tensor_scalar", xln[:, sl], xt[:, sl],
               stats[:, 32 + i:33 + i], stats[:, 48 + i:49 + i],
               op0=ALU.subtract, op1=ALU.mult)
    return ctx.sig("vector", "sv")


def emit_tm_to_fm(ctx, src_tm, dst_fm, ident_bf, wait_sv_val, ncols=8,
                  copy_eng="scalar", tiles=None):
    p = ctx.p
    pst = [ctx.ps["ptA"], ctx.ps["ptB"]]
    if tiles is None:
        tiles = range(NTILE)
    k = 0
    base_st = p.val("st")
    base_cp = p.val("ss" if copy_eng == "scalar" else "sv")
    cp_sem = "ss" if copy_eng == "scalar" else "sv"
    ctx.wait("tensor", "sv", wait_sv_val)
    ctx.wait("tensor", "dC", p.val("dC"))
    for i in tiles:
        for dc in range(ncols):
            pt = pst[k % 2][:, :]
            ctx.wait("tensor", cp_sem, base_cp + k - 1 if k >= 2 else base_cp)
            ctx.op("tensor", "transpose", pt, src_tm[:, i * 1024 + dc * 128: i * 1024 + dc * 128 + 128], ident_bf)
            tv = ctx.sig("tensor", "st")
            ctx.wait(copy_eng, "st", tv)
            dst = dst_fm[:, dc * 2048 + i * 128: dc * 2048 + i * 128 + 128]
            if copy_eng == "scalar":
                ctx.op("scalar", "copy", dst, pt)
            else:
                ctx.op("vector", "tensor_copy", dst, pt)
            ctx.sig(copy_eng, cp_sem)
            k += 1
    return p.val(cp_sem)


def emit_weight_prep(ctx, w_in, wnat, wt, idn, oc_n=8, ec_n=8, pre_sv=None):
    p = ctx.p
    E = ec_n * 128
    w_res = w_in.rearrange("(c p) e -> p c e", p=128)
    dv = ctx.dma("gpsimd", wnat.rearrange("p (c e) -> p c e", c=oc_n), w_res, "dG")
    pst = [ctx.ps["ptA"], ctx.ps["ptB"]]
    ctx.wait("tensor", "dG", dv)
    ctx.wait("tensor", "dC", p.val("dC"))
    if pre_sv is not None:
        ctx.wait("tensor", "sv", pre_sv)
    k = 0
    base_ss = p.val("ss")
    for oc in range(oc_n):
        for ec in range(ec_n):
            pt = pst[k % 2][:, :]
            ctx.wait("tensor", "ss", base_ss + k - 1 if k >= 2 else base_ss)
            ctx.op("tensor", "transpose", pt,
                   wnat[:, oc * E + ec * 128: oc * E + ec * 128 + 128], idn)
            tv = ctx.sig("tensor", "st")
            ctx.wait("scalar", "st", tv)
            ctx.op("scalar", "copy",
                   wt[:, ec * (oc_n * 128) + oc * 128: ec * (oc_n * 128) + oc * 128 + 128], pt)
            ctx.sig("scalar", "ss")
            k += 1
    return p.val("ss")



def build_s2(core_id):
    nc = bass.Bass("TRN2", target_bir_lowering=False, debug=False)
    p = P()
    ctx = Ctx(nc, p)
    x_in = nc.dram_tensor("x", [NTOK, D], F32, kind="ExternalInput").ap()
    w_in = nc.dram_tensor("w", [D, D], F32, kind="ExternalInput").ap()
    alpha_in = nc.dram_tensor("alpha", [1, 1], F32, kind="ExternalInput").ap()
    bias_in = nc.dram_tensor("bias", [1, D], F32, kind="ExternalInput").ap()
    ident_in = nc.dram_tensor("ident", [128, 128], BF16, kind="ExternalInput").ap()
    x_out = nc.dram_tensor("xo", [NTOK, D], F32, kind="ExternalOutput").ap()

    with ExitStack() as es:
        sb = lambda name, shape, dt: es.enter_context(nc.sbuf_tensor(name, shape, dt))
        psa = lambda name, shape, dt: es.enter_context(nc.psum_tensor(name, shape, dt))
        alloc_common(ctx, es)
        XT = sb("XT", [128, NTILE * 1024], F32)
        XLN = sb("XLN", [128, NTILE * 1024], BF16)
        XLF = sb("XLF", [128, 8 * NTOK], BF16)
        WF = sb("WF", [128, 8 * 1024], F32)
        WTN = XLN
        WT = sb("WT", [128, 8 * 1024], BF16)
        IDN = sb("IDN", [128, 128], BF16)
        STATS = sb("STATS", [128, 64], F32)
        SQ = sb("SQ", [128, 1024], F32)
        SMALL = sb("SMALL", [128, 8], F32)
        VROW = sb("VROW", [1, 1024], F32)
        BB = sb("BB", [128, 1024], F32)
        YT = sb("YT", [128, 512], F32)
        ctx.ps["ptA"] = psa("ptA", [128, 128], BF16)
        ctx.ps["ptB"] = psa("ptB", [128, 128], BF16)
        PSJ = [psa("psj0", [128, 512], F32), psa("psj1", [128, 512], F32)]
        PSB = psa("psb", [128, 1024], F32)
        PSD = psa("psd", [128, 8], F32)

        x_res = x_in.rearrange("(i p) e -> p i e", p=128)
        ctx.dma("sync", XT[:, :].rearrange("p (i e) -> p i e", i=NTILE), x_res, "dS")
        w_res = w_in.rearrange("(c p) e -> p c e", p=128)
        ctx.dma("sync", WF[:, :].rearrange("p (c e) -> p c e", c=8), w_res, "dS")
        ctx.dma("scalar", IDN[:, :], ident_in, "dC")
        ctx.dma("scalar", SMALL[0:1, 2:3], alpha_in, "dC")
        ctx.dma("scalar", VROW[:, :], bias_in, "dC")

        ctx.wait("vector", "dS", p.val("dS"))
        ctx.op("vector", "reduce_sum", SMALL[:, 0:1], WF[:, :], axis=AX.X,
               apply_absolute_value=True)
        v_rows = ctx.sig("vector", "sv")
        ctx.wait("tensor", "sv", v_rows)
        ctx.op("vector", "memset", SMALL[:, 1:2], 1.0)
        v_ones = ctx.sig("vector", "sv")
        ctx.wait("tensor", "sv", v_ones)
        ctx.op("tensor", "matmul", PSD[0:1, 0:1], SMALL[:, 0:1], SMALL[:, 1:2],
               start=True, stop=True)
        t_sum = ctx.sig("tensor", "st")
        ctx.wait("scalar", "st", t_sum)
        ctx.op("scalar", "mul", SMALL[0:1, 3:4], PSD[0:1, 0:1], 0.7 / (1024.0 * 1024.0))
        s_delta = ctx.sig("scalar", "ss")
        ctx.wait("tensor", "ss", s_delta)
        ctx.wait("tensor", "sv", 1)
        ctx.op("tensor", "matmul", PSD[:, 1:2], ctx.sb["ones"][:, :], SMALL[0:1, 3:4],
               start=True, stop=True)
        ctx.wait("tensor", "dC", p.val("dC"))
        ctx.op("tensor", "matmul", PSD[:, 2:3], ctx.sb["ones"][:, :], SMALL[0:1, 2:3],
               start=True, stop=True)
        t_bc = ctx.sig("tensor", "st")
        ctx.wait("scalar", "st", t_bc)
        ctx.op("scalar", "copy", SMALL[:, 4:5], PSD[:, 1:2])
        ctx.op("scalar", "mul", SMALL[:, 5:6], PSD[:, 1:2], -1.0)
        ctx.op("scalar", "copy", SMALL[:, 6:7], PSD[:, 2:3])
        s_cols = ctx.sig("scalar", "ss")

        ctx.wait("vector", "ss", s_cols)
        ctx.op("vector", "tensor_scalar", SQ[:, 0:1024], WF[:, 0:1024],
               SMALL[:, 4:5], None, op0=ALU.is_gt)
        p.ops["vector"].pop()
        GT = XLF
        ctx.op("vector", "tensor_scalar", GT[:, 0:8192], WF[:, :], SMALL[:, 4:5],
               None, op0=ALU.is_gt)
        ctx.op("vector", "tensor_scalar", GT[:, 8192:16384], WF[:, :], SMALL[:, 5:6],
               None, op0=ALU.is_lt)
        ctx.drain("vector")
        ctx.op("vector", "tensor_sub", WTN[:, 0:8192], GT[:, 0:8192], GT[:, 8192:16384])
        v_tern = ctx.sig("vector", "sv")

        pst = [ctx.ps["ptA"], ctx.ps["ptB"]]
        ctx.wait("tensor", "sv", v_tern)
        ctx.wait("tensor", "dC", p.val("dC"))
        k = 0
        base_ss = p.val("ss")
        for oc in range(8):
            for ec in range(8):
                pt = pst[k % 2][:, :]
                ctx.wait("tensor", "ss", base_ss + k - 1 if k >= 2 else base_ss)
                ctx.op("tensor", "transpose", pt,
                       WTN[:, oc * 1024 + ec * 128: oc * 1024 + ec * 128 + 128], IDN[:, :])
                tv = ctx.sig("tensor", "st")
                ctx.wait("scalar", "st", tv)
                ctx.op("scalar", "copy", WT[:, ec * 1024 + oc * 128: ec * 1024 + oc * 128 + 128], pt)
                ctx.sig("scalar", "ss")
                k += 1
        wt_done = p.val("ss")

        ctx.wait("tensor", "dC", p.val("dC"))
        ctx.op("tensor", "matmul", PSB[:, 0:512], ctx.sb["ones"][:, :], VROW[0:1, 0:512],
               start=True, stop=True)
        ctx.op("tensor", "matmul", PSB[:, 512:1024], ctx.sb["ones"][:, :], VROW[0:1, 512:1024],
               start=True, stop=True)
        t_bb = ctx.sig("tensor", "st")
        ctx.wait("scalar", "st", t_bb)
        ctx.op("scalar", "copy", BB[:, :], PSB[:, :])
        bb_done = ctx.sig("scalar", "ss")

        ctx.wait("vector", "dS", 32)
        ctx.wait("vector", "ss", wt_done)
        ln_done = emit_ln(ctx, XT, XLN, STATS, SQ)

        fm_done = emit_tm_to_fm(ctx, XLN, XLF, IDN[:, :], ln_done)

        ctx.wait("tensor", "ss", wt_done)
        ctx.wait("tensor", "ss", fm_done)
        ctx.wait("vector", "ss", bb_done)
        base_v = p.val("sv")
        kk = 0
        for i in range(NTILE):
            for j in range(2):
                psj = PSJ[j][:, :]
                if kk >= 2:
                    ctx.wait("tensor", "sv", base_v + 2 * (kk - 1))
                for c in range(8):
                    ctx.op("tensor", "matmul", psj,
                           XLF[:, c * 2048 + i * 128: c * 2048 + i * 128 + 128],
                           WT[:, c * 1024 + j * 512: c * 1024 + (j + 1) * 512],
                           start=(c == 0), stop=(c == 7))
                tv = ctx.sig("tensor", "st")
                ctx.wait("vector", "st", tv)
                sl = slice(i * 1024 + j * 512, i * 1024 + (j + 1) * 512)
                bsl = slice(j * 512, (j + 1) * 512)
                ctx.op("vector", "scalar_tensor_tensor", YT[:, :], psj,
                       SMALL[:, 6:7], BB[:, bsl], op0=ALU.mult, op1=ALU.add)
                ctx.drain("vector")
                ctx.op("vector", "tensor_add", XT[:, sl], XT[:, sl], YT[:, :])
                ctx.sig("vector", "sv", 2)
                kk += 1
        out_ready = p.val("sv")

        ctx.wait("sync", "sv", out_ready)
        ctx.dma("sync", x_out.rearrange("(i p) e -> p i e", p=128),
                XT[:, :].rearrange("p (i e) -> p i e", i=NTILE), "dO")
        ctx.wait("sync", "dO", p.val("dO"))
        finish(ctx)
    return nc



def emit_proj_fm(ctx, wt_view, brow_bf, xlf, psA, obuf, obuf_sl, out_dram, gate_vals):
    p = ctx.p
    ones_bf = ctx.sb["ones_bf"]
    ctx.wait("scalar", "dOa", p.val("dOa"))
    ctx.wait("scalar", "dOb", p.val("dOb"))
    par_sem = ["dOa", "dOb"]
    last_dma = [None, None]
    for m in range(8):
        ctx.wait("tensor", "ss", p.val("ss"))
        for g in gate_vals:
            ctx.wait("tensor", g[0], g[1])
        gate_vals = ()
        for t in range(4):
            for c in range(8):
                ctx.op("tensor", "matmul", psA[t][:, :],
                       wt_view[:, c * 1024 + m * 128: c * 1024 + m * 128 + 128],
                       xlf[:, c * 2048 + t * 512: c * 2048 + (t + 1) * 512],
                       start=(c == 0), stop=False)
            ctx.op("tensor", "matmul", psA[t][:, :],
                   brow_bf[0:1, m * 128: m * 128 + 128],
                   ones_bf[0:1, 0:512], start=False, stop=True)
        tv = ctx.sig("tensor", "st")
        ctx.wait("scalar", "st", tv)
        sl = obuf_sl[m % 2]
        if last_dma[m % 2] is not None:
            ctx.wait("scalar", par_sem[m % 2], last_dma[m % 2])
        for t in range(4):
            ctx.op("scalar", "copy", obuf[:, sl.start + t * 512: sl.start + (t + 1) * 512], psA[t][:, :])
        sv = ctx.sig("scalar", "ss")
        ctx.wait("sync", "ss", sv)
        out_ap = bass.AP(out_dram.tensor, m * 128 * NTOK, [[NTOK, 128], [1, NTOK]])
        last_dma[m % 2] = ctx.dma("sync", out_ap, obuf[:, sl], par_sem[m % 2])


def emit_proj_tm(ctx, wt_view, brow_bf, xlf, psB, obuf, out_dram, gate_vals,
                 nout=1024, out_sbuf=None, resid_xt=None):
    p = ctx.p
    ones_bf = ctx.sb["ones_bf"]
    ctx.wait("scalar", "dOa", p.val("dOa"))
    ctx.wait("scalar", "dOb", p.val("dOb"))
    par_sem = ["dOa", "dOb"]
    nj = nout // 512
    last_dma = [None, None]
    for i in range(NTILE):
        ctx.wait("tensor", "ss", p.val("ss"))
        for g in gate_vals:
            ctx.wait("tensor", g[0], g[1])
        gate_vals = ()
        for j in range(nj):
            for c in range(8):
                ctx.op("tensor", "matmul", psB[j][:, :],
                       xlf[:, c * 2048 + i * 128: c * 2048 + i * 128 + 128],
                       wt_view[:, c * nout + j * 512: c * nout + (j + 1) * 512],
                       start=(c == 0), stop=False)
            ctx.op("tensor", "matmul", psB[j][:, :],
                   ones_bf[0:1, 0:128],
                   brow_bf[0:1, j * 512: (j + 1) * 512], start=False, stop=True)
        tv = ctx.sig("tensor", "st")
        if out_sbuf is not None:
            ctx.wait("scalar", "st", tv)
            for j in range(nj):
                ctx.op("scalar", "copy", out_sbuf[:, i * nout + j * 512: i * nout + (j + 1) * 512],
                       psB[j][:, :])
            ctx.sig("scalar", "ss")
        elif resid_xt is not None:
            ctx.wait("vector", "st", tv)
            for j in range(nj):
                sl = slice(i * nout + j * 512, i * nout + (j + 1) * 512)
                ctx.op("vector", "tensor_add", resid_xt[:, sl], resid_xt[:, sl], psB[j][:, :])
            ctx.sig("vector", "ss")
        else:
            ctx.wait("scalar", "st", tv)
            base = (i % 2) * nout
            if last_dma[i % 2] is not None:
                ctx.wait("scalar", par_sem[i % 2], last_dma[i % 2])
            for j in range(nj):
                ctx.op("scalar", "copy", obuf[:, base + j * 512: base + (j + 1) * 512], psB[j][:, :])
            sv = ctx.sig("scalar", "ss")
            ctx.wait("sync", "ss", sv)
            out_ap = bass.AP(out_dram.tensor, i * 128 * nout, [[nout, 128], [1, nout]])
            last_dma[i % 2] = ctx.dma("sync", out_ap, obuf[:, base: base + nout], par_sem[i % 2])


def build_s1a():
    nc = bass.Bass("TRN2", target_bir_lowering=False, debug=False)
    p = P()
    ctx = Ctx(nc, p)
    x_in = nc.dram_tensor("x", [NTOK, D], F32, kind="ExternalInput").ap()
    wq_in = nc.dram_tensor("wq", [D, D], F32, kind="ExternalInput").ap()
    wk_in = nc.dram_tensor("wk", [D, D], F32, kind="ExternalInput").ap()
    wv_in = nc.dram_tensor("wv", [D, D], F32, kind="ExternalInput").ap()
    bq_in = nc.dram_tensor("bq", [1, D], F32, kind="ExternalInput").ap()
    bk_in = nc.dram_tensor("bk", [1, D], F32, kind="ExternalInput").ap()
    bv_in = nc.dram_tensor("bv", [1, D], F32, kind="ExternalInput").ap()
    ident_in = nc.dram_tensor("ident", [128, 128], BF16, kind="ExternalInput").ap()
    qT_out = nc.dram_tensor("qT", [D, NTOK], BF16, kind="ExternalOutput").ap()
    kT_out = nc.dram_tensor("kT", [D, NTOK], BF16, kind="ExternalOutput").ap()
    v_out = nc.dram_tensor("v", [NTOK, D], BF16, kind="ExternalOutput").ap()

    with ExitStack() as es:
        sb = lambda name, shape, dt: es.enter_context(nc.sbuf_tensor(name, shape, dt))
        psa = lambda name, shape, dt: es.enter_context(nc.psum_tensor(name, shape, dt))
        alloc_common(ctx, es)
        XT = sb("XT", [128, NTILE * 1024], F32)
        ABF = sb("ABF", [128, 55296], BF16)
        XLN = ABF[:, 0:16384]
        XLF = ABF[:, 16384:32768]
        WNAT = ABF[:, 0:8192]
        WTQ = ABF[:, 8192:16384]
        WTK = ABF[:, 32768:40960]
        WTV = ABF[:, 40960:49152]
        OBUF = ABF[:, 49152:53248]
        IDN = sb("IDN", [128, 128], BF16)
        STATS = sb("STATS", [128, 64], F32)
        SQ = sb("SQ", [128, 1024], F32)
        VROW = sb("VROW", [1, 1024], F32)
        BROWS = sb("BROWS", [1, 3 * 1024], BF16)
        ctx.ps["ptA"] = psa("ptA", [128, 128], BF16)
        ctx.ps["ptB"] = psa("ptB", [128, 128], BF16)
        PSA = [psa(f"psa{t}", [128, 512], F32) for t in range(4)]

        x_res = x_in.rearrange("(i p) e -> p i e", p=128)
        ctx.dma("sync", XT[:, :].rearrange("p (i e) -> p i e", i=NTILE), x_res, "dS")
        ctx.dma("scalar", IDN[:, :], ident_in, "dC")
        for bi, b_in in enumerate((bq_in, bk_in, bv_in)):
            dv = ctx.dma("scalar", VROW[:, :], b_in, "dC")
            ctx.wait("vector", "dC", dv)
            ctx.op("vector", "tensor_copy", BROWS[0:1, bi * 1024:(bi + 1) * 1024], VROW[0:1, :])
            bv_done = ctx.sig("vector", "sv")
            if bi < 2:
                ctx.wait("scalar", "sv", bv_done)

        ctx.wait("vector", "dS", 16)
        ln_done = emit_ln(ctx, XT, XLN, STATS, SQ)
        fm_done = emit_tm_to_fm(ctx, XLN, XLF, IDN[:, :], ln_done)

        ctx.wait("gpsimd", "st", p.val("st"))
        wq_done = emit_weight_prep(ctx, wq_in, WNAT, WTQ, IDN[:, :])
        ctx.wait("gpsimd", "st", p.val("st"))
        wk_done = emit_weight_prep(ctx, wk_in, WNAT, WTK, IDN[:, :])
        ctx.wait("gpsimd", "st", p.val("st"))
        wv_done = emit_weight_prep(ctx, wv_in, WNAT, WTV, IDN[:, :])

        emit_proj_fm(ctx, WTQ, BROWS[:, 0:1024], XLF, PSA, OBUF,
                     [slice(0, 2048), slice(2048, 4096)], qT_out,
                     [("ss", wq_done), ("sv", p.val("sv"))])
        emit_proj_fm(ctx, WTK, BROWS[:, 1024:2048], XLF, PSA, OBUF,
                     [slice(0, 2048), slice(2048, 4096)], kT_out,
                     [("ss", wk_done)])
        emit_proj_tm(ctx, WTV, BROWS[:, 2048:3072], XLF, PSA[:2], OBUF, v_out,
                     [("ss", wv_done)])
        ctx.wait("sync", "dO", p.val("dO"))
        finish(ctx)
    return nc


def emit_tm_to_fm_stream(ctx, src_tm, out_dram, obuf, ident_bf, wait_sv_val):
    p = ctx.p
    pst = [ctx.ps["ptA"], ctx.ps["ptB"]]
    par_sem = ["dOa", "dOb"]
    ctx.wait("scalar", "dOa", p.val("dOa"))
    ctx.wait("scalar", "dOb", p.val("dOb"))
    last_dma = [None, None]
    k = 0
    base_cp = p.val("ss")
    ctx.wait("tensor", "sv", wait_sv_val)
    ctx.wait("tensor", "dC", p.val("dC"))
    for dc in range(8):
        par = dc % 2
        if last_dma[par] is not None:
            ctx.wait("scalar", par_sem[par], last_dma[par])
        for i in range(NTILE):
            pt = pst[k % 2][:, :]
            ctx.wait("tensor", "ss", base_cp + k - 1 if k >= 2 else base_cp)
            ctx.op("tensor", "transpose", pt,
                   src_tm[:, i * 1024 + dc * 128: i * 1024 + dc * 128 + 128], ident_bf)
            tv = ctx.sig("tensor", "st")
            ctx.wait("scalar", "st", tv)
            ctx.op("scalar", "copy", obuf[:, par * 2048 + i * 128: par * 2048 + (i + 1) * 128], pt)
            ctx.sig("scalar", "ss")
            k += 1
        sv = p.val("ss")
        ctx.wait("sync", "ss", sv)
        out_ap = bass.AP(out_dram.tensor, dc * 128 * NTOK, [[NTOK, 128], [1, NTOK]])
        last_dma[par] = ctx.dma("sync", out_ap, obuf[:, par * 2048: par * 2048 + 2048], par_sem[par])


def emit_rope(ctx, qtm, costab, sintab, rs):
    for i in range(NTILE):
        l = i % LTILES
        base = i * 1024
        q1 = qtm[:, base:].ap if False else None
        a_q1 = bass.AP(qtm.tensor, qtm.offset + base, [[qtm.ap[0][0], 128], [64, 16], [1, 32]])
        a_q2 = bass.AP(qtm.tensor, qtm.offset + base + 32, [[qtm.ap[0][0], 128], [64, 16], [1, 32]])
        a_cos = bass.AP(costab.tensor, costab.offset + l * 32, [[costab.ap[0][0], 128], [0, 16], [1, 32]])
        a_sin = bass.AP(sintab.tensor, sintab.offset + l * 32, [[sintab.ap[0][0], 128], [0, 16], [1, 32]])
        a_t1 = bass.AP(rs.tensor, rs.offset, [[rs.ap[0][0], 128], [32, 16], [1, 32]])
        a_t2 = bass.AP(rs.tensor, rs.offset + 512, [[rs.ap[0][0], 128], [32, 16], [1, 32]])
        a_t3 = bass.AP(rs.tensor, rs.offset + 1024, [[rs.ap[0][0], 128], [32, 16], [1, 32]])
        a_t4 = bass.AP(rs.tensor, rs.offset + 1536, [[rs.ap[0][0], 128], [32, 16], [1, 32]])
        ctx.op("vector", "tensor_mul", a_t1, a_q1, a_cos)
        ctx.op("vector", "tensor_mul", a_t2, a_q2, a_sin)
        ctx.op("vector", "tensor_mul", a_t3, a_q2, a_cos)
        ctx.op("vector", "tensor_mul", a_t4, a_q1, a_sin)
        ctx.drain("vector")
        ctx.op("vector", "tensor_sub", a_q1, a_t1, a_t2)
        ctx.op("vector", "tensor_add", a_q2, a_t3, a_t4)
        ctx.drain("vector")
    return ctx.sig("vector", "sv")


def build_s3a():
    nc = bass.Bass("TRN2", target_bir_lowering=False, debug=False)
    p = P()
    ctx = Ctx(nc, p)
    x_in = nc.dram_tensor("x", [NTOK, D], F32, kind="ExternalInput").ap()
    wq_in = nc.dram_tensor("wq", [D, D], F32, kind="ExternalInput").ap()
    wk_in = nc.dram_tensor("wk", [D, D], F32, kind="ExternalInput").ap()
    wv_in = nc.dram_tensor("wv", [D, D], F32, kind="ExternalInput").ap()
    bq_in = nc.dram_tensor("bq", [1, D], F32, kind="ExternalInput").ap()
    bk_in = nc.dram_tensor("bk", [1, D], F32, kind="ExternalInput").ap()
    bv_in = nc.dram_tensor("bv", [1, D], F32, kind="ExternalInput").ap()
    cos_in = nc.dram_tensor("costab", [512, 32], F32, kind="ExternalInput").ap()
    sin_in = nc.dram_tensor("sintab", [512, 32], F32, kind="ExternalInput").ap()
    ident_in = nc.dram_tensor("ident", [128, 128], BF16, kind="ExternalInput").ap()
    qT_out = nc.dram_tensor("qT", [D, NTOK], BF16, kind="ExternalOutput").ap()
    kT_out = nc.dram_tensor("kT", [D, NTOK], BF16, kind="ExternalOutput").ap()
    v_out = nc.dram_tensor("v", [NTOK, D], BF16, kind="ExternalOutput").ap()

    with ExitStack() as es:
        sb = lambda name, shape, dt: es.enter_context(nc.sbuf_tensor(name, shape, dt))
        psa = lambda name, shape, dt: es.enter_context(nc.psum_tensor(name, shape, dt))
        alloc_common(ctx, es)
        XT = sb("XT", [128, NTILE * 1024], F32)
        ABF = sb("ABF", [128, 53248], BF16)
        XLN = ABF[:, 0:16384]
        XLF = ABF[:, 16384:32768]
        WNAT = ABF[:, 0:8192]
        WT = ABF[:, 8192:16384]
        QTM = ABF[:, 32768:49152]
        OBUF = ABF[:, 49152:53248]
        IDN = sb("IDN", [128, 128], BF16)
        STATS = sb("STATS", [128, 64], F32)
        SQ = sb("SQ", [128, 1024], F32)
        VROW = sb("VROW", [1, 1024], F32)
        BROWS = sb("BROWS", [1, 3 * 1024], BF16)
        COS = sb("COS", [128, 128], F32)
        SIN = sb("SIN", [128, 128], F32)
        RS = sb("RS", [128, 2048], F32)
        ctx.ps["ptA"] = psa("ptA", [128, 128], BF16)
        ctx.ps["ptB"] = psa("ptB", [128, 128], BF16)
        PSA = [psa(f"psa{t}", [128, 512], F32) for t in range(2)]

        x_res = x_in.rearrange("(i p) e -> p i e", p=128)
        ctx.dma("sync", XT[:, :].rearrange("p (i e) -> p i e", i=NTILE), x_res, "dS")
        ctx.dma("scalar", IDN[:, :], ident_in, "dC")
        ctx.dma("sync", COS[:, :].rearrange("p (l f) -> p l f", l=4),
                cos_in.rearrange("(l p) f -> p l f", p=128), "dS")
        ctx.dma("sync", SIN[:, :].rearrange("p (l f) -> p l f", l=4),
                sin_in.rearrange("(l p) f -> p l f", p=128), "dS")
        for bi, b_in in enumerate((bq_in, bk_in, bv_in)):
            dv = ctx.dma("scalar", VROW[:, :], b_in, "dC")
            ctx.wait("vector", "dC", dv)
            ctx.op("vector", "tensor_copy", BROWS[0:1, bi * 1024:(bi + 1) * 1024], VROW[0:1, :])
            bv_done = ctx.sig("vector", "sv")
            if bi < 2:
                ctx.wait("scalar", "sv", bv_done)

        ctx.wait("vector", "dS", p.val("dS"))
        ln_done = emit_ln(ctx, XT, XLN, STATS, SQ)
        fm_done = emit_tm_to_fm(ctx, XLN, XLF, IDN[:, :], ln_done)

        ctx.wait("gpsimd", "st", p.val("st"))
        wq_done = emit_weight_prep(ctx, wq_in, WNAT, WT, IDN[:, :])
        emit_proj_tm(ctx, WT, BROWS[:, 0:1024], XLF, PSA, OBUF, None,
                     [("ss", wq_done)], out_sbuf=QTM)
        ctx.wait("vector", "ss", p.val("ss"))
        ctx.wait("vector", "dS", 48)
        rq_done = emit_rope(ctx, QTM, COS[:, :], SIN[:, :], RS[:, :])
        emit_tm_to_fm_stream(ctx, QTM, qT_out, OBUF, IDN[:, :], rq_done)

        ctx.wait("gpsimd", "st", p.val("st"))
        wk_done = emit_weight_prep(ctx, wk_in, WNAT, WT, IDN[:, :])
        ctx.wait("tensor", "st", p.val("st"))
        emit_proj_tm(ctx, WT, BROWS[:, 1024:2048], XLF, PSA, OBUF, None,
                     [("ss", wk_done)], out_sbuf=QTM)
        ctx.wait("vector", "ss", p.val("ss"))
        rk_done = emit_rope(ctx, QTM, COS[:, :], SIN[:, :], RS[:, :])
        emit_tm_to_fm_stream(ctx, QTM, kT_out, OBUF, IDN[:, :], rk_done)

        ctx.wait("gpsimd", "st", p.val("st"))
        wv_done = emit_weight_prep(ctx, wv_in, WNAT, WT, IDN[:, :])
        emit_proj_tm(ctx, WT, BROWS[:, 2048:3072], XLF, PSA, OBUF, v_out,
                     [("ss", wv_done)])
        ctx.wait("sync", "dOa", p.val("dOa"))
        ctx.wait("sync", "dOb", p.val("dOb"))
        finish(ctx)
    return nc


def build_s3b():
    nc = bass.Bass("TRN2", target_bir_lowering=False, debug=False)
    p = P()
    ctx = Ctx(nc, p)
    x_in = nc.dram_tensor("x", [NTOK, D], F32, kind="ExternalInput").ap()
    qT_in = nc.dram_tensor("qT", [D, NTOK], BF16, kind="ExternalInput").ap()
    kT_in = nc.dram_tensor("kband", [LTILES * 4 * D, 512], BF16, kind="ExternalInput").ap()
    v_in = nc.dram_tensor("vband", [LTILES * 4 * 512, D], BF16, kind="ExternalInput").ap()
    wo_in = nc.dram_tensor("wo", [D, D], F32, kind="ExternalInput").ap()
    bo_in = nc.dram_tensor("bo", [1, D], F32, kind="ExternalInput").ap()
    masks_in = nc.dram_tensor("masks", [LTILES * 128, 512], F32, kind="ExternalInput").ap()
    ident_in = nc.dram_tensor("ident", [128, 128], BF16, kind="ExternalInput").ap()
    x_out = nc.dram_tensor("xo", [NTOK, D], F32, kind="ExternalOutput").ap()

    with ExitStack() as es:
        sb = lambda name, shape, dt: es.enter_context(nc.sbuf_tensor(name, shape, dt))
        psa = lambda name, shape, dt: es.enter_context(nc.psum_tensor(name, shape, dt))
        alloc_common(ctx, es)
        XT = sb("XT", [128, NTILE * 1024], F32)
        QF = sb("QF", [128, 8 * 2048], BF16)
        ZFM = sb("ZFM", [128, 8 * 2048], BF16)
        KBa = sb("KBa", [128, 4096], BF16)
        VBa = sb("VBa", [128, 4096], BF16)
        MASKS = sb("MASKS", [128, 4 * 512], F32)
        SC = sb("SC", [128, 2 * 512], F32)
        PB = sb("PB", [128, 2 * 512], BF16)
        PBT = sb("PBT", [128, 2 * 512], BF16)
        ZTM = sb("ZTM", [128, 2 * 1024], BF16)
        SMALL = sb("SMALL", [128, 16], F32)
        WNAT = sb("WNAT", [128, 8192], BF16)
        WTO = sb("WTO", [128, 8192], BF16)
        VROW = sb("VROW", [1, 1024], F32)
        BROW = sb("BROW", [1, 1024], BF16)
        IDN = sb("IDN", [128, 128], BF16)
        KBbH = nc.virtual_sbuf_tensor("KBb", [128, 4096], BF16)
        VBbH = nc.virtual_sbuf_tensor("VBb", [128, 4096], BF16)
        for _h, _off in ((KBbH, 0), (VBbH, 8192)):
            nc._tensor(_h.name, _h.shape, _h.dtype, type="SB")
            _m = nc.lookup_mloc(_h)
            _m.allocated = True
            _m.addr = nc.lookup_mloc(WNAT).addr + _off
        KBs = [KBa, KBbH]
        VBs = [VBa, VBbH]
        ctx.ps["ptA"] = psa("ptA", [128, 128], BF16)
        ctx.ps["ptB"] = psa("ptB", [128, 128], BF16)
        PSS = [psa(f"psS{k}", [128, 512], F32) for k in range(2)]
        PSZ = [psa(f"psZh{k}", [128, 64], F32) for k in range(2)]
        PSB = [psa(f"psb{j}", [128, 512], F32) for j in range(2)]

        x_res = x_in.rearrange("(i p) e -> p i e", p=128)
        ctx.dma("sync", XT[:, :].rearrange("p (i e) -> p i e", i=NTILE), x_res, "dS")
        ctx.dma("scalar", IDN[:, :], ident_in, "dC")
        ctx.dma("sync", QF[:, :].rearrange("p (c t) -> p c t", c=8),
                qT_in.rearrange("(c p) t -> p c t", p=128), "dS")
        ctx.dma("sync", MASKS[:, :].rearrange("p (l n) -> p l n", l=4),
                masks_in.rearrange("(l p) n -> p l n", p=128), "dS")
        dv = ctx.dma("scalar", VROW[:, :], bo_in, "dC")
        ctx.wait("vector", "dC", dv)
        ctx.op("vector", "tensor_copy", BROW[0:1, :], VROW[0:1, :])
        ctx.sig("vector", "sv")
        wo_done = emit_weight_prep(ctx, wo_in, WNAT, WTO, IDN[:, :])

        q_loaded = p.val("dS")
        wo_prep_st = p.val("st")
        ctx.wait("tensor", "dS", q_loaded)
        maskadd_done = [None, None]
        exp_done = [None, None]
        tp_done = [None, None]
        zevac_done = [None, None]
        kb_consumed = [None, None]
        vb_consumed = [None, None]
        ztmT_done = [None, None]
        kv_ready = [None, None]
        kv_ready_v = [None, None]
        ksem_par = ["dKa", "dKb"]
        vsem_par = ["dVa", "dVb"]
        pst = [ctx.ps["ptA"], ctx.ps["ptB"]]
        for b in range(B):
            for l in range(LTILES):
                i = b * 4 + l
                blp = i % 2
                if kb_consumed[blp] is not None:
                    ctx.wait("sync", "st", kb_consumed[blp])
                    ctx.wait("scalar", "st", vb_consumed[blp])
                elif blp == 1:
                    ctx.wait("sync", "st", wo_prep_st)
                    ctx.wait("scalar", "st", wo_prep_st)
                for slot in range(4):
                    blk = l * 4 + slot
                    ka = bass.AP(kT_in.tensor, (blk * D) * 512 + b * 128,
                                 [[512, 128], [128 * 512, 8], [1, 128]])
                    ko = bass.AP(KBs[blp][:, :].tensor, slot * 128,
                                 [[4096, 128], [512, 8], [1, 128]])
                    ctx.dma("sync", ko, ka, ksem_par[blp])
                    va = bass.AP(v_in.tensor, (blk * 512 + b * 128) * D,
                                 [[D, 128], [1, D]])
                    ctx.dma("scalar", VBs[blp][:, slot * 1024:(slot + 1) * 1024],
                            va, vsem_par[blp])
                kv_ready[blp] = p.val(ksem_par[blp])
                kv_ready_v[blp] = p.val(vsem_par[blp])
                for h in range(16):
                    hp = h % 2
                    hb = (h % 2) * 64
                    c = h // 2
                    sb8 = hp * 8
                    if h == 0:
                        ctx.wait("tensor", ksem_par[blp], kv_ready[blp])
                    if maskadd_done[hp] is not None:
                        ctx.wait("tensor", "sv", maskadd_done[hp])
                    ctx.op("tensor", "matmul", PSS[hp][:, :],
                           QF[hb:hb + 64, c * 2048 + b * 512 + l * 128: c * 2048 + b * 512 + l * 128 + 128],
                           KBs[blp][hb:hb + 64, c * 512: (c + 1) * 512],
                           start=True, stop=True)
                    tsc = ctx.sig("tensor", "st")
                    if h == 15:
                        kb_consumed[blp] = tsc
                    ctx.wait("vector", "st", tsc)
                    if exp_done[hp] is not None:
                        ctx.wait("vector", "ss", exp_done[hp])
                    ctx.op("vector", "tensor_add", SC[:, hp * 512:(hp + 1) * 512],
                           PSS[hp][:, :], MASKS[:, l * 512:(l + 1) * 512])
                    ctx.drain("vector")
                    ctx.op("vector", "reduce_max", SMALL[:, sb8:sb8 + 1],
                           SC[:, hp * 512:(hp + 1) * 512], axis=AX.X)
                    ctx.drain("vector")
                    ctx.op("vector", "tensor_scalar_mul", SMALL[:, sb8 + 1:sb8 + 2],
                           SMALL[:, sb8:sb8 + 1], -1.0 / 8.0)
                    maskadd_done[hp] = ctx.sig("vector", "sv")
                    ctx.wait("scalar", "sv", maskadd_done[hp])
                    if tp_done[hp] is not None:
                        ctx.wait("scalar", "st", tp_done[hp])
                    ctx.op("scalar", "activation", PB[:, hp * 512:(hp + 1) * 512],
                           SC[:, hp * 512:(hp + 1) * 512], AF.Exp,
                           bias=SMALL[:, sb8 + 1:sb8 + 2], scale=1.0 / 8.0,
                           accum_out=SMALL[:, sb8 + 2:sb8 + 3])
                    exp_done[hp] = ctx.sig("scalar", "ss")
                    ctx.wait("vector", "ss", exp_done[hp])
                    ctx.op("vector", "reciprocal", SMALL[:, sb8 + 3:sb8 + 4],
                           SMALL[:, sb8 + 2:sb8 + 3])
                    rc_done = ctx.sig("vector", "sv")
                    ctx.wait("tensor", "ss", exp_done[hp])
                    for t in range(4):
                        ctx.wait("tensor", "ss", p.val("ss") - 1 if t >= 2 else p.val("ss"))
                        ctx.op("tensor", "transpose", pst[t % 2][:, :],
                               PB[:, hp * 512 + t * 128: hp * 512 + (t + 1) * 128], IDN[:, :])
                        tv = ctx.sig("tensor", "st")
                        ctx.wait("scalar", "st", tv)
                        ctx.op("scalar", "copy",
                               PBT[:, hp * 512 + t * 128: hp * 512 + (t + 1) * 128],
                               pst[t % 2][:, :])
                        ctx.sig("scalar", "ss")
                    tp_done[hp] = p.val("st")
                    ctx.wait("tensor", "ss", p.val("ss"))
                    if h < 2:
                        ctx.wait("tensor", vsem_par[blp], kv_ready_v[blp])
                    if zevac_done[hp] is not None:
                        ctx.wait("tensor", "sv", zevac_done[hp])
                    if h == 0 and ztmT_done[blp] is not None:
                        ctx.wait("tensor", "st", ztmT_done[blp])
                    for t in range(4):
                        ctx.op("tensor", "matmul", PSZ[hp][:, :],
                               PBT[:, hp * 512 + t * 128: hp * 512 + (t + 1) * 128],
                               VBs[blp][:, t * 1024 + h * 64: t * 1024 + h * 64 + 64],
                               start=(t == 0), stop=(t == 3))
                    tz = ctx.sig("tensor", "st")
                    if h == 15:
                        vb_consumed[blp] = tz
                    ctx.wait("vector", "st", tz)
                    ctx.wait("vector", "sv", rc_done)
                    if h == 0 and ztmT_done[blp] is not None:
                        ctx.wait("vector", "st", ztmT_done[blp])
                    ctx.op("vector", "tensor_scalar_mul", ZTM[:, blp * 1024 + h * 64: blp * 1024 + (h + 1) * 64],
                           PSZ[hp][:, :], SMALL[:, sb8 + 3:sb8 + 4])
                    zevac_done[hp] = ctx.sig("vector", "sv")
                zs = p.val("sv")
                ctx.wait("tensor", "sv", zs)
                for dc in range(8):
                    ctx.wait("tensor", "ss", p.val("ss") - 1 if dc >= 2 else p.val("ss"))
                    ctx.op("tensor", "transpose", pst[dc % 2][:, :],
                           ZTM[:, blp * 1024 + dc * 128: blp * 1024 + (dc + 1) * 128], IDN[:, :])
                    tv = ctx.sig("tensor", "st")
                    ctx.wait("scalar", "st", tv)
                    ctx.op("scalar", "copy", ZFM[:, dc * 2048 + i * 128: dc * 2048 + (i + 1) * 128],
                           pst[dc % 2][:, :])
                    ctx.sig("scalar", "ss")
                ztmT_done[blp] = p.val("st")

        emit_proj_tm(ctx, WTO, BROW, ZFM, PSB, None, None,
                     [("ss", p.val("ss"))], resid_xt=XT)
        out_ready = p.val("ss")
        ctx.wait("sync", "ss", out_ready)
        ctx.dma("sync", x_out.rearrange("(i p) e -> p i e", p=128),
                XT[:, :].rearrange("p (i e) -> p i e", i=NTILE), "dO")
        ctx.wait("sync", "dO", p.val("dO"))
        finish(ctx)
    return nc


def host_rope_tables():
    DH = 64
    inv = (1.0 / 10000.0 ** (np.arange(0, DH, 2, dtype=np.float32) / DH)).astype(np.float32)
    pos = np.arange(T, dtype=np.float32)
    fr = (pos[:, None] * inv[None, :]).astype(np.float32)
    return np.cos(fr).astype(np.float32), np.sin(fr).astype(np.float32)


def host_s3_masks(core_id):
    WIN = 256
    out = np.zeros((LTILES, 128, 4, 128), np.float32)
    NEG = -1.0e30
    for l in range(LTILES):
        g = l * 8 + core_id
        kts = [0, g - 2, g - 1, g]
        seen = set()
        for slot, kt in enumerate(kts):
            if kt < 0 or (kt in seen and not (slot == 0)):
                out[l, :, slot, :] = NEG
                continue
            if slot == 0 and (g - 2 <= 0 <= g):
                out[l, :, slot, :] = NEG
                continue
            seen.add(kt)
            ii = g * 128 + np.arange(128)[:, None]
            jj = kt * 128 + np.arange(128)[None, :]
            keep = (jj <= ii) & ((ii - jj < WIN) | (jj == 0))
            out[l, :, slot, :] = np.where(keep, 0.0, NEG)
    return out.reshape(LTILES * 128, 512)


def run_s3(x_full, q_w, q_b, k_w, k_b, v_w, v_b, o_w, o_b):
    nc_a = build_s3a()
    cos_t, sin_t = host_rope_tables()
    in_maps = []
    for c in range(NCORES):
        rows = ((np.arange(LTILES)[:, None] * 8 + c) * 128 + np.arange(128)[None, :]).reshape(-1)
        in_maps.append({
            "x": shard_tokens(x_full, c),
            "wq": np.asarray(q_w, np.float32), "wk": np.asarray(k_w, np.float32),
            "wv": np.asarray(v_w, np.float32),
            "bq": np.asarray(q_b, np.float32).reshape(1, D),
            "bk": np.asarray(k_b, np.float32).reshape(1, D),
            "bv": np.asarray(v_b, np.float32).reshape(1, D),
            "costab": np.ascontiguousarray(cos_t[rows]),
            "sintab": np.ascontiguousarray(sin_t[rows]),
            "ident": IDENT_BF,
        })
    res = run_bass_kernel_spmd(nc_a, in_maps, core_ids=list(range(NCORES)))
    qT = [res.results[c]["qT"] for c in range(NCORES)]
    kT_all = np.concatenate([res.results[c]["kT"] for c in range(NCORES)], axis=0)
    v_all = np.concatenate([res.results[c]["v"] for c in range(NCORES)], axis=0)

    nc_b = build_s3b()
    in_maps = []
    for c in range(NCORES):
        in_maps.append({
            "x": shard_tokens(x_full, c),
            "qT": qT[c],
            "kband": host_gather_s3_kband(kT_all, c),
            "vband": host_gather_s3_vband(v_all, c),
            "wo": np.asarray(o_w, np.float32),
            "bo": np.asarray(o_b, np.float32).reshape(1, D),
            "masks": host_s3_masks(c),
            "ident": IDENT_BF,
        })
    res = run_bass_kernel_spmd(nc_b, in_maps, core_ids=list(range(NCORES)))
    return unshard_tokens([res.results[c]["xo"] for c in range(NCORES)])


def host_gather_s3_kband(kT_all, core):
    out = np.zeros((LTILES, 4, 1024, 4, 128), dtype=kT_all.dtype)
    for l in range(LTILES):
        g = l * 8 + core
        for slot, kt in enumerate([0, g - 2, g - 1, g]):
            ktt = max(kt, 0)
            r, lt = ktt % 8, ktt // 8
            for b in range(B):
                out[l, slot, :, b] = kT_all[r * D:(r + 1) * D, b * 512 + lt * 128: b * 512 + (lt + 1) * 128]
    return np.ascontiguousarray(out.reshape(LTILES * 4 * 1024, 512))


def host_gather_s3_vband(v_all, core):
    out = np.zeros((LTILES, 4, 4, 128, D), dtype=v_all.dtype)
    for l in range(LTILES):
        g = l * 8 + core
        for slot, kt in enumerate([0, g - 2, g - 1, g]):
            ktt = max(kt, 0)
            r, lt = ktt % 8, ktt // 8
            for b in range(B):
                out[l, slot, b] = v_all[r * NTOK + b * 512 + lt * 128: r * NTOK + b * 512 + (lt + 1) * 128]
    return np.ascontiguousarray(out.reshape(LTILES * 4 * 512, D))

def build_s1b():
    nc = bass.Bass("TRN2", target_bir_lowering=False, debug=False)
    p = P()
    ctx = Ctx(nc, p)
    x_in = nc.dram_tensor("x", [NTOK, D], F32, kind="ExternalInput").ap()
    qT_in = nc.dram_tensor("qT", [D, NTOK], BF16, kind="ExternalInput").ap()
    kT_in = nc.dram_tensor("kTall", [NCORES * D, NTOK], BF16, kind="ExternalInput").ap()
    v_in = nc.dram_tensor("vall", [NCORES * NTOK, D], BF16, kind="ExternalInput").ap()
    x_out = nc.dram_tensor("xo", [NTOK, D], F32, kind="ExternalOutput").ap()

    with ExitStack() as es:
        sb = lambda name, shape, dt: es.enter_context(nc.sbuf_tensor(name, shape, dt))
        psa = lambda name, shape, dt: es.enter_context(nc.psum_tensor(name, shape, dt))
        alloc_common(ctx, es)
        XT = sb("XT", [128, NTILE * 1024], F32)
        ZAC = sb("ZAC", [128, 4 * 1024], F32)
        KCH = sb("KCH", [128, 2 * 4096], BF16)
        VCH = sb("VCH", [128, 2 * 4096], BF16)
        QTL = sb("QTL", [128, 4096], BF16)
        PB = sb("PB", [128, 1024], BF16)
        PBT = sb("PBT", [128, 1024], BF16)
        SMALL = sb("SMALL", [128, 24], F32)
        IDN = sb("IDN", [128, 128], BF16)
        ctx.ps["ptA"] = psa("ptA", [128, 128], BF16)
        ctx.ps["ptB"] = psa("ptB", [128, 128], BF16)
        PSS = [psa(f"psS{k}", [128, 512], F32) for k in range(2)]
        PSZ = [psa(f"psZ{k}", [128, 1024], F32) for k in range(2)]

        ident_in = nc.dram_tensor("ident", [128, 128], BF16, kind="ExternalInput").ap()
        ctx.dma("scalar", IDN[:, :], ident_in, "dC")

        x_res = x_in.rearrange("(i p) e -> p i e", p=128)
        ctx.dma("sync", XT[:, :].rearrange("p (i e) -> p i e", i=NTILE), x_res, "dS")

        ctx.wait("tensor", "dC", p.val("dC"))
        ctx.wait("vector", "dS", p.val("dS"))
        kv_consumed = [None, None]
        exp_done = [None, None]
        tp_done = [None, None]
        zadd_done = [None, None]
        for b in range(B):
            qa = bass.AP(qT_in.tensor, b * 512,
                         [[NTOK, 128], [128 * NTOK, 8], [128, 4], [1, 128]])
            qs = QTL[:, :].rearrange("p (c l j) -> p c l j", c=8, l=4)
            ctx.wait("sync", "st", p.val("st"))
            ctx.wait("scalar", "st", p.val("st"))
            dq = ctx.dma("sync", qs, qa, "dQ")
            ctx.op("vector", "memset", SMALL[:, 0:4], -3.0e38)
            ctx.op("vector", "memset", SMALL[:, 4:8], 0.0)
            ctx.op("vector", "memset", ZAC[:, :], 0.0)
            zinit = ctx.sig("vector", "sv")
            ctx.wait("tensor", "dQ", dq)
            kv_dma = [None, None]
            for kc in range(8):
                buf = kc % 2
                if kv_consumed[buf] is not None:
                    ctx.wait("sync", "st", kv_consumed[buf])
                    ctx.wait("scalar", "st", kv_consumed[buf])
                ksem = "dKa" if buf == 0 else "dKb"
                vsem = "dVa" if buf == 0 else "dVb"
                for gs in range(4):
                    g = kc * 4 + gs
                    r, l = g % 8, g // 8
                    ka = bass.AP(kT_in.tensor, (r * D) * NTOK + b * 512 + l * 128,
                                 [[NTOK, 128], [128 * NTOK, 8], [1, 128]])
                    ko = KCH[:, buf * 4096: buf * 4096 + 4096].rearrange(
                        "p (c j) -> p c j", c=8)[:, :, gs * 128:(gs + 1) * 128]
                    ctx.dma("sync", ko, ka, ksem)
                    va = bass.AP(v_in.tensor, (r * NTOK + b * 512 + l * 128) * D,
                                 [[D, 128], [1, D]])
                    ctx.dma("scalar", VCH[:, buf * 4096 + gs * 1024: buf * 4096 + (gs + 1) * 1024],
                            va, vsem)
                kv_dma[buf] = (p.val(ksem), p.val(vsem))
                for l in range(4):
                    lp = l % 2
                    sb8 = 8 + lp * 8
                    if l == 0:
                        ctx.wait("tensor", ksem, kv_dma[buf][0])
                    if exp_done[lp] is not None:
                        ctx.wait("tensor", "ss", exp_done[lp])
                    for c in range(8):
                        ctx.op("tensor", "matmul", PSS[lp][:, :],
                               QTL[:, c * 512 + l * 128: c * 512 + l * 128 + 128],
                               KCH[:, buf * 4096 + c * 512: buf * 4096 + (c + 1) * 512],
                               start=(c == 0), stop=(c == 7))
                    tsc = ctx.sig("tensor", "st")
                    if l == 3:
                        kv_consumed[buf] = None
                    ctx.wait("vector", "st", tsc)
                    if exp_done[lp] is not None:
                        ctx.wait("vector", "ss", exp_done[lp])
                    ctx.op("vector", "reduce_max", SMALL[:, sb8:sb8 + 1], PSS[lp][:, :], axis=AX.X)
                    ctx.drain("vector")
                    ctx.op("vector", "tensor_max", SMALL[:, sb8 + 1:sb8 + 2], SMALL[:, l:l + 1], SMALL[:, sb8:sb8 + 1])
                    ctx.drain("vector")
                    ctx.op("vector", "tensor_sub", SMALL[:, sb8 + 2:sb8 + 3], SMALL[:, l:l + 1], SMALL[:, sb8 + 1:sb8 + 2])
                    ctx.drain("vector")
                    ctx.op("vector", "tensor_scalar_mul", SMALL[:, sb8 + 2:sb8 + 3], SMALL[:, sb8 + 2:sb8 + 3], 1.0 / 32.0)
                    ctx.op("vector", "tensor_scalar_mul", SMALL[:, sb8 + 3:sb8 + 4], SMALL[:, sb8 + 1:sb8 + 2], -1.0 / 32.0)
                    ctx.op("vector", "tensor_copy", SMALL[:, l:l + 1], SMALL[:, sb8 + 1:sb8 + 2])
                    vst = ctx.sig("vector", "sv")
                    ctx.wait("scalar", "sv", vst)
                    if tp_done[lp] is not None:
                        ctx.wait("scalar", "st", tp_done[lp])
                    ctx.op("scalar", "activation", SMALL[:, sb8 + 4:sb8 + 5], SMALL[:, sb8 + 2:sb8 + 3], AF.Exp)
                    ctx.drain("scalar")
                    ctx.op("scalar", "activation", PB[:, lp * 512:(lp + 1) * 512], PSS[lp][:, :], AF.Exp,
                           bias=SMALL[:, sb8 + 3:sb8 + 4], scale=1.0 / 32.0, accum_out=SMALL[:, sb8 + 5:sb8 + 6])
                    sse = ctx.sig("scalar", "ss")
                    exp_done[lp] = sse
                    ctx.wait("vector", "ss", sse)
                    ctx.op("vector", "scalar_tensor_tensor", SMALL[:, 4 + l:5 + l],
                           SMALL[:, 4 + l:5 + l], SMALL[:, sb8 + 4:sb8 + 5], SMALL[:, sb8 + 5:sb8 + 6],
                           op0=ALU.mult, op1=ALU.add)
                    ctx.op("vector", "tensor_scalar_mul", ZAC[:, l * 1024:(l + 1) * 1024],
                           ZAC[:, l * 1024:(l + 1) * 1024], SMALL[:, sb8 + 4:sb8 + 5])
                    vrs = ctx.sig("vector", "sv")
                    pst = [ctx.ps["ptA"], ctx.ps["ptB"]]
                    ctx.wait("tensor", "ss", sse)
                    for t in range(4):
                        ctx.wait("tensor", "ss", p.val("ss") - 1 if t >= 2 else p.val("ss"))
                        ctx.op("tensor", "transpose", pst[t % 2][:, :],
                               PB[:, lp * 512 + t * 128: lp * 512 + (t + 1) * 128], IDN[:, :])
                        tv = ctx.sig("tensor", "st")
                        ctx.wait("scalar", "st", tv)
                        ctx.op("scalar", "copy", PBT[:, lp * 512 + t * 128: lp * 512 + (t + 1) * 128], pst[t % 2][:, :])
                        ctx.sig("scalar", "ss")
                    tp_done[lp] = p.val("st")
                    ctx.wait("tensor", "ss", p.val("ss"))
                    if l == 0:
                        ctx.wait("tensor", vsem, kv_dma[buf][1])
                    if zadd_done[lp] is not None:
                        ctx.wait("tensor", "sv", zadd_done[lp])
                    for j in range(2):
                        for t in range(4):
                            ctx.op("tensor", "matmul", PSZ[lp][:, j * 512:(j + 1) * 512],
                                   PBT[:, lp * 512 + t * 128: lp * 512 + (t + 1) * 128],
                                   VCH[:, buf * 4096 + t * 1024 + j * 512: buf * 4096 + t * 1024 + (j + 1) * 512],
                                   start=(t == 0), stop=(t == 3))
                    tz = ctx.sig("tensor", "st")
                    kv_consumed[buf] = tz
                    ctx.wait("vector", "st", tz)
                    ctx.wait("vector", "sv", vrs)
                    ctx.op("vector", "tensor_add", ZAC[:, l * 1024:(l + 1) * 1024],
                           ZAC[:, l * 1024:(l + 1) * 1024], PSZ[lp][:, :])
                    zadd_done[lp] = ctx.sig("vector", "sv")
            for l in range(4):
                i = b * 4 + l
                ctx.op("vector", "reciprocal", SMALL[:, 14:15], SMALL[:, 4 + l:5 + l])
                ctx.drain("vector")
                ctx.op("vector", "scalar_tensor_tensor", XT[:, i * 1024:(i + 1) * 1024],
                       ZAC[:, l * 1024:(l + 1) * 1024], SMALL[:, 14:15],
                       XT[:, i * 1024:(i + 1) * 1024], op0=ALU.mult, op1=ALU.add)
                ctx.drain("vector")
            ctx.sig("vector", "sv")

        out_ready = p.val("sv")
        ctx.wait("sync", "sv", out_ready)
        ctx.dma("sync", x_out.rearrange("(i p) e -> p i e", p=128),
                XT[:, :].rearrange("p (i e) -> p i e", i=NTILE), "dO")
        ctx.wait("sync", "dO", p.val("dO"))
        finish(ctx)
    return nc


def run_s1(x_full, Wq, bq, Wk, bk, Wv, bv):
    nc_a = build_s1a()
    in_maps = []
    for c in range(NCORES):
        in_maps.append({
            "x": shard_tokens(x_full, c),
            "wq": np.asarray(Wq, np.float32), "wk": np.asarray(Wk, np.float32),
            "wv": np.asarray(Wv, np.float32),
            "bq": np.asarray(bq, np.float32).reshape(1, D),
            "bk": np.asarray(bk, np.float32).reshape(1, D),
            "bv": np.asarray(bv, np.float32).reshape(1, D),
            "ident": IDENT_BF,
        })
    res = run_bass_kernel_spmd(nc_a, in_maps, core_ids=list(range(NCORES)))
    qT = [res.results[c]["qT"] for c in range(NCORES)]
    kT_all = np.concatenate([res.results[c]["kT"] for c in range(NCORES)], axis=0)
    v_all = np.concatenate([res.results[c]["v"] for c in range(NCORES)], axis=0)

    nc_b = build_s1b()
    in_maps = []
    for c in range(NCORES):
        in_maps.append({
            "x": shard_tokens(x_full, c),
            "qT": qT[c], "kTall": kT_all, "vall": v_all,
            "ident": IDENT_BF,
        })
    res = run_bass_kernel_spmd(nc_b, in_maps, core_ids=list(range(NCORES)))
    return unshard_tokens([res.results[c]["xo"] for c in range(NCORES)])



def build_s7b():
    nc = bass.Bass("TRN2", target_bir_lowering=False, debug=False)
    p = P()
    ctx = Ctx(nc, p)
    x_in = nc.dram_tensor("x", [NTOK, D], F32, kind="ExternalInput").ap()
    kT_in = nc.dram_tensor("kT", [D, NTOK], BF16, kind="ExternalInput").ap()
    qT_in = nc.dram_tensor("qTall", [NCORES * D, NTOK], BF16, kind="ExternalInput").ap()
    v_in = nc.dram_tensor("vall", [NCORES * NTOK, D], BF16, kind="ExternalInput").ap()
    mq_in = nc.dram_tensor("maskq", [4 * 128, T], BF16, kind="ExternalInput").ap()
    ident_in = nc.dram_tensor("ident", [128, 128], BF16, kind="ExternalInput").ap()
    x_out = nc.dram_tensor("xo", [NTOK, D], F32, kind="ExternalOutput").ap()

    with ExitStack() as es:
        sb = lambda name, shape, dt: es.enter_context(nc.sbuf_tensor(name, shape, dt))
        psa = lambda name, shape, dt: es.enter_context(nc.psum_tensor(name, shape, dt))
        alloc_common(ctx, es)
        XT = sb("XT", [128, NTILE * 1024], F32)
        ZAC = sb("ZAC", [128, 4 * 1024], F32)
        QCH = sb("QCH", [128, 2 * 4096], BF16)
        VCH = sb("VCH", [128, 2 * 4096], BF16)
        KTL = sb("KTL", [128, 4096], BF16)
        MQ = sb("MQ", [128, 4 * 4096], BF16)
        PB = sb("PB", [128, 1024], BF16)
        PBM = sb("PBM", [128, 1024], BF16)
        PBT = sb("PBT", [128, 1024], BF16)
        SMALL = sb("SMALL", [128, 24], F32)
        IDN = sb("IDN", [128, 128], BF16)
        ctx.ps["ptA"] = psa("ptA", [128, 128], BF16)
        ctx.ps["ptB"] = psa("ptB", [128, 128], BF16)
        PSS = [psa(f"psS{k}", [128, 512], F32) for k in range(2)]
        PSZ = [psa(f"psZ{k}", [128, 1024], F32) for k in range(2)]

        ctx.dma("scalar", IDN[:, :], ident_in, "dC")
        x_res = x_in.rearrange("(i p) e -> p i e", p=128)
        ctx.dma("sync", XT[:, :].rearrange("p (i e) -> p i e", i=NTILE), x_res, "dS")
        ctx.dma("sync", MQ[:, :].rearrange("p (m t) -> p m t", m=4),
                mq_in.rearrange("(m p) t -> p m t", p=128), "dS")

        ctx.wait("tensor", "dC", p.val("dC"))
        ctx.wait("vector", "dS", p.val("dS"))
        kv_consumed = [None, None]
        exp_done = [None, None]
        tp_done = [None, None]
        zadd_done = [None, None]
        for b in range(B):
            ka = bass.AP(kT_in.tensor, b * 512,
                         [[NTOK, 128], [128 * NTOK, 8], [128, 4], [1, 128]])
            ks = KTL[:, :].rearrange("p (c m j) -> p c m j", c=8, m=4)
            ctx.wait("sync", "st", p.val("st"))
            ctx.wait("scalar", "st", p.val("st"))
            dk = ctx.dma("sync", ks, ka, "dQ")
            ctx.op("vector", "memset", SMALL[:, 0:4], -3.0e38)
            ctx.op("vector", "memset", SMALL[:, 4:8], 0.0)
            ctx.op("vector", "memset", ZAC[:, :], 0.0)
            ctx.sig("vector", "sv")
            ctx.wait("tensor", "dQ", dk)
            kv_dma = [None, None]
            for kc in range(8):
                buf = kc % 2
                if kv_consumed[buf] is not None:
                    ctx.wait("sync", "st", kv_consumed[buf])
                    ctx.wait("scalar", "st", kv_consumed[buf])
                ksem = "dKa" if buf == 0 else "dKb"
                vsem = "dVa" if buf == 0 else "dVb"
                for gs in range(4):
                    g = kc * 4 + gs
                    r, l = g % 8, g // 8
                    qa = bass.AP(qT_in.tensor, (r * D) * NTOK + b * 512 + l * 128,
                                 [[NTOK, 128], [128 * NTOK, 8], [1, 128]])
                    qo = QCH[:, buf * 4096: buf * 4096 + 4096].rearrange(
                        "p (c j) -> p c j", c=8)[:, :, gs * 128:(gs + 1) * 128]
                    ctx.dma("sync", qo, qa, ksem)
                    va = bass.AP(v_in.tensor, (r * NTOK + b * 512 + l * 128) * D,
                                 [[D, 128], [1, D]])
                    ctx.dma("scalar", VCH[:, buf * 4096 + gs * 1024: buf * 4096 + (gs + 1) * 1024],
                            va, vsem)
                kv_dma[buf] = (p.val(ksem), p.val(vsem))
                for m in range(4):
                    lp = m % 2
                    sb8 = 8 + lp * 8
                    if m == 0:
                        ctx.wait("tensor", ksem, kv_dma[buf][0])
                    if exp_done[lp] is not None:
                        ctx.wait("tensor", "ss", exp_done[lp])
                    for c in range(8):
                        ctx.op("tensor", "matmul", PSS[lp][:, :],
                               KTL[:, c * 512 + m * 128: c * 512 + m * 128 + 128],
                               QCH[:, buf * 4096 + c * 512: buf * 4096 + (c + 1) * 512],
                               start=(c == 0), stop=(c == 7))
                    tsc = ctx.sig("tensor", "st")
                    ctx.wait("vector", "st", tsc)
                    if exp_done[lp] is not None:
                        ctx.wait("vector", "ss", exp_done[lp])
                    ctx.op("vector", "reduce_max", SMALL[:, sb8:sb8 + 1], PSS[lp][:, :], axis=AX.X)
                    ctx.drain("vector")
                    ctx.op("vector", "tensor_max", SMALL[:, sb8 + 1:sb8 + 2], SMALL[:, m:m + 1], SMALL[:, sb8:sb8 + 1])
                    ctx.drain("vector")
                    ctx.op("vector", "tensor_sub", SMALL[:, sb8 + 2:sb8 + 3], SMALL[:, m:m + 1], SMALL[:, sb8 + 1:sb8 + 2])
                    ctx.drain("vector")
                    ctx.op("vector", "tensor_scalar_mul", SMALL[:, sb8 + 2:sb8 + 3], SMALL[:, sb8 + 2:sb8 + 3], 1.0 / 32.0)
                    ctx.op("vector", "tensor_scalar_mul", SMALL[:, sb8 + 3:sb8 + 4], SMALL[:, sb8 + 1:sb8 + 2], -1.0 / 32.0)
                    ctx.op("vector", "tensor_copy", SMALL[:, m:m + 1], SMALL[:, sb8 + 1:sb8 + 2])
                    vst = ctx.sig("vector", "sv")
                    ctx.wait("scalar", "sv", vst)
                    if tp_done[lp] is not None:
                        ctx.wait("scalar", "st", tp_done[lp])
                    ctx.op("scalar", "activation", SMALL[:, sb8 + 4:sb8 + 5], SMALL[:, sb8 + 2:sb8 + 3], AF.Exp)
                    ctx.drain("scalar")
                    ctx.op("scalar", "activation", PB[:, lp * 512:(lp + 1) * 512], PSS[lp][:, :], AF.Exp,
                           bias=SMALL[:, sb8 + 3:sb8 + 4], scale=1.0 / 32.0, accum_out=SMALL[:, sb8 + 5:sb8 + 6])
                    sse = ctx.sig("scalar", "ss")
                    exp_done[lp] = sse
                    ctx.wait("vector", "ss", sse)
                    ctx.op("vector", "scalar_tensor_tensor", SMALL[:, 4 + m:5 + m],
                           SMALL[:, 4 + m:5 + m], SMALL[:, sb8 + 4:sb8 + 5], SMALL[:, sb8 + 5:sb8 + 6],
                           op0=ALU.mult, op1=ALU.add)
                    ctx.op("vector", "tensor_scalar_mul", ZAC[:, m * 1024:(m + 1) * 1024],
                           ZAC[:, m * 1024:(m + 1) * 1024], SMALL[:, sb8 + 4:sb8 + 5])
                    ctx.drain("vector")
                    ctx.op("vector", "tensor_mul", PBM[:, lp * 512:(lp + 1) * 512], PB[:, lp * 512:(lp + 1) * 512],
                           MQ[:, m * 4096 + kc * 512: m * 4096 + (kc + 1) * 512])
                    vrs = ctx.sig("vector", "sv")
                    pst = [ctx.ps["ptA"], ctx.ps["ptB"]]
                    ctx.wait("tensor", "sv", vrs)
                    for t in range(4):
                        ctx.wait("tensor", "ss", p.val("ss") - 1 if t >= 2 else p.val("ss"))
                        ctx.op("tensor", "transpose", pst[t % 2][:, :],
                               PBM[:, lp * 512 + t * 128: lp * 512 + (t + 1) * 128], IDN[:, :])
                        tv = ctx.sig("tensor", "st")
                        ctx.wait("scalar", "st", tv)
                        ctx.op("scalar", "copy", PBT[:, lp * 512 + t * 128: lp * 512 + (t + 1) * 128], pst[t % 2][:, :])
                        ctx.sig("scalar", "ss")
                    tp_done[lp] = p.val("st")
                    ctx.wait("tensor", "ss", p.val("ss"))
                    if m == 0:
                        ctx.wait("tensor", vsem, kv_dma[buf][1])
                    if zadd_done[lp] is not None:
                        ctx.wait("tensor", "sv", zadd_done[lp])
                    for j in range(2):
                        for t in range(4):
                            ctx.op("tensor", "matmul", PSZ[lp][:, j * 512:(j + 1) * 512],
                                   PBT[:, lp * 512 + t * 128: lp * 512 + (t + 1) * 128],
                                   VCH[:, buf * 4096 + t * 1024 + j * 512: buf * 4096 + t * 1024 + (j + 1) * 512],
                                   start=(t == 0), stop=(t == 3))
                    tz = ctx.sig("tensor", "st")
                    kv_consumed[buf] = tz
                    ctx.wait("vector", "st", tz)
                    ctx.op("vector", "tensor_add", ZAC[:, m * 1024:(m + 1) * 1024],
                           ZAC[:, m * 1024:(m + 1) * 1024], PSZ[lp][:, :])
                    zadd_done[lp] = ctx.sig("vector", "sv")
            for m in range(4):
                i = b * 4 + m
                ctx.op("vector", "reciprocal", SMALL[:, 14:15], SMALL[:, 4 + m:5 + m])
                ctx.drain("vector")
                ctx.op("vector", "scalar_tensor_tensor", XT[:, i * 1024:(i + 1) * 1024],
                       ZAC[:, m * 1024:(m + 1) * 1024], SMALL[:, 14:15],
                       XT[:, i * 1024:(i + 1) * 1024], op0=ALU.mult, op1=ALU.add)
                ctx.drain("vector")
            ctx.sig("vector", "sv")

        out_ready = p.val("sv")
        ctx.wait("sync", "sv", out_ready)
        ctx.dma("sync", x_out.rearrange("(i p) e -> p i e", p=128),
                XT[:, :].rearrange("p (i e) -> p i e", i=NTILE), "dO")
        ctx.wait("sync", "dO", p.val("dO"))
        finish(ctx)
    return nc


def host_s7_maskq(core):
    out = np.zeros((LTILES, 128, T), np.float32)
    for m in range(LTILES):
        gk = m * 8 + core
        kk = gk * 128 + np.arange(128)[:, None]
        qq = np.arange(T)[None, :]
        out[m] = (qq <= kk).astype(np.float32)
    return out.reshape(LTILES * 128, T).astype(BF16NP)


def run_s7(x_full, wq, wk, wv):
    nc_a = build_s1a()
    zb = np.zeros((1, D), np.float32)
    in_maps = []
    for c in range(NCORES):
        in_maps.append({
            "x": shard_tokens(x_full, c),
            "wq": np.asarray(wq, np.float32), "wk": np.asarray(wk, np.float32),
            "wv": np.asarray(wv, np.float32),
            "bq": zb, "bk": zb, "bv": zb,
            "ident": IDENT_BF,
        })
    res = run_bass_kernel_spmd(nc_a, in_maps, core_ids=list(range(NCORES)))
    qT_all = np.concatenate([res.results[c]["qT"] for c in range(NCORES)], axis=0)
    kT = [res.results[c]["kT"] for c in range(NCORES)]
    v_all = np.concatenate([res.results[c]["v"] for c in range(NCORES)], axis=0)

    nc_b = build_s7b()
    in_maps = []
    for c in range(NCORES):
        in_maps.append({
            "x": shard_tokens(x_full, c),
            "kT": kT[c], "qTall": qT_all, "vall": v_all,
            "maskq": host_s7_maskq(c),
            "ident": IDENT_BF,
        })
    res = run_bass_kernel_spmd(nc_b, in_maps, core_ids=list(range(NCORES)))
    return unshard_tokens([res.results[c]["xo"] for c in range(NCORES)])



def build_moe_prep():
    nc = bass.Bass("TRN2", target_bir_lowering=False, debug=False)
    p = P()
    ctx = Ctx(nc, p)
    H = 4096
    w1_in = nc.dram_tensor("w1v", [H, D], F32, kind="ExternalInput").ap()
    w1g_in = nc.dram_tensor("w1gc", [128, 32], F32, kind="ExternalInput").ap()
    w2_in = nc.dram_tensor("w2v", [D, H], F32, kind="ExternalInput").ap()
    w2g_in = nc.dram_tensor("w2gc", [128, 8], F32, kind="ExternalInput").ap()
    ident_in = nc.dram_tensor("ident", [128, 128], BF16, kind="ExternalInput").ap()
    w1t_out = nc.dram_tensor("w1nT", [D, H], BF16, kind="ExternalOutput").ap()
    w2t_out = nc.dram_tensor("w2nT", [H, D], BF16, kind="ExternalOutput").ap()

    EPS = 1e-12
    with ExitStack() as es:
        sb = lambda name, shape, dt: es.enter_context(nc.sbuf_tensor(name, shape, dt))
        psa = lambda name, shape, dt: es.enter_context(nc.psum_tensor(name, shape, dt))
        alloc_common(ctx, es)
        WS = sb("WS", [128, 4096], F32)
        SQ2 = sb("SQ2", [128, 4096], F32)
        WNB = sb("WNB", [128, 4096], BF16)
        WTB = sb("WTB", [128, 4096], BF16)
        G1C = sb("G1C", [128, 32], F32)
        G2C = sb("G2C", [128, 8], F32)
        NRM = sb("NRM", [128, 8], F32)
        IDN = sb("IDN", [128, 128], BF16)
        ctx.ps["ptA"] = psa("ptA", [128, 128], BF16)
        ctx.ps["ptB"] = psa("ptB", [128, 128], BF16)

        ctx.dma("scalar", IDN[:, :], ident_in, "dC")
        ctx.dma("scalar", G1C[:, :], w1g_in, "dC")
        ctx.dma("scalar", G2C[:, :], w2g_in, "dC")
        gl = p.val("dC")
        ctx.wait("vector", "dC", gl)
        ctx.wait("tensor", "dC", gl)

        def norm_scale(nsq_ap, g_ap, out_ap, width):
            v = ctx.sig("vector", "sv")
            ctx.wait("scalar", "sv", v)
            ctx.op("scalar", "activation", nsq_ap, nsq_ap, AF.Sqrt)
            sv = ctx.sig("scalar", "ss")
            ctx.wait("vector", "ss", sv)
            ctx.op("vector", "tensor_scalar_max", nsq_ap, nsq_ap, EPS)
            ctx.drain("vector")
            ctx.op("vector", "reciprocal", nsq_ap, nsq_ap)
            ctx.drain("vector")
            ctx.op("vector", "tensor_mul", out_ap, nsq_ap, g_ap)
            ctx.drain("vector")

        for hs in range(8):
            dv = ctx.dma("sync",
                         WS[:, :].rearrange("p (s e) -> p s e", s=4),
                         bass.AP(w1_in.tensor, hs * 512 * D,
                                 [[D, 128], [128 * D, 4], [1, D]]), "dQ")
            ctx.wait("vector", "dQ", dv)
            ctx.op("vector", "tensor_mul", SQ2[:, :], WS[:, :], WS[:, :])
            ctx.drain("vector")
            for s4 in range(4):
                ctx.op("vector", "reduce_sum", NRM[:, s4:s4 + 1],
                       SQ2[:, s4 * 1024:(s4 + 1) * 1024], axis=AX.X)
            ctx.drain("vector")
            norm_scale(NRM[:, 0:4], G1C[:, hs * 4:(hs + 1) * 4], NRM[:, 4:8], 4)
            for s4 in range(4):
                ctx.op("vector", "tensor_scalar_mul", WNB[:, s4 * 1024:(s4 + 1) * 1024],
                       WS[:, s4 * 1024:(s4 + 1) * 1024], NRM[:, 4 + s4:5 + s4])
            wnb = ctx.sig("vector", "sv")
            pst = [ctx.ps["ptA"], ctx.ps["ptB"]]
            ctx.wait("tensor", "sv", wnb)
            ctx.wait("scalar", "dOa", p.val("dOa"))
            ctx.wait("scalar", "dOb", p.val("dOb"))
            base_ss = p.val("ss")
            k = 0
            for s4 in range(4):
                for ec in range(8):
                    pt = pst[k % 2][:, :]
                    ctx.wait("tensor", "ss", base_ss + k - 1 if k >= 2 else base_ss)
                    ctx.op("tensor", "transpose", pt,
                           WNB[:, s4 * 1024 + ec * 128: s4 * 1024 + ec * 128 + 128], IDN[:, :])
                    tv = ctx.sig("tensor", "st")
                    ctx.wait("scalar", "st", tv)
                    ctx.op("scalar", "copy",
                           WTB[:, ec * 512 + s4 * 128: ec * 512 + s4 * 128 + 128], pt)
                    ctx.sig("scalar", "ss")
                    k += 1
            sv = p.val("ss")
            ctx.wait("sync", "ss", sv)
            dma_par = "dOa" if hs % 2 == 0 else "dOb"
            ctx.wait("sync", dma_par, p.val(dma_par))
            ctx.dma("sync", bass.AP(w1t_out.tensor, hs * 512,
                                    [[H, 128], [128 * H, 8], [1, 512]]),
                    WTB[:, :].rearrange("p (c h) -> p c h", c=8), dma_par)
            ctx.wait("sync", "st", p.val("st"))
        for os_ in range(8):
            dv = ctx.dma("sync", WS[:, :],
                         bass.AP(w2_in.tensor, os_ * 128 * H, [[H, 128], [1, H]]), "dQ")
            ctx.wait("vector", "dQ", dv)
            ctx.wait("vector", "st", p.val("st"))
            ctx.op("vector", "tensor_mul", SQ2[:, :], WS[:, :], WS[:, :])
            ctx.drain("vector")
            ctx.op("vector", "reduce_sum", NRM[:, 0:1], SQ2[:, :], axis=AX.X)
            ctx.drain("vector")
            norm_scale(NRM[:, 0:1], G2C[:, os_:os_ + 1], NRM[:, 4:5], 1)
            ctx.op("vector", "tensor_scalar_mul", WNB[:, :], WS[:, :], NRM[:, 4:5])
            wnb = ctx.sig("vector", "sv")
            pst = [ctx.ps["ptA"], ctx.ps["ptB"]]
            ctx.wait("tensor", "sv", wnb)
            ctx.wait("scalar", "dOa", p.val("dOa"))
            ctx.wait("scalar", "dOb", p.val("dOb"))
            base_ss = p.val("ss")
            for hc in range(32):
                pt = pst[hc % 2][:, :]
                ctx.wait("tensor", "ss", base_ss + hc - 1 if hc >= 2 else base_ss)
                ctx.op("tensor", "transpose", pt, WNB[:, hc * 128:(hc + 1) * 128], IDN[:, :])
                tv = ctx.sig("tensor", "st")
                ctx.wait("scalar", "st", tv)
                ctx.op("scalar", "copy", WTB[:, hc * 128:(hc + 1) * 128], pt)
                ctx.sig("scalar", "ss")
            sv = p.val("ss")
            ctx.wait("sync", "ss", sv)
            dma_par = "dOa" if os_ % 2 == 0 else "dOb"
            ctx.wait("sync", dma_par, p.val(dma_par))
            ctx.dma("sync", bass.AP(w2t_out.tensor, os_ * 128,
                                    [[D, 128], [128 * D, 32], [1, 128]]),
                    WTB[:, :].rearrange("p (c h) -> p c h", c=32), dma_par)
            ctx.wait("sync", "st", p.val("st"))
        ctx.wait("sync", "dOa", p.val("dOa"))
        ctx.wait("sync", "dOb", p.val("dOb"))
        finish(ctx)
    return nc


def build_moe():
    nc = bass.Bass("TRN2", target_bir_lowering=False, debug=False)
    p = P()
    ctx = Ctx(nc, p)
    H = 4096
    x_in = nc.dram_tensor("x", [NTOK, D], F32, kind="ExternalInput").ap()
    w1t_in = nc.dram_tensor("w1nT", [8 * 8 * 128, H], BF16, kind="ExternalInput").ap()
    w2t_in = nc.dram_tensor("w2nT", [8 * 2 * 8 * 128, 2048], BF16, kind="ExternalInput").ap()
    gv_in = nc.dram_tensor("gatev", [8, D], F32, kind="ExternalInput").ap()
    gvt_in = nc.dram_tensor("gatevT", [D, 8], F32, kind="ExternalInput").ap()
    gg_in = nc.dram_tensor("gateg", [8, 1], F32, kind="ExternalInput").ap()
    gb_in = nc.dram_tensor("gateb", [1, 8], F32, kind="ExternalInput").ap()
    b1c_in = nc.dram_tensor("b1c", [8 * 128, 32], F32, kind="ExternalInput").ap()
    b2_in = nc.dram_tensor("b2r", [8, D], F32, kind="ExternalInput").ap()
    ident_in = nc.dram_tensor("ident", [128, 128], BF16, kind="ExternalInput").ap()
    x_out = nc.dram_tensor("xo", [NTOK, D], F32, kind="ExternalOutput").ap()

    EPS = 1e-12
    with ExitStack() as es:
        sb = lambda name, shape, dt: es.enter_context(nc.sbuf_tensor(name, shape, dt))
        psa = lambda name, shape, dt: es.enter_context(nc.psum_tensor(name, shape, dt))
        alloc_common(ctx, es)
        XT = sb("XT", [128, NTILE * 1024], F32)
        ABF = sb("ABF", [128, 61440], BF16)
        XLN = ABF[:, 0:16384]
        XLF = ABF[:, 16384:32768]
        H1F = ABF[:, 32768:49152]
        W1TS = [ABF[:, 49152:53248], ABF[:, 53248:57344]]
        W2TS = [ABF[:, 57344:59392], ABF[:, 59392:61440]]
        STATS = sb("STATS", [128, 64], F32)
        SQ = sb("SQ", [128, 1024], F32)
        GVN = sb("GVN", [8, 1024], F32)
        GVT = sb("GVT", [128, 64], BF16)
        GROWS = sb("GROWS", [1, 32], F32)
        SC8 = sb("SC8", [128, 8], F32)
        GB8 = sb("GB8", [128, 8], F32)
        MASKT = sb("MASKT", [128, NTILE * 8], F32)
        B1C = sb("B1C", [128, 32], F32)
        B2R = sb("B2R", [1, 1024], BF16)
        VROW = sb("VROW", [1, 1024], F32)
        GSM = sb("GSM", [128, 8], F32)
        IDN = sb("IDN", [128, 128], BF16)
        ctx.ps["ptA"] = psa("ptA", [128, 128], BF16)
        ctx.ps["ptB"] = psa("ptB", [128, 128], BF16)
        PSY = [psa(f"psy{t}", [128, 512], F32) for t in range(6)]

        x_res = x_in.rearrange("(i p) e -> p i e", p=128)
        ctx.dma("sync", XT[:, :].rearrange("p (i e) -> p i e", i=NTILE), x_res, "dS")
        ctx.dma("scalar", IDN[:, :], ident_in, "dC")
        ctx.dma("scalar", GVN[:, :], gv_in, "dC")
        dgvt = ctx.dma("gpsimd", GVT[:, :].rearrange("p (c e) -> p c e", c=8),
                       gvt_in.rearrange("(c p) e -> p c e", p=128), "dG")

        ctx.wait("vector", "dS", p.val("dS"))
        ln_done = emit_ln(ctx, XT, XLN, STATS, SQ)
        fm_done = emit_tm_to_fm(ctx, XLN, XLF, IDN[:, :], ln_done, copy_eng="scalar")

        ctx.wait("vector", "dC", p.val("dC"))
        ctx.op("vector", "tensor_mul", GVN[0:8, :], GVN[0:8, :], GVN[0:8, :])
        ctx.drain("vector")
        ctx.op("vector", "reduce_sum", GSM[0:8, 0:1], GVN[0:8, :], axis=AX.X)
        v = ctx.sig("vector", "sv")
        ctx.wait("scalar", "sv", v)
        ctx.op("scalar", "activation", GSM[0:8, 1:2], GSM[0:8, 0:1], AF.Sqrt)
        sv = ctx.sig("scalar", "ss")
        ctx.wait("vector", "ss", sv)
        ctx.op("vector", "tensor_scalar_max", GSM[0:8, 1:2], GSM[0:8, 1:2], EPS)
        ctx.drain("vector")
        ctx.op("vector", "reciprocal", GSM[0:8, 2:3], GSM[0:8, 1:2])
        gr = ctx.sig("vector", "sv")
        dgg = ctx.dma("scalar", GSM[0:8, 4:5], gg_in, "dC")
        dgb = ctx.dma("scalar", GROWS[0:1, 8:16], gb_in, "dC")
        ctx.wait("vector", "dC", p.val("dC"))
        ctx.wait("vector", "sv", gr)
        ctx.op("vector", "tensor_mul", GSM[0:8, 3:4], GSM[0:8, 2:3], GSM[0:8, 4:5])
        sdone = ctx.sig("vector", "sv")
        ctx.wait("gpsimd", "sv", sdone)
        dsc = ctx.dma("gpsimd", GROWS[0:1, 0:8],
                      bass.AP(GSM[0:8, :].tensor, 3, [[8, 8], [1, 1]]), "dG")
        ctx.wait("tensor", "dG", p.val("dG"))
        ctx.wait("tensor", "dC", p.val("dC"))
        ctx.wait("tensor", "sv", 1)
        ctx.op("tensor", "matmul", PSY[0][:, 0:8], ctx.sb["ones"][:, :], GROWS[0:1, 0:8],
               start=True, stop=True)
        ctx.op("tensor", "matmul", PSY[1][:, 0:8], ctx.sb["ones"][:, :], GROWS[0:1, 8:16],
               start=True, stop=True)
        tb = ctx.sig("tensor", "st")
        ctx.wait("scalar", "st", tb)
        ctx.op("scalar", "copy", SC8[:, :], PSY[0][:, 0:8])
        ctx.op("scalar", "copy", GB8[:, :], PSY[1][:, 0:8])
        rows_done = ctx.sig("scalar", "ss")

        ctx.wait("tensor", "ss", rows_done)
        ctx.wait("tensor", "ss", fm_done)
        for i in range(NTILE):
            ctx.wait("tensor", "sv", p.val("sv"))
            for c in range(8):
                ctx.op("tensor", "matmul", PSY[0][:, 0:8],
                       XLF[:, c * 2048 + i * 128: c * 2048 + i * 128 + 128],
                       GVT[:, c * 8:(c + 1) * 8], start=(c == 0), stop=(c == 7))
            tg = ctx.sig("tensor", "st")
            ctx.wait("vector", "st", tg)
            ctx.op("vector", "tensor_mul", GSM[:, 0:8], PSY[0][:, 0:8], SC8[:, :])
            ctx.drain("vector")
            ctx.op("vector", "tensor_add", GSM[:, 0:8], GSM[:, 0:8], GB8[:, :])
            ctx.drain("vector")
            ctx.op("vector", "reduce_max", STATS[:, 0:1], GSM[:, 0:8], axis=AX.X)
            ctx.drain("vector")
            ctx.op("vector", "tensor_single_scalar", SQ[:, 0:8], GSM[:, 0:8],
                   STATS[:, 0:1], op=ALU.is_ge)
            ctx.drain("vector")
            ctx.op("vector", "scalar_tensor_tensor", SQ[:, 8:16], SQ[:, 0:8],
                   -1.0e30, GSM[:, 0:8], op0=ALU.mult, op1=ALU.add)
            ctx.drain("vector")
            ctx.op("vector", "reduce_max", STATS[:, 1:2], SQ[:, 8:16], axis=AX.X)
            ctx.drain("vector")
            ctx.op("vector", "tensor_single_scalar", MASKT[:, i * 8:(i + 1) * 8],
                   GSM[:, 0:8], STATS[:, 1:2], op=ALU.is_ge)
            ctx.sig("vector", "sv")

        w1_consumed = [None, None]
        w2_consumed = [None, None]
        y2_evac_a = [None]
        y2_evac_b = [None]
        ctx.wait("tensor", "sv", p.val("sv"))
        for e in range(8):
            ctx.wait("scalar", "st", p.val("st"))
            ctx.wait("vector", "st", p.val("st"))
            dv = ctx.dma("scalar", B1C[:, :],
                         bass.AP(b1c_in.tensor, e * 128 * 32, [[32, 128], [1, 32]]), "dC")
            ctx.wait("scalar", "dC", dv)
            dv2 = ctx.dma("scalar", VROW[:, :],
                          bass.AP(b2_in.tensor, e * D, [[D, 1], [1, D]]), "dC")
            ctx.wait("vector", "dC", dv2)
            ctx.op("vector", "tensor_copy", B2R[0:1, :], VROW[0:1, :])
            b2done = ctx.sig("vector", "sv")
            w1_par = ["dKa", "dKb"]
            w2_par = ["dVa", "dVb"]
            w1_gpar = ["dGa", "dGb"]
            w2_gpar = ["dGc", "dGd"]
            for q in range(4):
                for hs in range(8):
                    buf = hs % 2
                    if w1_consumed[buf] is not None:
                        ctx.wait("sync", "st", w1_consumed[buf])
                        ctx.wait("gpsimd", "st", w1_consumed[buf])
                    wa1 = bass.AP(w1t_in.tensor, (e * 8 + hs) * 128 * H, [[H, 128], [1, 2048]])
                    wa2 = bass.AP(w1t_in.tensor, (e * 8 + hs) * 128 * H + 2048, [[H, 128], [1, 2048]])
                    ctx.dma("sync", W1TS[buf][:, 0:2048], wa1, w1_par[buf])
                    ctx.dma("gpsimd", W1TS[buf][:, 2048:4096], wa2, w1_gpar[buf])
                    w1_lastd = p.val(w1_par[buf])
                    ctx.wait("tensor", w1_par[buf], w1_lastd)
                    ctx.wait("tensor", w1_gpar[buf], p.val(w1_gpar[buf]))
                    ctx.wait("tensor", "ss", p.val("ss"))
                    for hsub in range(4):
                        bank = PSY[[4, 5, 0, 1][hsub]]
                        if hsub == 2:
                            if y2_evac_a[0] is not None:
                                ctx.wait("tensor", "sv", y2_evac_a[0] + 1)
                        for c in range(8):
                            ctx.op("tensor", "matmul", bank[:, 0:512],
                                   W1TS[buf][:, c * 512 + hsub * 128: c * 512 + hsub * 128 + 128],
                                   XLF[:, c * 2048 + q * 512: c * 2048 + (q + 1) * 512],
                                   start=(c == 0), stop=(c == 7))
                    th = ctx.sig("tensor", "st")
                    w1_consumed[buf] = th
                    ctx.wait("scalar", "st", th)
                    for hsub in range(4):
                        ctx.op("scalar", "activation",
                               H1F[:, (hs * 4 + hsub) * 512: (hs * 4 + hsub + 1) * 512],
                               PSY[[4, 5, 0, 1][hsub]][:, 0:512], AF.Silu,
                               bias=B1C[:, hs * 4 + hsub: hs * 4 + hsub + 1], scale=1.0)
                    ctx.sig("scalar", "ss")
                h1done = p.val("ss")
                for j in range(2):
                    for hs2 in range(8):
                        buf = hs2 % 2
                        if w2_consumed[buf] is not None:
                            ctx.wait("scalar", "st", w2_consumed[buf])
                            ctx.wait("gpsimd", "st", w2_consumed[buf])
                        wb1 = bass.AP(w2t_in.tensor,
                                      ((e * 2 + j) * 8 + hs2) * 128 * 2048,
                                      [[2048, 128], [1, 1024]])
                        wb2 = bass.AP(w2t_in.tensor,
                                      ((e * 2 + j) * 8 + hs2) * 128 * 2048 + 1024,
                                      [[2048, 128], [1, 1024]])
                        ctx.dma("scalar", W2TS[buf][:, 0:1024], wb1, w2_par[buf])
                        ctx.dma("gpsimd", W2TS[buf][:, 1024:2048], wb2, w2_gpar[buf])
                        w2_lastd = p.val(w2_par[buf])
                        ctx.wait("tensor", w2_par[buf], w2_lastd)
                        ctx.wait("tensor", w2_gpar[buf], p.val(w2_gpar[buf]))
                        if hs2 == 0:
                            ctx.wait("tensor", "ss", h1done)
                        for ti in range(4):
                            if hs2 == 0:
                                if ti < 2 and y2_evac_a[0] is not None:
                                    ctx.wait("tensor", "sv", y2_evac_a[0] + (1 if ti == 1 else 0))
                                elif ti >= 2 and y2_evac_b[0] is not None:
                                    ctx.wait("tensor", "sv", y2_evac_b[0])
                            for hsub in range(4):
                                ctx.op("tensor", "matmul", PSY[ti][:, :],
                                       H1F[:, (hs2 * 4 + hsub) * 512 + ti * 128: (hs2 * 4 + hsub) * 512 + ti * 128 + 128],
                                       W2TS[buf][:, hsub * 512:(hsub + 1) * 512],
                                       start=(hs2 == 0 and hsub == 0), stop=False)
                        w2_consumed[buf] = ctx.sig("tensor", "st")
                    ctx.wait("tensor", "sv", b2done)
                    for ti in range(4):
                        ctx.op("tensor", "matmul", PSY[ti][:, :],
                               ctx.sb["ones_bf"][0:1, 0:128],
                               B2R[0:1, j * 512:(j + 1) * 512], start=False, stop=True)
                    ty = ctx.sig("tensor", "st")
                    ctx.wait("vector", "st", ty)
                    for ti in range(4):
                        i = q * 4 + ti
                        sl = slice(i * 1024 + j * 512, i * 1024 + (j + 1) * 512)
                        ctx.op("vector", "scalar_tensor_tensor", XT[:, sl],
                               PSY[ti][:, :],
                               MASKT[:, i * 8 + e: i * 8 + e + 1],
                               XT[:, sl], op0=ALU.mult, op1=ALU.add)
                        if ti == 0:
                            y2_evac_a[0] = ctx.sig("vector", "sv")
                        elif ti == 1:
                            ctx.sig("vector", "sv")
                    y2_evac_b[0] = ctx.sig("vector", "sv")

        out_ready = p.val("sv")
        ctx.wait("sync", "sv", out_ready)
        ctx.dma("sync", x_out.rearrange("(i p) e -> p i e", p=128),
                XT[:, :].rearrange("p (i e) -> p i e", i=NTILE), "dO")
        ctx.wait("sync", "dO", p.val("dO"))
        finish(ctx)
    return nc


_MOE_PREP_CACHE = {}


def run_moe_prep(w1_v, w1_g, w2_v, w2_g):
    key = id(w1_v)
    nc = build_moe_prep()
    in_maps = []
    for c in range(NCORES):
        in_maps.append({
            "w1v": np.ascontiguousarray(np.asarray(w1_v[c], np.float32)),
            "w1gc": np.ascontiguousarray(np.asarray(w1_g[c], np.float32).reshape(32, 128).T),
            "w2v": np.ascontiguousarray(np.asarray(w2_v[c], np.float32)),
            "w2gc": np.ascontiguousarray(np.asarray(w2_g[c], np.float32).reshape(8, 128).T),
            "ident": IDENT_BF,
        })
    res = run_bass_kernel_spmd(nc, in_maps, core_ids=list(range(NCORES)))
    w1nT = np.concatenate([res.results[c]["w1nT"] for c in range(NCORES)], axis=0)
    w2nT = np.concatenate([res.results[c]["w2nT"] for c in range(NCORES)], axis=0)
    return w1nT, w2nT


def relayout_moe_weights(w1nT, w2nT):
    w1 = w1nT.reshape(8, 8, 128, 8, 512)
    w1R = np.ascontiguousarray(w1.transpose(0, 3, 2, 1, 4)).reshape(8 * 8 * 128, 8 * 512)
    w2 = w2nT.reshape(8, 8, 4, 128, 2, 512)
    w2R = np.ascontiguousarray(w2.transpose(0, 4, 1, 3, 2, 5)).reshape(8 * 2 * 8 * 128, 4 * 512)
    return w1R, w2R


def run_moe(x_full, w1nT, w2nT, gate_v, gate_g, gate_b, b1, b2):
    w1nT, w2nT = relayout_moe_weights(w1nT, w2nT)
    nc = build_moe()
    b1c = np.ascontiguousarray(np.asarray(b1, np.float32).reshape(8, 32, 128)
                               .transpose(0, 2, 1).reshape(8 * 128, 32))
    in_maps = []
    for c in range(NCORES):
        in_maps.append({
            "x": shard_tokens(x_full, c),
            "w1nT": w1nT, "w2nT": w2nT,
            "gatev": np.asarray(gate_v, np.float32),
            "gatevT": np.ascontiguousarray(np.asarray(gate_v, np.float32).T),
            "gateg": np.asarray(gate_g, np.float32).reshape(8, 1),
            "gateb": np.asarray(gate_b, np.float32).reshape(1, 8),
            "b1c": b1c,
            "b2r": np.asarray(b2, np.float32),
            "ident": IDENT_BF,
        })
    res = run_bass_kernel_spmd(nc, in_maps, core_ids=list(range(NCORES)))
    return unshard_tokens([res.results[c]["xo"] for c in range(NCORES)])



def build_s5a():
    nc = bass.Bass("TRN2", target_bir_lowering=False, debug=False)
    p = P()
    ctx = Ctx(nc, p)
    x_in = nc.dram_tensor("x", [NTOK, D], F32, kind="ExternalInput").ap()
    w_in = nc.dram_tensor("wqkv", [3 * D, D], F32, kind="ExternalInput").ap()
    b_in = nc.dram_tensor("bqkv", [1, 3 * D], F32, kind="ExternalInput").ap()
    ident_in = nc.dram_tensor("ident", [128, 128], BF16, kind="ExternalInput").ap()
    q_out = nc.dram_tensor("q", [NTOK, D], BF16, kind="ExternalOutput").ap()
    k_out = nc.dram_tensor("k", [NTOK, D], BF16, kind="ExternalOutput").ap()
    v_out = nc.dram_tensor("v", [NTOK, D], BF16, kind="ExternalOutput").ap()

    with ExitStack() as es:
        sb = lambda name, shape, dt: es.enter_context(nc.sbuf_tensor(name, shape, dt))
        psa = lambda name, shape, dt: es.enter_context(nc.psum_tensor(name, shape, dt))
        alloc_common(ctx, es)
        XT = sb("XT", [128, NTILE * 1024], F32)
        ABF = sb("ABF", [128, 40960], BF16)
        XLN = ABF[:, 0:16384]
        XLF = ABF[:, 16384:32768]
        WNAT = ABF[:, 0:8192]
        WT = ABF[:, 8192:16384]
        OBUF = ABF[:, 32768:36864]
        IDN = sb("IDN", [128, 128], BF16)
        STATS = sb("STATS", [128, 64], F32)
        SQ = sb("SQ", [128, 1024], F32)
        VROW = sb("VROW", [1, 1024], F32)
        BROWS = sb("BROWS", [1, 3 * 1024], BF16)
        ctx.ps["ptA"] = psa("ptA", [128, 128], BF16)
        ctx.ps["ptB"] = psa("ptB", [128, 128], BF16)
        PSA = [psa(f"psa{t}", [128, 512], F32) for t in range(2)]

        x_res = x_in.rearrange("(i p) e -> p i e", p=128)
        ctx.dma("sync", XT[:, :].rearrange("p (i e) -> p i e", i=NTILE), x_res, "dS")
        ctx.dma("scalar", IDN[:, :], ident_in, "dC")
        for bi in range(3):
            dv = ctx.dma("scalar", VROW[:, :],
                         bass.AP(b_in.tensor, bi * 1024, [[3072, 1], [1, 1024]]), "dC")
            ctx.wait("vector", "dC", dv)
            ctx.op("vector", "tensor_copy", BROWS[0:1, bi * 1024:(bi + 1) * 1024], VROW[0:1, :])
            bv_done = ctx.sig("vector", "sv")
            if bi < 2:
                ctx.wait("scalar", "sv", bv_done)

        ctx.wait("vector", "dS", 16)
        ln_done = emit_ln(ctx, XT, XLN, STATS, SQ)
        fm_done = emit_tm_to_fm(ctx, XLN, XLF, IDN[:, :], ln_done)

        outs = [q_out, k_out, v_out]
        for part in range(3):
            ctx.wait("gpsimd", "st", p.val("st"))
            wa = bass.AP(w_in.tensor, part * 1024 * D, [[D, 128], [128 * D, 8], [1, D]])
            dv = ctx.dma("gpsimd", WNAT.rearrange("p (c e) -> p c e", c=8), wa, "dG")
            pst = [ctx.ps["ptA"], ctx.ps["ptB"]]
            ctx.wait("tensor", "dG", dv)
            ctx.wait("tensor", "dC", p.val("dC"))
            base_ss = p.val("ss")
            k = 0
            for oc in range(8):
                for ec in range(8):
                    pt = pst[k % 2][:, :]
                    ctx.wait("tensor", "ss", base_ss + k - 1 if k >= 2 else base_ss)
                    ctx.op("tensor", "transpose", pt,
                           WNAT[:, oc * 1024 + ec * 128: oc * 1024 + ec * 128 + 128], IDN[:, :])
                    tv = ctx.sig("tensor", "st")
                    ctx.wait("scalar", "st", tv)
                    ctx.op("scalar", "copy",
                           WT[:, ec * 1024 + oc * 128: ec * 1024 + oc * 128 + 128], pt)
                    ctx.sig("scalar", "ss")
                    k += 1
            wt_done = p.val("ss")
            emit_proj_tm(ctx, WT, BROWS[:, part * 1024:(part + 1) * 1024], XLF, PSA,
                         OBUF, outs[part], [("ss", wt_done)])
        ctx.wait("sync", "dOa", p.val("dOa"))
        ctx.wait("sync", "dOb", p.val("dOb"))
        finish(ctx)
    return nc


def build_s5b():
    nc = bass.Bass("TRN2", target_bir_lowering=False, debug=False)
    p = P()
    ctx = Ctx(nc, p)
    x_in = nc.dram_tensor("x", [NTOK, D], F32, kind="ExternalInput").ap()
    q_in = nc.dram_tensor("q", [NTOK, D], BF16, kind="ExternalInput").ap()
    k_in = nc.dram_tensor("k", [NTOK, D], BF16, kind="ExternalInput").ap()
    v_in = nc.dram_tensor("v", [NTOK, D], BF16, kind="ExternalInput").ap()
    wo_in = nc.dram_tensor("wo", [D, D], F32, kind="ExternalInput").ap()
    bo_in = nc.dram_tensor("bo", [1, D], F32, kind="ExternalInput").ap()
    ident_in = nc.dram_tensor("ident", [128, 128], BF16, kind="ExternalInput").ap()
    x_out = nc.dram_tensor("xo", [NTOK, D], F32, kind="ExternalOutput").ap()

    with ExitStack() as es:
        sb = lambda name, shape, dt: es.enter_context(nc.sbuf_tensor(name, shape, dt))
        psa = lambda name, shape, dt: es.enter_context(nc.psum_tensor(name, shape, dt))
        alloc_common(ctx, es)
        ABF = sb("ABF", [128, 60416], BF16)
        QTM = ABF[:, 0:16384]
        KTM = ABF[:, 16384:32768]
        VTM = ABF[:, 32768:49152]
        EXPC = [ABF[:, 49152:53248], ABF[:, 53248:57344]]
        TSTG = ABF[:, 57344:59392]
        BROW = ABF[:, 59392:59904]
        WNAT = KTM[:, 0:8192]
        WOT = KTM[:, 8192:16384]
        F32B = sb("F32B", [128, 16384], F32)
        XT = F32B[:, 0:16384]
        AB2H = nc.virtual_sbuf_tensor("AB2", [128, 32768], BF16)
        nc._tensor(AB2H.name, AB2H.shape, AB2H.dtype, type="SB")
        _m = nc.lookup_mloc(AB2H)
        _m.allocated = True
        _m.addr = nc.lookup_mloc(F32B).addr
        AB2 = AB2H
        CCs = [AB2[:, 0:4096], AB2[:, 4096:8192]]
        Es = [AB2[:, 8192:12288], AB2[:, 12288:16384]]
        ECB = AB2[:, 16384:20480]
        PRD = AB2[:, 20480:24576]
        TMPP = AB2[:, 24576:28672]
        PREPF = F32B[:, 0:8192]
        NZ = sb("NZ", [128, 384], F32)
        IDN = sb("IDN", [128, 128], BF16)
        VROW = sb("VROW", [1, 1024], F32)
        ctx.ps["ptA"] = psa("ptA", [128, 128], BF16)
        ctx.ps["ptB"] = psa("ptB", [128, 128], BF16)
        PSB = [psa(f"psb{j}", [128, 512], F32) for j in range(2)]

        ctx.dma("scalar", IDN[:, :], ident_in, "dC")
        for nm, dst in ((q_in, QTM), (k_in, KTM), (v_in, VTM)):
            ctx.dma("sync", dst.rearrange("p (i e) -> p i e", i=NTILE),
                    nm.rearrange("(i p) e -> p i e", p=128), "dS")
        dv = ctx.dma("scalar", VROW[:, :], bo_in, "dC")
        ctx.wait("vector", "dC", dv)
        ctx.op("vector", "tensor_copy", BROW[0:1, 0:512], VROW[0:1, 0:512])
        ctx.op("vector", "tensor_copy", BROW[0:1, 512:1024]
               if False else ABF[0:1, 59904:60416], VROW[0:1, 512:1024])
        ctx.sig("vector", "sv")
        BROW2 = ABF[:, 59904:60416]

        ctx.wait("vector", "dS", p.val("dS"))
        ctx.wait("gpsimd", "dS", p.val("dS"))

        def s_ap(base_col, t, h, kind, width=64):
            pass

        engs = ["vector", "gpsimd"]
        sigs = ["sv", "sg"]
        for st in range(LTILES):
            ctx.wait("vector", "sg", p.val("sg"))
            for src in (QTM, KTM):
                tiles = [(t * 4 + st) * 1024 for t in range(4)]
                MEAN = PREPF[:, 4096:5120]
                T1 = PREPF[:, 5120:6144]
                T2 = PREPF[:, 6144:7168]
                XC = PREPF[:, 0:4096]
                ctx.op("vector", "tensor_add", T1, src[:, tiles[0]:tiles[0] + 1024],
                       src[:, tiles[1]:tiles[1] + 1024])
                ctx.op("vector", "tensor_add", T2, src[:, tiles[2]:tiles[2] + 1024],
                       src[:, tiles[3]:tiles[3] + 1024])
                ctx.drain("vector")
                ctx.op("vector", "tensor_add", MEAN, T1, T2)
                ctx.drain("vector")
                ctx.op("vector", "tensor_scalar_mul", MEAN, MEAN, 0.25)
                ctx.drain("vector")
                for t in range(4):
                    ctx.op("vector", "tensor_sub", XC[:, t * 1024:(t + 1) * 1024],
                           src[:, tiles[t]:tiles[t] + 1024], MEAN)
                ctx.drain("vector")
                SX = PREPF[:, 4096:5120]
                SQT = PREPF[:, 5120:6144]
                ctx.op("vector", "tensor_mul", SX, XC[:, 0:1024], XC[:, 0:1024])
                ctx.drain("vector")
                for t in range(1, 4):
                    ctx.op("vector", "tensor_mul", SQT, XC[:, t * 1024:(t + 1) * 1024],
                           XC[:, t * 1024:(t + 1) * 1024])
                    ctx.drain("vector")
                    ctx.op("vector", "tensor_add", SX, SX, SQT)
                    ctx.drain("vector")
                vsx = ctx.sig("vector", "sv")
                ctx.wait("scalar", "sv", vsx)
                ctx.op("scalar", "activation", SQT, SX, AF.Sqrt)
                ssq = ctx.sig("scalar", "ss")
                ctx.wait("vector", "ss", ssq)
                ctx.op("vector", "reciprocal", SQT, SQT)
                ctx.drain("vector")
                for t in range(4):
                    ctx.op("vector", "tensor_mul", src[:, tiles[t]:tiles[t] + 1024],
                           XC[:, t * 1024:(t + 1) * 1024], SQT)
                ctx.drain("vector")
            prep_done = ctx.sig("vector", "sv")
            ctx.wait("gpsimd", "sv", prep_done)

            for h in range(16):
                pu = h % 2
                CC = CCs[pu]
                EX = Es[pu]
                Z = NZ[:, 0:64]
                RZ = NZ[:, 320:384]
                NT = [NZ[:, 64 + t * 64: 128 + t * 64] for t in range(4)]
                abf_row = ABF[:, :].ap[0][0]
                ab2_row = AB2[:, :].ap[0][0]

                def xap(t, h=h, st=st, abf_row=abf_row):
                    off = (t * 4 + st) * 1024 + h * 64
                    return bass.AP(ABF[:, :].tensor, off, [[abf_row, 128], [0, 64], [1, 64]])

                def yap(t, h=h, st=st, abf_row=abf_row):
                    off = 16384 + (t * 4 + st) * 1024 + h * 64
                    return bass.AP(ABF[:, :].tensor, off, [[abf_row, 128], [1, 64], [0, 64]])

                def vap(t, h=h, st=st, abf_row=abf_row):
                    off = 32768 + (t * 4 + st) * 1024 + h * 64
                    return bass.AP(ABF[:, :].tensor, off, [[abf_row, 128], [0, 64], [1, 64]])

                cc3 = bass.AP(AB2[:, :].tensor, CC.offset, [[ab2_row, 128], [64, 64], [1, 64]])
                tp3 = bass.AP(AB2[:, :].tensor, TMPP.offset, [[ab2_row, 128], [64, 64], [1, 64]])
                ec3 = bass.AP(AB2[:, :].tensor, ECB.offset, [[ab2_row, 128], [64, 64], [1, 64]])
                pr3 = bass.AP(AB2[:, :].tensor, PRD.offset, [[ab2_row, 128], [64, 64], [1, 64]])

                ctx.wait("gpsimd", "ss", p.val("ss"))
                ctx.op("gpsimd", "tensor_tensor", cc3, xap(0), yap(0), op=ALU.mult)
                ctx.drain("gpsimd")
                ctx.op("gpsimd", "tensor_tensor", tp3, xap(1), yap(1), op=ALU.mult)
                ctx.drain("gpsimd")
                ctx.op("gpsimd", "tensor_add", CC[:, :], CC[:, :], TMPP[:, 0:4096])
                vc = ctx.sig("gpsimd", "sg")
                ctx.op("vector", "tensor_tensor", pr3, xap(2), yap(2), op=ALU.mult)
                ctx.drain("vector")
                ctx.op("vector", "tensor_tensor", ec3, xap(3), yap(3), op=ALU.mult)
                ctx.drain("vector")
                ctx.op("vector", "tensor_add", PRD[:, :], PRD[:, :], ECB[:, :])
                ctx.drain("vector")
                ctx.wait("vector", "sg", vc)
                ctx.op("vector", "tensor_add", CC[:, :], CC[:, :], PRD[:, :])
                vv = ctx.sig("vector", "sv")
                ctx.wait("scalar", "sv", vv)
                ctx.op("scalar", "activation", EX[:, :], CC[:, :], AF.Exp)
                se = ctx.sig("scalar", "ss")
                ctx.wait("vector", "ss", se)
                ctx.op("vector", "tensor_scalar_max", ECB[:, :], EX[:, :], 1.0)
                ctx.drain("vector")
                ctx.op("vector", "reduce_sum", Z, ec3, axis=AX.X)
                for t in range(4):
                    ctx.drain("vector")
                    ctx.op("vector", "tensor_tensor", pr3, ec3, vap(t), op=ALU.mult)
                    ctx.drain("vector")
                    ctx.op("vector", "reduce_sum", NT[t], pr3, axis=AX.X)
                ctx.drain("vector")
                ctx.op("vector", "reciprocal", RZ, Z)
                ctx.drain("vector")
                for t in range(4):
                    off = (t * 4 + st) * 1024 + h * 64
                    ctx.op("vector", "tensor_mul", ABF[:, off:off + 64], NT[t], RZ)
                ctx.drain("vector")
                ctx.sig("vector", "sv")

        ctx.wait("gpsimd", "sv", p.val("sv"))
        ctx.wait("gpsimd", "sg", p.val("sg"))
        wa = bass.AP(wo_in.tensor, 0, [[D, 128], [128 * D, 8], [1, D]])
        dv = ctx.dma("gpsimd", WNAT.rearrange("p (c e) -> p c e", c=8), wa, "dG")
        pst = [ctx.ps["ptA"], ctx.ps["ptB"]]
        ctx.wait("tensor", "dG", dv)
        ctx.wait("tensor", "dC", p.val("dC"))
        ctx.wait("tensor", "sg", p.val("sg"))
        ctx.wait("tensor", "sv", p.val("sv"))
        base_ss = p.val("ss")
        k = 0
        for oc in range(8):
            for ec in range(8):
                pt = pst[k % 2][:, :]
                ctx.wait("tensor", "ss", base_ss + k - 1 if k >= 2 else base_ss)
                ctx.op("tensor", "transpose", pt,
                       WNAT[:, oc * 1024 + ec * 128: oc * 1024 + ec * 128 + 128], IDN[:, :])
                tv = ctx.sig("tensor", "st")
                ctx.wait("scalar", "st", tv)
                ctx.op("scalar", "copy",
                       WOT[:, ec * 1024 + oc * 128: ec * 1024 + oc * 128 + 128], pt)
                ctx.sig("scalar", "ss")
                k += 1
        wt_done = p.val("ss")

        x_res = x_in.rearrange("(i p) e -> p i e", p=128)
        ctx.wait("sync", "sv", p.val("sv"))
        ctx.wait("sync", "sg", p.val("sg"))
        dx = ctx.dma("sync", XT.rearrange("p (i e) -> p i e", i=NTILE), x_res, "dQ")
        ctx.wait("vector", "dQ", dx)

        ctx.wait("tensor", "ss", wt_done)
        for i in range(NTILE):
            ctx.wait("tensor", "sv", p.val("sv"))
            for c in range(8):
                pt = pst[c % 2][:, :]
                ctx.wait("tensor", "ss", p.val("ss") - 1 if c >= 2 else p.val("ss"))
                ctx.op("tensor", "transpose", pt,
                       QTM[:, i * 1024 + c * 128: i * 1024 + c * 128 + 128], IDN[:, :])
                tv = ctx.sig("tensor", "st")
                ctx.wait("scalar", "st", tv)
                ctx.op("scalar", "copy", TSTG[:, c * 128:(c + 1) * 128], pt)
                ctx.sig("scalar", "ss")
            ctx.wait("tensor", "ss", p.val("ss"))
            for j in range(2):
                for c in range(8):
                    ctx.op("tensor", "matmul", PSB[j][:, :],
                           TSTG[:, c * 128:(c + 1) * 128],
                           WOT[:, c * 1024 + j * 512: c * 1024 + (j + 1) * 512],
                           start=(c == 0), stop=False)
                ctx.op("tensor", "matmul", PSB[j][:, :],
                       ctx.sb["ones_bf"][0:1, 0:128],
                       (BROW if j == 0 else BROW2)[0:1, 0:512], start=False, stop=True)
            ty = ctx.sig("tensor", "st")
            ctx.wait("vector", "st", ty)
            for j in range(2):
                sl = slice(i * 1024 + j * 512, i * 1024 + (j + 1) * 512)
                ctx.op("vector", "tensor_add", XT[:, sl], XT[:, sl], PSB[j][:, :])
            ctx.sig("vector", "sv")

        out_ready = p.val("sv")
        ctx.wait("sync", "sv", out_ready)
        ctx.dma("sync", x_out.rearrange("(i p) e -> p i e", p=128),
                XT.rearrange("p (i e) -> p i e", i=NTILE), "dO")
        ctx.wait("sync", "dO", p.val("dO"))
        finish(ctx)
    return nc


def run_s5(x_full, Wqkv, bqkv, Wout, bout):
    nc_a = build_s5a()
    in_maps = []
    for c in range(NCORES):
        in_maps.append({
            "x": shard_tokens(x_full, c),
            "wqkv": np.asarray(Wqkv, np.float32),
            "bqkv": np.asarray(bqkv, np.float32).reshape(1, 3 * D),
            "ident": IDENT_BF,
        })
    res = run_bass_kernel_spmd(nc_a, in_maps, core_ids=list(range(NCORES)))
    nc_b = build_s5b()
    in_maps = []
    for c in range(NCORES):
        in_maps.append({
            "x": shard_tokens(x_full, c),
            "q": res.results[c]["q"], "k": res.results[c]["k"], "v": res.results[c]["v"],
            "wo": np.asarray(Wout, np.float32),
            "bo": np.asarray(bout, np.float32).reshape(1, D),
            "ident": IDENT_BF,
        })
    res2 = run_bass_kernel_spmd(nc_b, in_maps, core_ids=list(range(NCORES)))
    return unshard_tokens([res2.results[c]["xo"] for c in range(NCORES)])



def shard_tokens(x, core):
    xr = x.reshape(B, T // 128, 128, x.shape[-1])
    sl = xr[:, core::8]
    return np.ascontiguousarray(sl.reshape(B * 512, x.shape[-1]))


def unshard_tokens(slices):
    Dd = slices[0].shape[-1]
    out = np.empty((B, T // 128, 128, Dd), dtype=slices[0].dtype)
    for c in range(8):
        out[:, c::8] = slices[c].reshape(B, LTILES, 128, Dd)
    return out.reshape(B, T, Dd)


IDENT_BF = np.eye(128, dtype=BF16NP)


def run_s2(x_full, W, alpha, bias):
    nc = build_s2(0)
    in_maps = []
    for c in range(NCORES):
        in_maps.append({
            "x": shard_tokens(x_full, c),
            "w": W,
            "alpha": np.asarray(alpha, np.float32).reshape(1, 1),
            "bias": np.asarray(bias, np.float32).reshape(1, D),
            "ident": IDENT_BF,
        })
    res = run_bass_kernel_spmd(nc, in_maps, core_ids=list(range(NCORES)))
    return unshard_tokens([res.results[c]["xo"] for c in range(NCORES)])


def kernel(**inputs):
    i = inputs
    x = np.asarray(i["L0_attn_a_x"], np.float32)
    x = run_s1(x, i["L0_attn_a_Wq"], i["L0_attn_a_bq"], i["L0_attn_a_Wk"],
               i["L0_attn_a_bk"], i["L0_attn_a_Wv"], i["L0_attn_a_bv"])
    x = run_s2(x, i["L0_ffn_weight"], i["L0_ffn_alpha"], i["L0_ffn_bias"])
    x = run_s3(x, i["L0_attn_b_q_w"], i["L0_attn_b_q_b"], i["L0_attn_b_k_w"],
               i["L0_attn_b_k_b"], i["L0_attn_b_v_w"], i["L0_attn_b_v_b"],
               i["L0_attn_b_o_w"], i["L0_attn_b_o_b"])
    x = run_s2(x, i["L0_ffn_weight"], i["L0_ffn_alpha"], i["L0_ffn_bias"])
    x = run_s5(x, i["L1_attn_a_Wqkv"], i["L1_attn_a_bqkv"], i["L1_attn_a_Wout"],
               i["L1_attn_a_bout"])
    w1nT, w2nT = run_moe_prep(i["L1_ffn_w1_v"], i["L1_ffn_w1_g"],
                              i["L1_ffn_w2_v"], i["L1_ffn_w2_g"])
    x = run_moe(x, w1nT, w2nT, i["L1_ffn_gate_v"], i["L1_ffn_gate_g"],
                i["L1_ffn_gate_b"], i["L1_ffn_b1"], i["L1_ffn_b2"])
    x = run_s7(x, i["L1_attn_b_wq"], i["L1_attn_b_wk"], i["L1_attn_b_wv"])
    x = run_moe(x, w1nT, w2nT, i["L1_ffn_gate_v"], i["L1_ffn_gate_g"],
                i["L1_ffn_gate_b"], i["L1_ffn_b1"], i["L1_ffn_b2"])
    return np.ascontiguousarray(x.astype(np.float32))
